# version 08
# baseline (speedup 1.0000x reference)
import math
import contextlib
import numpy as np

import concourse.bass as bass
import concourse.mybir as mybir
from concourse.bass_utils import run_bass_kernel_spmd
from concourse.alu_op_type import AluOpType as OP

F32 = mybir.dt.float32
BF16 = mybir.dt.bfloat16
AF = mybir.ActivationFunctionType
AX = mybir.AxisListType.X

B, SEQ, E = 4, 4096, 2048
NQ, NKV, HD = 16, 4, 128
IFF = 8192
TOK = 2048
HALO = 512
TOKH = TOK + HALO
NT = TOK // 128
NTH = TOKH // 128
NEG = -1.0e30
C23 = 1.5 * 2.0 ** 23
MAGIC_K = float(np.float32((1.5 * 2.0 ** 52 - C23) * 2.0 ** 30))
EPS = 1.0e-5
ISR = float(np.float32(1.0 / math.sqrt(HD)))
TWO_PI_HI = float(np.float32(2.0 * math.pi))
TWO_PI_LO = float(np.float32(2.0 * math.pi - np.float64(np.float32(2.0 * math.pi))))
HALF_PI = float(np.float32(0.5 * math.pi))
PI_CLAMP = 3.1415925
NLN1E4_O64 = float(np.float32(-math.log(10000.0) / 64.0))

ENGINES = ("sync", "scalar", "vector", "tensor", "gpsimd")


def _hadamard(n):
    h = np.array([[1.0]])
    while h.shape[0] < n:
        h = np.block([[h, h], [h, -h]])
    return h


class Sched:

    def __init__(self):
        self.q = {e: [] for e in ENGINES}
        self.cnt = {}
        self.sems = {}

    def sem(self, name):
        self.cnt.setdefault(name, 0)

    def on(self, eng, fn):
        self.q[eng].append(fn)

    def wait(self, eng, sem, val=None):
        v = self.cnt[sem] if val is None else val
        if v > 0:
            self.on(eng, lambda e, s=sem, vv=v: e.wait_ge(self.sems[s], vv))

    def do(self, eng, build, inc=None, k=1):
        if inc is not None:
            self.cnt[inc] += k

            def fn(e, b=build, s=inc, kk=k):
                b(e).then_inc(self.sems[s], kk)
            self.on(eng, fn)
        else:
            self.on(eng, lambda e, b=build: b(e))

    def dma(self, eng, out, in_, inc):
        self.do(eng, lambda e, o=out, i=in_: e.dma_start(out=o, in_=i), inc, 16)

    def mark(self, sem):
        return (sem, self.cnt[sem])

    def emit(self, block):
        for eng in ENGINES:
            ops = self.q[eng]

            def section(e, ops=ops):
                for fn in ops:
                    fn(e)
            getattr(block, eng)(section)


class Kern:
    def __init__(self, debug=False):
        self.nc = bass.Bass("TRN2", target_bir_lowering=False)
        self.debug = debug
        self.S = Sched()
        for s in ("qs", "qa", "qg", "pe", "ve", "ac"):
            self.S.sem(s)
        self.t = {}
        self.dram = {}
        self.u = {}

    def sb(self, es, name, shape, dtype):
        h = es.enter_context(self.nc.sbuf_tensor(name, list(shape), dtype))
        self.t[name] = h
        return h

    def psum(self, es, name, shape, dtype):
        h = es.enter_context(self.nc.psum_tensor(name, list(shape), dtype))
        self.t[name] = h
        return h

    def din(self, name, shape, dtype=F32):
        h = self.nc.dram_tensor(name, list(shape), dtype, kind="ExternalInput")
        self.dram[name] = h
        return h

    def dout(self, name, shape, dtype=F32):
        h = self.nc.dram_tensor(name, list(shape), dtype, kind="ExternalOutput")
        self.dram[name] = h
        return h

    def dtmp(self, name, shape, dtype):
        h = self.nc.dram_tensor(name, list(shape), dtype, kind="Internal")
        self.dram[name] = h
        return h

    def ap(self, name):
        return self.t[name].ap()

    def w(self, eng, *sems):
        for s in sems:
            self.S.wait(eng, s)

    def wm(self, eng, mark):
        if mark is not None:
            self.S.wait(eng, mark[0], mark[1])

    def wu_(self, eng, key):
        self.wm(eng, self.u.get(key))

    def v(self, build):
        self.S.do("vector", build, "ve")

    def a(self, build):
        self.S.do("scalar", build, "ac")

    def p(self, build, inc=False):
        self.S.do("tensor", build, "pe" if inc else None)

    def vself(self):
        self.S.wait("vector", "ve")

    def aself(self):
        self.S.wait("scalar", "ac")

    def barrier(self):
        for eng in ENGINES:
            for s in ("pe", "ve", "ac", "qs", "qa", "qg"):
                self.S.wait(eng, s)


def build_program(debug=False, phases="AB", nt=NT, dump=(), subs="0123"):
    K = Kern(debug=debug)
    K.dump = dump
    nc, S = K.nc, K.S
    ap = K.ap

    x_in = K.din("x", [TOKH, E]).ap()
    wqkv_in = K.din("wqkv", [E, 3072]).ap()
    wo_in = K.din("wo", [NQ * HD, E]).ap()
    bqkv_in = K.din("bqkv", [1, 3072]).ap()
    bo_in = K.din("bo", [1, E]).ap()
    wg_in = K.din("wg", [E, IFF]).ap()
    wu_in = K.din("wu", [E, IFF]).ap()
    wd_in = K.din("wd", [IFF, E]).ap()
    pos0_in = K.din("pos0", [128, 1]).ap()
    blk_in = K.din("blkmask", [128, 4]).ap()
    ident_in = K.din("ident", [128, 128]).ap()
    tri_in = K.din("tri", [128, 512]).ap()
    hcat_in = K.din("hcat", [128, 2048]).ap()
    swp_in = K.din("swp", [128, 128]).ap()
    sgn_in = K.din("sgn", [128, 1]).ap()

    out = K.dout("out", [TOK, E]).ap()
    wqkv16_d = K.dtmp("wqkv16_d", [E, 3072], BF16).ap()
    wo16_d = K.dtmp("wo16_d", [NQ * HD, E], BF16).ap()
    if "A" not in phases and debug:
        x2_d = K.din("x2_d", [TOK, E]).ap()
    elif debug:
        x2_d = K.dout("x2_d", [TOK, E]).ap()
    else:
        x2_d = K.dtmp("x2_d", [TOK, E], F32).ap()
    if debug:
        h_d = K.dout("h_d", [NT, 128, IFF], BF16).ap()
        n2t_d = K.dout("n2t_d", [64, NT, 128, 128], BF16).ap()
    else:
        h_d = K.dtmp("h_d", [NT, 128, IFF], BF16).ap()
        n2t_d = K.dtmp("n2t_d", [64, NT, 128, 128], BF16).ap()

    es_root = contextlib.ExitStack()
    with es_root:
        K.sb(es_root, "ident16", [128, 128], BF16)
        K.sb(es_root, "swp16", [128, 128], BF16)
        K.sb(es_root, "sgnc", [128, 1], F32)
        K.sb(es_root, "trim", [128, 512], F32)
        K.sb(es_root, "acol", [128, 1], F32)
        K.sb(es_root, "nacol", [128, 1], F32)
        K.sb(es_root, "m12c", [128, 1], F32)
        K.sb(es_root, "onesc", [128, 1], F32)
        K.sb(es_root, "biasrow", [1, 5632], BF16)
        K.sb(es_root, "ones_r32", [1, 128], F32)
        K.sb(es_root, "blkcol", [128, 4], F32)
        K.sb(es_root, "pos0c", [128, 1], F32)
        K.sb(es_root, "freqc", [128, 1], F32)
        K.sb(es_root, "swcol", [128, 12], F32)
        K.sb(es_root, "cols", [128, 112], F32)
        K.sb(es_root, "wsum", [128, 4], F32)
        K.sb(es_root, "onesc16", [128, 1], BF16)
        K.sb(es_root, "wsb0", [128, 512], F32)
        K.sb(es_root, "wsb1", [128, 512], F32)
        K.sb(es_root, "wsb2", [128, 512], F32)
        K.sb(es_root, "wsb3", [128, 512], F32)
        K.sb(es_root, "wsd", [128, 512], BF16)
        K.sb(es_root, "srow", [1, 16], F32)
        K.sb(es_root, "cs1", [128, TOKH], BF16)
        K.sb(es_root, "cs2", [128, TOKH], BF16)

        psA = [K.psum(es_root, f"psA{i}", [128, 512], F32) for i in range(4)]
        psS = [K.psum(es_root, f"psS{i}", [128, 640], F32) for i in range(2)]
        K.psA, K.psS = psA, psS

        with contextlib.ExitStack() as es:
            stg = K.sb(es, "stg", [128, 2048], F32)
            stg2 = K.sb(es, "stg2", [128, 2048], F32)
            w16s = K.sb(es, "w16s", [128, 3072], BF16)
            ang = K.sb(es, "ang", [128, TOKH], F32)
            rr = K.sb(es, "rr", [128, TOKH], F32)
            emit_pre(K, x2_d, wqkv16_d, wo16_d,
                     dict(wqkv_in=wqkv_in, wo_in=wo_in, bqkv_in=bqkv_in,
                          bo_in=bo_in, tri_in=tri_in, blk_in=blk_in,
                          pos0_in=pos0_in, ident_in=ident_in, hcat_in=hcat_in,
                          swp_in=swp_in, sgn_in=sgn_in,
                          wg_in=wg_in, wu_in=wu_in, wd_in=wd_in))
        K.barrier()

        if "A" in phases:
            es_a = contextlib.ExitStack()
            with (contextlib.nullcontext(es_root) if K.dump else es_a) as es:
                K.sb(es, "wbuf0", [128, 16 * 512], BF16)
                K.sb(es, "wbuf1", [128, 16 * 512], BF16)
                K.sb(es, "wvbuf", [128, 8 * 512], BF16)
                K.sb(es, "wobuf", [128, 8 * 512], BF16)
                K.sb(es, "ln1t", [128, 16 * 512], BF16)
                K.sb(es, "ln1b", [128, 2048], BF16)
                K.sb(es, "disc", [128, 1536], BF16)
                K.sb(es, "qrot", [128, 16 * 512], BF16)
                K.sb(es, "krot", [128, 4 * TOKH], BF16)
                K.sb(es, "vsb", [128, NTH * 512], BF16)
                K.sb(es, "xa0", [128, 2048], F32)
                K.sb(es, "xa1", [128, 2048], F32)
                K.sb(es, "xb0", [128, 512], F32)
                K.sb(es, "xb1", [128, 512], F32)
                K.sb(es, "x2s0", [128, 512], F32)
                K.sb(es, "x2s1", [128, 512], F32)
                K.sb(es, "pt16", [128, 2 * 640], BF16)
                K.sb(es, "rbsb", [128, 2048], BF16)
                K.sb(es, "avb", [128, 4 * 2048], BF16)
                emit_attention(K, x_in, wqkv16_d, wo16_d, x2_d, nt)
            K.barrier()

        emit_wscale_fin(K)
        K.barrier()
        if "B" in phases:
            emit_ffn(K, es_root, x2_d, h_d, n2t_d, wg_in, wu_in, wd_in, out, nt,
                     subs=subs)

        for name in K.dump:
            h = K.t[name]
            sh = list(h.ap().shape)
            dt_ = h.ap().dtype
            d = K.dout("dump_" + name, sh, dt_)
            for s in ("pe", "ve", "ac"):
                S.wait("gpsimd", s)
            S.dma("gpsimd", d.ap()[:], h.ap()[:], "qg")
        S.wait("gpsimd", "qg")
        S.wait("gpsimd", "qs")
        S.wait("gpsimd", "qa")
        for eng in ("sync", "scalar"):
            S.wait(eng, "qs")
            S.wait(eng, "qa")

        block = es_root.enter_context(nc.Block())
        for name in list(S.cnt):
            S.sems[name] = es_root.enter_context(nc.semaphore("sem_" + name))
        S.emit(block)
    return K


def emit_pre(K, x2_d, wqkv16_d, wo16_d, D):
    S, ap = K.S, K.ap
    S.dma("sync", ap("trim")[:], D["tri_in"][:], "qs")
    S.dma("sync", ap("blkcol")[:], D["blk_in"][:], "qs")
    S.dma("sync", ap("pos0c")[:], D["pos0_in"][:], "qs")
    S.dma("sync", ap("stg")[:, 0:128], D["ident_in"][:], "qs")
    S.dma("sync", ap("stg2")[:, 0:128], D["swp_in"][:], "qs")
    S.dma("sync", ap("sgnc")[:], D["sgn_in"][:], "qs")
    S.dma("sync", ap("stg")[0:1, 128:2048], D["bqkv_in"][0:1, 0:1920], "qs")
    m = S.mark("qs")
    K.wm("vector", m)
    K.v(lambda e: e.tensor_copy(ap("ident16")[:], ap("stg")[:, 0:128]))
    K.v(lambda e: e.tensor_copy(ap("swp16")[:], ap("stg2")[:, 0:128]))
    K.v(lambda e: e.tensor_copy(ap("biasrow")[0:1, 0:1920], ap("stg")[0:1, 128:2048]))
    K.wm("sync", S.mark("ve"))
    S.dma("sync", ap("stg")[0:1, 0:1152], D["bqkv_in"][0:1, 1920:3072], "qs")
    S.dma("sync", ap("stg2")[0:1, 0:2048], D["bo_in"][0:1, :], "qs")
    m = S.mark("qs")
    K.wm("vector", m)
    K.v(lambda e: e.tensor_copy(ap("biasrow")[0:1, 1920:3072], ap("stg")[0:1, 0:1152]))
    K.v(lambda e: e.tensor_copy(ap("biasrow")[0:1, 3072:5120], ap("stg2")[0:1, 0:2048]))
    K.u["stg_rd"] = S.mark("ve")
    K.v(lambda e: e.memset(ap("onesc")[:], 1.0))
    K.v(lambda e: e.memset(ap("m12c")[:], -12.0))
    K.v(lambda e: e.memset(ap("onesc16")[:], 1.0))
    K.v(lambda e: e.memset(ap("biasrow")[0:1, 5120:5632], 1.0))
    K.v(lambda e: e.memset(ap("ones_r32")[:], 1.0))
    K.v(lambda e: e.tensor_scalar(ap("cols")[:, 0:1], ap("onesc")[:], 2.0 ** -30,
                                  1.0, OP.add, OP.subtract))
    K.vself()
    K.v(lambda e: e.tensor_scalar(ap("acol")[:], ap("cols")[:, 0:1], MAGIC_K,
                                  C23, OP.mult, OP.add))
    K.vself()
    K.v(lambda e: e.tensor_scalar(ap("nacol")[:], ap("acol")[:], -1.0, None,
                                  OP.mult))
    m_acol = S.mark("ve")

    S.do("gpsimd", lambda e: e.iota(ap("freqc")[:], [[1, 1]], channel_multiplier=1,
                                    allow_small_or_imprecise_dtypes=True), "qg", 1)
    S.do("gpsimd", lambda e: e.iota(ap("ang")[:], [[1, TOKH]], channel_multiplier=0,
                                    allow_small_or_imprecise_dtypes=True), "qg", 1)
    m_io = S.mark("qg")
    K.wm("vector", m_io)
    K.v(lambda e: e.tensor_scalar(ap("cols")[:, 9:10], ap("freqc")[:], 64.0, None,
                                  OP.is_ge))
    K.vself()
    K.v(lambda e: e.scalar_tensor_tensor(ap("freqc")[:], ap("cols")[:, 9:10],
                                         -64.0, ap("freqc")[:], OP.mult, OP.add))
    m_f0 = S.mark("ve")
    K.wm("scalar", m_f0)
    K.a(lambda e: e.activation(ap("freqc")[:], ap("freqc")[:], AF.Exp,
                               scale=NLN1E4_O64))
    m_frq = S.mark("ac")
    K.wm("vector", m_frq)
    K.v(lambda e: e.tensor_scalar(ap("ang")[:], ap("ang")[:], ap("pos0c")[:, 0:1],
                                  ap("freqc")[:, 0:1], OP.add, OP.mult))
    K.vself()
    for idx, shift in ((0, 0.0), (1, HALF_PI)):
        K.wm("vector", m_acol)
        K.v(lambda e, shift=shift: e.tensor_scalar(
            ap("rr")[:], ap("ang")[:], 1.0 / (2 * math.pi),
            shift / (2 * math.pi), OP.mult, OP.add))
        K.vself()
        K.v(lambda e: e.tensor_scalar(ap("rr")[:], ap("rr")[:],
                                      ap("acol")[:, 0:1], ap("acol")[:, 0:1],
                                      OP.add, OP.subtract))
        K.vself()
        K.v(lambda e: e.scalar_tensor_tensor(
            ap("rr")[:], ap("rr")[:], -TWO_PI_HI, ap("ang")[:],
            OP.mult, OP.add))
        K.vself()
        K.v(lambda e, shift=shift: e.tensor_scalar(
            ap("rr")[:], ap("rr")[:], shift, -PI_CLAMP, OP.add, OP.max))
        K.vself()
        K.v(lambda e: e.tensor_scalar(ap("rr")[:], ap("rr")[:], PI_CLAMP, None,
                                      OP.min))
        m_r = S.mark("ve")
        K.wm("scalar", m_r)
        if idx == 0:
            K.a(lambda e: e.activation(ap("rr")[:], ap("rr")[:], AF.Sin))
            m_sn = S.mark("ac")
            K.wm("vector", m_sn)
            K.v(lambda e: e.tensor_scalar(ap("cs2")[:], ap("rr")[:],
                                          ap("sgnc")[:, 0:1], None, OP.mult))
        else:
            K.a(lambda e: e.activation(ap("cs1")[:], ap("rr")[:], AF.Sin))

    for kc in range(16):
        K.wu_("sync", "stg_rd")
        S.dma("sync", ap("stg")[:, 0:2048], D["wqkv_in"][kc * 128:(kc + 1) * 128, 0:2048], "qs")
        S.dma("sync", ap("stg2")[:, 0:1024], D["wqkv_in"][kc * 128:(kc + 1) * 128, 2048:3072], "qs")
        m = S.mark("qs")
        K.wm("vector", m)
        K.wu_("vector", "w16s_st")
        K.v(lambda e: e.tensor_copy(ap("w16s")[:, 0:2048], ap("stg")[:, 0:2048]))
        K.v(lambda e: e.tensor_copy(ap("w16s")[:, 2048:3072], ap("stg2")[:, 0:1024]))
        K.u["stg_rd"] = S.mark("ve")
        m = S.mark("ve")
        K.wm("scalar", m)
        S.dma("scalar", wqkv16_d[kc * 128:(kc + 1) * 128, :], ap("w16s")[:, 0:3072], "qa")
        K.u["w16s_st"] = S.mark("qa")
    for kc in range(16):
        K.wu_("sync", "stg_rd")
        S.dma("sync", ap("stg")[:, 0:2048], D["wo_in"][kc * 128:(kc + 1) * 128, :], "qs")
        m = S.mark("qs")
        K.wm("vector", m)
        K.wu_("vector", "w16s_st")
        K.v(lambda e: e.tensor_copy(ap("w16s")[:, 0:2048], ap("stg")[:, 0:2048]))
        K.u["stg_rd"] = S.mark("ve")
        m = S.mark("ve")
        K.wm("scalar", m)
        S.dma("scalar", wo16_d[kc * 128:(kc + 1) * 128, :], ap("w16s")[:, 0:2048], "qa")
        K.u["w16s_st"] = S.mark("qa")

    K.v(lambda e: e.memset(ap("wsum")[:], 0.0))
    K.wscale_ws0 = S.mark("ve")
    K.wscale_cast = S.mark("qa")
    K.wscale_list = []
    for widx, (srcw, rows) in enumerate(((D["wg_in"], E), (D["wu_in"], E),
                                         (D["wd_in"], IFF))):
        ncol = IFF if rows == E else E
        for r in range(rows // 128):
            for c in range(ncol // 512):
                K.wscale_list.append((widx, srcw, r * 128, c * 512))
    K.wscale_i = 0
    K.wscale_prev = [None, None, None, None]


def emit_wscale_chunks(K, n):
    S, ap = K.S, K.ap
    for _ in range(n):
        i = K.wscale_i
        if i >= len(K.wscale_list):
            return
        K.wscale_i += 1
        widx, srcw, r, c = K.wscale_list[i]
        bi = i % 4
        buf = f"wsb{bi}"
        if K.wscale_prev[bi] is not None:
            K.wm("gpsimd", K.wscale_prev[bi])
        S.dma("gpsimd", ap(buf)[:], srcw[r:r + 128, c:c + 512], "qg")
        m = S.mark("qg")
        K.wm("scalar", m)
        if i == 0:
            K.wm("scalar", K.wscale_ws0)
        K.wu_("scalar", "wsacc")
        K.a(lambda e, buf=buf: e.activation(ap("wsd")[:], ap(buf)[:],
                                            AF.Abs, accum_out=ap("cols")[:, 40:41]))
        K.wscale_prev[bi] = S.mark("ac")
        K.wm("vector", K.wscale_prev[bi])
        K.v(lambda e, widx=widx: e.tensor_tensor(
            ap("wsum")[:, widx:widx + 1], ap("wsum")[:, widx:widx + 1],
            ap("cols")[:, 40:41], OP.add))
        K.u["wsacc"] = S.mark("ve")


def emit_wscale_fin(K):
    S, ap = K.S, K.ap
    emit_wscale_chunks(K, len(K.wscale_list))
    m_wsum = S.mark("ve")
    K.wm("tensor", m_wsum)
    K.wu_("tensor", ("psS", 0))
    K.p(lambda e: e.matmul(K.psS[0].ap()[0:1, 0:4], ap("onesc")[:],
                           ap("wsum")[:, 0:4], start=True, stop=True), inc=True)
    m = S.mark("pe")
    K.wm("vector", m)
    K.v(lambda e: e.tensor_copy(ap("srow")[0:1, 0:4], K.psS[0].ap()[0:1, 0:4]))
    K.vself()
    K.v(lambda e: e.tensor_scalar(ap("srow")[0:1, 4:8], ap("srow")[0:1, 0:4],
                                  1.0 / (IFF * E), EPS, OP.mult, OP.max))
    K.vself()
    K.v(lambda e: e.tensor_scalar(ap("srow")[0:1, 8:12], ap("srow")[0:1, 4:8],
                                  0.5, None, OP.mult))
    K.vself()
    K.v(lambda e: e.tensor_scalar(ap("srow")[0:1, 12:16], ap("srow")[0:1, 8:12],
                                  -1.0, None, OP.mult))
    m_s = S.mark("ve")
    K.wm("tensor", m_s)
    K.p(lambda e: e.matmul(K.psS[0].ap()[:, 0:4], ap("ones_r32")[0:1, 0:128],
                           ap("srow")[0:1, 4:8], start=True, stop=True))
    K.p(lambda e: e.matmul(K.psS[0].ap()[:, 4:8], ap("ones_r32")[0:1, 0:128],
                           ap("srow")[0:1, 8:12], start=True, stop=True))
    K.p(lambda e: e.matmul(K.psS[0].ap()[:, 8:12], ap("ones_r32")[0:1, 0:128],
                           ap("srow")[0:1, 12:16], start=True, stop=True), inc=True)
    m = S.mark("pe")
    K.wm("vector", m)
    K.v(lambda e: e.tensor_copy(ap("swcol")[:, 0:12], K.psS[0].ap()[:, 0:12]))
    K.u[("psS", 0)] = S.mark("ve")


def emit_attention(K, x_in, wqkv16_d, wo16_d, x2_d, nt):
    S, ap = K.S, K.ap
    psA, psS = K.psA, K.psS
    ngroups = (nt + 4) // 4
    xa = ["xa0", "xa1"]
    xb = ["xb0", "xb1"]
    x2s = ["x2s0", "x2s1"]
    wbuf = ["wbuf0", "wbuf1"]

    def emit_ln_group(g):

        for t4 in range(4):
            tt = g * 4 + t4
            bi = t4 % 2
            K.wu_("sync", ("xa", bi))
            S.dma("sync", ap(xa[bi])[:], x_in[tt * 128:(tt + 1) * 128, :], "qs")
            mx = S.mark("qs")
            K.wm("vector", mx)
            K.wu_("vector", "cols_act")
            K.v(lambda e, bi=bi: e.reduce_sum(ap("cols")[:, 0:1], ap(xa[bi])[:], AX))
            m_sum = S.mark("ve")
            K.wm("scalar", mx)
            K.wu_("scalar", "ln1b")
            K.a(lambda e, bi=bi: e.activation(ap("ln1b")[:], ap(xa[bi])[:], AF.Square,
                                              accum_out=ap("cols")[:, 1:2]))
            K.u["disc_sq"] = S.mark("ac")
            K.wm("scalar", m_sum)
            K.wu_("scalar", "cols_free")
            K.aself()
            K.a(lambda e: e.activation(ap("cols")[:, 2:3], ap("cols")[:, 0:1],
                                       AF.Identity, scale=1.0 / 2048))
            K.wu_("vector", "disc_sq")
            K.v(lambda e: e.tensor_scalar(ap("cols")[:, 3:4], ap("cols")[:, 1:2],
                                          1.0 / 2048, EPS, OP.mult, OP.add))
            m_q = S.mark("ve")
            K.wm("scalar", m_q)
            K.aself()
            K.a(lambda e: e.activation(ap("cols")[:, 4:5], ap("cols")[:, 2:3],
                                       AF.Square))
            K.aself()
            K.a(lambda e: e.activation(ap("cols")[:, 5:6], ap("cols")[:, 4:5],
                                       AF.Identity, scale=-1.0, bias=ap("cols")[:, 3:4]))
            K.aself()
            K.a(lambda e: e.activation(ap("cols")[:, 6:7], ap("cols")[:, 5:6], AF.Ln))
            K.aself()
            K.a(lambda e: e.activation(ap("cols")[:, 6:7], ap("cols")[:, 6:7],
                                       AF.Exp, scale=-0.5))
            K.aself()
            K.a(lambda e: e.activation(ap("cols")[:, 7:8], ap("cols")[:, 2:3],
                                       AF.Identity, scale=ap("cols")[:, 6:7]))
            m_act = S.mark("ac")
            K.u["cols_act"] = m_act
            K.wm("vector", m_act)
            K.wu_("vector", "ln1b")
            K.v(lambda e, bi=bi: e.tensor_scalar(ap("ln1b")[:], ap(xa[bi])[:],
                                                 ap("cols")[:, 6:7], ap("cols")[:, 7:8],
                                                 OP.mult, OP.subtract))
            K.u[("xa", bi)] = S.mark("ve")
            K.u["cols_free"] = S.mark("ve")
            m_ln = S.mark("ve")
            K.wu_("sync", "ln1t_r")
            m_t = emit_dmat(K, blk3(ap("ln1b")[:], 16, 128),
                            blk3(ap("ln1t")[:], 16, 512, off=t4 * 128), m_ln)
            K.u["ln1b"] = m_t
        m_ln1t = S.mark("qs")


        return m_ln1t

    m_ln1t = emit_ln_group(0)
    for g in range(ngroups):
        own = g >= 1
        og_list = [list(range(0, 4)), list(range(4, 8)), list(range(8, 12)),
                   list(range(12, 16)), list(range(16, 20))]
        if not own:
            og_list = [list(range(16, 20))]
        for ogi, og in enumerate(og_list):
            wb = wbuf[ogi % 2]
            c0 = og[0] * 128
            ncol = len(og) * 128
            K.wu_("sync", ("wbuf", ogi % 2))
            S.dma("sync", ap(wb)[:, 0:16 * ncol].rearrange("p (k c) -> p k c", c=ncol),
                  wqkv16_d[:, c0:c0 + ncol].rearrange("(k p) c -> p k c", p=128), "qs")
            m_w = S.mark("qs")
            K.wm("tensor", m_w)
            K.wm("tensor", m_ln1t)
            for oi, o in enumerate(og):
                if not own and o < 16:
                    continue
                pai = oi % 3
                K.wu_("tensor", ("psA", pai))
                for k in range(16):
                    K.p(lambda e, k=k, o=o, wb=wb, pai=pai, ncol=ncol, c0=c0: e.matmul(
                        psA[pai].ap()[:],
                        ap(wb)[:, k * ncol + o * 128 - c0: k * ncol + (o + 1) * 128 - c0],
                        ap("ln1t")[:, k * 512:(k + 1) * 512],
                        start=(k == 0), stop=False))
                K.p(lambda e, o=o, pai=pai: e.matmul(
                    psA[pai].ap()[:], ap("biasrow")[0:1, o * 128:(o + 1) * 128],
                    ap("biasrow")[0:1, 5120:5632], start=False, stop=True), inc=True)
                m_mm = S.mark("pe")
                emit_rope_evac(K, pai, o, g, m_mm)
                K.u[("psA", pai)] = S.mark("ve")
            K.u[("wbuf", ogi % 2)] = S.mark("pe")
        for kh in range(2):
            K.wu_("sync", "wvbuf")
            S.dma("sync", ap("wvbuf")[:].rearrange("p (k c) -> p k c", c=512),
                  wqkv16_d[kh * 1024:(kh + 1) * 1024, 2560:3072].rearrange(
                      "(k p) c -> p k c", p=128), "qs")
            m_wv = S.mark("qs")
            K.wm("tensor", m_wv)
            for t4 in range(4):
                if kh == 0:
                    K.wu_("tensor", ("psAv", t4))
                    K.wu_("tensor", ("psA", t4))
                for kk in range(8):
                    k = kh * 8 + kk
                    K.p(lambda e, k=k, kk=kk, t4=t4: e.matmul(
                        psA[t4].ap()[:],
                        ap("ln1t")[:, k * 512 + t4 * 128: k * 512 + (t4 + 1) * 128],
                        ap("wvbuf")[:, kk * 512:(kk + 1) * 512],
                        start=(k == 0), stop=False),
                        inc=(kh == 0 and t4 == 3 and kk == 7))
                if kh == 1:
                    K.p(lambda e, t4=t4: e.matmul(
                        psA[t4].ap()[:], ap("biasrow")[0:1, 5120:5248],
                        ap("biasrow")[0:1, 2560:3072], start=False, stop=True),
                        inc=(t4 == 3))
            K.u["wvbuf"] = S.mark("pe")
        m_vmm = S.mark("pe")
        for t4 in range(4):
            tt = g * 4 + t4
            K.wm("vector", m_vmm)
            K.v(lambda e, tt=tt, t4=t4: e.tensor_copy(
                ap("vsb")[:, tt * 512:(tt + 1) * 512], psA[t4].ap()[:]))
            K.u[("psAv", t4)] = S.mark("ve")
            K.u[("psA", t4)] = S.mark("ve")
        K.u["ln1t_r"] = S.mark("pe")

        if g + 1 < ngroups:
            m_ln1t_next = emit_ln_group(g + 1)
        else:
            m_ln1t_next = m_ln1t
        if own:
            m_a1 = S.mark("ve")
            for t4 in range(4):
                qt = g * 4 + t4 - 4
                emit_attn_tile(K, qt, t4, m_a1)
            m_avb = S.mark("ve")
            for ob in range(4):
                for kh in range(2):
                    K.wu_("sync", "wobuf")
                    S.dma("sync", ap("wobuf")[:].rearrange("p (k c) -> p k c", c=512),
                          wo16_d[kh * 1024:(kh + 1) * 1024,
                                 ob * 512:(ob + 1) * 512].rearrange(
                                     "(k p) c -> p k c", p=128), "qs")
                    m_wo = S.mark("qs")
                    for t4 in range(4):
                        K.wm("tensor", m_wo)
                        K.wm("tensor", m_avb)
                        if kh == 0:
                            K.wu_("tensor", ("psA", t4))
                        for hh in range(8):
                            h = kh * 8 + hh
                            K.p(lambda e, h=h, hh=hh, t4=t4, kh=kh: e.matmul(
                                psA[t4].ap()[:],
                                ap("avb")[:, t4 * 2048 + h * 128: t4 * 2048 + (h + 1) * 128],
                                ap("wobuf")[:, hh * 512:(hh + 1) * 512],
                                start=(h == 0), stop=False),
                                inc=(kh == 0 and t4 == 3 and hh == 7))
                        if kh == 1:
                            K.p(lambda e, ob=ob, t4=t4: e.matmul(
                                psA[t4].ap()[:], ap("biasrow")[0:1, 5120:5248],
                                ap("biasrow")[0:1, 3072 + ob * 512:3072 + (ob + 1) * 512],
                                start=False, stop=True), inc=(t4 == 3))
                    K.u["wobuf"] = S.mark("pe")
                m_mmall = S.mark("pe")
                for t4 in range(4):
                    qt = g * 4 + t4 - 4
                    bi = t4 % 2
                    K.wu_("scalar", ("xb", bi))
                    S.dma("scalar", ap(xb[bi])[:],
                          x_in[(qt + 4) * 128:(qt + 5) * 128,
                               ob * 512:(ob + 1) * 512], "qa")
                    m_xb = S.mark("qa")
                    K.wm("vector", m_mmall)
                    K.wm("vector", m_xb)
                    K.wu_("vector", ("x2st", t4 % 2))
                    K.v(lambda e, t4=t4, bi=bi: e.tensor_tensor(
                        ap(x2s[t4 % 2])[:], psA[t4].ap()[:],
                        ap(xb[bi])[:], OP.add))
                    K.u[("psA", t4)] = S.mark("ve")
                    K.u[("xb", bi)] = S.mark("ve")
                    m_x2 = S.mark("ve")
                    K.wm("scalar", m_x2)
                    S.dma("scalar", x2_d[qt * 128:(qt + 1) * 128, ob * 512:(ob + 1) * 512],
                          ap(x2s[t4 % 2])[:], "qa")
                    K.u[("x2st", t4 % 2)] = S.mark("qa")
            K.u["avb"] = S.mark("pe")


        m_ln1t = m_ln1t_next

def emit_rope_evac(K, pai, o, g, m_mm):
    S, ap = K.S, K.ap
    tok0 = g * 512
    K.wm("vector", m_mm)
    K.wu_("vector", "disc_rope")
    K.v(lambda e: e.tensor_copy(ap("disc")[:, 0:512], K.psA[pai].ap()[:]))
    m_qb = S.mark("ve")
    K.wm("tensor", m_qb)
    K.wu_("tensor", ("psS", 0))
    K.p(lambda e: e.matmul(K.psS[0].ap()[:, 0:512], ap("swp16")[:],
                           ap("disc")[:, 0:512], start=True, stop=True), inc=True)
    m_sw = S.mark("pe")
    K.wm("vector", m_sw)
    K.v(lambda e: e.tensor_copy(ap("disc")[:, 512:1024], K.psS[0].ap()[:, 0:512]))
    K.u[("psS", 0)] = S.mark("ve")
    K.vself()
    K.v(lambda e, tok0=tok0: e.tensor_tensor(
        ap("disc")[:, 1024:1536], ap("disc")[:, 0:512],
        ap("cs1")[:, tok0:tok0 + 512], OP.mult))
    K.v(lambda e, tok0=tok0: e.tensor_tensor(
        ap("disc")[:, 512:1024], ap("disc")[:, 512:1024],
        ap("cs2")[:, tok0:tok0 + 512], OP.mult))
    K.vself()
    if o < 16:
        off = o * 512
        K.wu_("vector", "qrot_free")
        K.v(lambda e, off=off: e.tensor_tensor(
            ap("qrot")[:, off:off + 512], ap("disc")[:, 1024:1536],
            ap("disc")[:, 512:1024], OP.add))
    else:
        off = (o - 16) * TOKH + tok0
        K.v(lambda e, off=off: e.tensor_tensor(
            ap("krot")[:, off:off + 512], ap("disc")[:, 1024:1536],
            ap("disc")[:, 512:1024], OP.add))
    K.u["disc_rope"] = S.mark("ve")


def emit_attn_tile(K, qt, t4, m_a1):
    S, ap = K.S, K.ap
    psA, psS = K.psA, K.psS
    K.wm("tensor", m_a1)

    def scores(h):
        hv = h // 4
        hp = h % 2
        ps = psS[hp]
        K.wu_("tensor", ("psS", hp))
        for kb in range(5):
            K.p(lambda e, h=h, hv=hv, kb=kb, ps=ps: e.matmul(
                ps.ap()[:, kb * 128:(kb + 1) * 128],
                ap("krot")[:, hv * TOKH + (qt + kb) * 128: hv * TOKH + (qt + kb + 1) * 128],
                ap("qrot")[:, h * 512 + t4 * 128: h * 512 + (t4 + 1) * 128],
                start=True, stop=True), inc=(kb == 4))
        return S.mark("pe")

    def softmax(h, m_s):
        hp = h % 2
        ps = psS[hp]
        K.wm("vector", m_s)
        K.v(lambda e, ps=ps: e.tensor_tensor(ps.ap()[:, 0:128], ps.ap()[:, 0:128],
                                             ap("trim")[:, 256:384], OP.add))
        K.v(lambda e, ps=ps: e.tensor_tensor(ps.ap()[:, 512:640],
                                             ps.ap()[:, 512:640],
                                             ap("trim")[:, 384:512], OP.add))
        for kb in range(0, max(0, 4 - qt)):
            K.v(lambda e, kb=kb, ps=ps: e.tensor_scalar(
                ps.ap()[:, kb * 128:(kb + 1) * 128],
                ps.ap()[:, kb * 128:(kb + 1) * 128],
                ap("blkcol")[:, qt + kb: qt + kb + 1], None, OP.add))
        m_msk = S.mark("ve")
        K.wm("scalar", m_msk)
        K.wu_("scalar", ("pt16", hp))
        K.a(lambda e, ps=ps, hp=hp: e.activation(
            ap("pt16")[:, hp * 640:hp * 640 + 640], ps.ap()[:], AF.Exp,
            scale=ISR, bias=ap("m12c")[:, 0:1]))
        m_exp = S.mark("ac")
        K.u[("psS", hp)] = m_exp
        return m_exp

    def av(h, m_exp):
        hp = h % 2
        hv = h // 4
        pa = psA[hp]
        K.wm("tensor", m_exp)
        K.wu_("tensor", ("psA", hp))
        for kb in range(5):
            K.p(lambda e, kb=kb, hv=hv, hp=hp, pa=pa: e.matmul(
                pa.ap()[:, 0:128],
                ap("vsb")[:, (qt + kb) * 512 + hv * 128: (qt + kb) * 512 + (hv + 1) * 128],
                ap("pt16")[:, hp * 640 + kb * 128: hp * 640 + (kb + 1) * 128],
                start=(kb == 0), stop=(kb == 4)))
        for kb in range(5):
            K.p(lambda e, kb=kb, hp=hp, pa=pa: e.matmul(
                pa.ap()[0:1, 128:256], ap("onesc16")[:],
                ap("pt16")[:, hp * 640 + kb * 128: hp * 640 + (kb + 1) * 128],
                start=(kb == 0), stop=(kb == 4)), inc=(kb == 4))
        m_av = S.mark("pe")
        K.u[("pt16", hp)] = m_av
        K.wm("vector", m_av)
        K.wu_("vector", "avb")
        K.v(lambda e, h=h, pa=pa: e.tensor_copy(
            ap("avb")[:, t4 * 2048 + h * 128: t4 * 2048 + (h + 1) * 128],
            pa.ap()[:, 0:128]))
        K.v(lambda e, h=h, pa=pa: e.tensor_copy(
            ap("rbsb")[0:1, h * 128:(h + 1) * 128], pa.ap()[0:1, 128:256]))
        K.u[("psA", hp)] = S.mark("ve")

    pend = None
    for h in range(16):
        m_s = scores(h)
        if pend is not None:
            av(*pend)
        m_exp = softmax(h, m_s)
        pend = (h, m_exp)
        emit_wscale_chunks(K, 3)
    av(*pend)
    K.vself()
    def _recip_bf16(e):
        with K.nc.allow_low_precision(reason="softmax 1/L at bf16"):
            return e.reciprocal(ap("rbsb")[0:1, 0:2048], ap("rbsb")[0:1, 0:2048])
    K.v(_recip_bf16)
    m_rl = S.mark("ve")
    K.wm("tensor", m_rl)
    for c4 in range(4):
        K.wu_("tensor", ("psAr", c4 % 2))
        K.p(lambda e, c4=c4: e.matmul(
            psA[2 + c4 % 2].ap()[:], ap("biasrow")[0:1, 5120:5248],
            ap("rbsb")[0:1, c4 * 512:(c4 + 1) * 512], start=True, stop=True),
            inc=True)
        m_rb = S.mark("pe")
        K.wm("vector", m_rb)
        K.wu_("vector", ("rbsb", c4 % 2))
        K.v(lambda e, c4=c4: e.tensor_copy(ap("rbsb")[:, c4 * 512:(c4 + 1) * 512],
                                           psA[2 + c4 % 2].ap()[:]))
        K.u[("psAr", c4 % 2)] = S.mark("ve")
    K.vself()
    K.v(lambda e: e.tensor_tensor(ap("avb")[:, t4 * 2048:(t4 + 1) * 2048],
                                  ap("avb")[:, t4 * 2048:(t4 + 1) * 2048],
                                  ap("rbsb")[:, 0:2048], OP.mult))
    K.u["avb"] = S.mark("ve")
    K.u[("rbsb", 0)] = S.mark("ve")
    K.u[("rbsb", 1)] = S.mark("ve")
    K.u["qrot_free"] = S.mark("pe")


def emit_ffn(K, es_root, x2_d, h_d, n2t_d, wg_in, wu_in, wd_in, out, nt,
             subs="0123"):
    import contextlib
    S, ap = K.S, K.ap
    psA, psS = K.psA, K.psS

    K.sb(es_root, "hcat16", [128, 2048], BF16)
    es01 = contextlib.ExitStack()
    with (contextlib.nullcontext(es_root) if K.dump else es01) as esn:
        K.sb(esn, "n1T", [128, 16 * NT * 128], BF16)
        es0 = contextlib.ExitStack()
        with (contextlib.nullcontext(es_root) if (set(K.dump) & {"x2a0","lnb","lnT","lnTp","n1b","rt","dsc0"}) else es0) as es:
          if True:
            K.sb(es, "x2a0", [128, 2048], F32)
            K.sb(es, "x2a1", [128, 2048], F32)
            K.sb(es, "lnb", [128, 2048], BF16)
            K.sb(es, "lnT", [128, 2048], BF16)
            K.sb(es, "lnTp", [128, 2048], BF16)
            K.sb(es, "n1b", [128, 2048], BF16)
            K.sb(es, "rt", [128, 512], F32)
            K.sb(es, "dsc0", [128, 2048], BF16)
            S.dma("sync", ap("x2a0")[:], K.dram["hcat"].ap()[:], "qs")
            K.wm("vector", S.mark("qs"))
            K.v(lambda e: e.tensor_copy(ap("hcat16")[:], ap("x2a0")[:]))
            K.u[("x2a", 0)] = S.mark("ve")
            if "0" in subs:
                for t in range(nt):
                    emit_b0_tile(K, x2_d, t)
        K.barrier()

        es1 = contextlib.ExitStack()
        with (contextlib.nullcontext(es_root) if (set(K.dump) & {"wstg0","wstg1","tq","trn","wgT0","wgT1","wuT0","wuT1","sg","uu","h160","h161"}) else es1) as es:
          if "1" in subs:
            K.sb(es, "wstg0", [128, 2048], F32)
            K.sb(es, "wstg1", [128, 2048], F32)
            K.sb(es, "tq0", [128, 2048], BF16)
            K.sb(es, "tq1", [128, 2048], BF16)
            K.sb(es, "wgT0", [128, 16 * 512], BF16)
            K.sb(es, "wgT1", [128, 16 * 512], BF16)
            K.sb(es, "wuT0", [128, 16 * 512], BF16)
            K.sb(es, "wuT1", [128, 16 * 512], BF16)
            K.sb(es, "sg", [128, 512], F32)
            K.sb(es, "uu", [128, 512], F32)
            K.sb(es, "h160", [128, 512], BF16)
            K.sb(es, "h161", [128, 512], BF16)
            if "1" in subs:
                for j in range(16):
                    emit_b1_j(K, wg_in, wu_in, h_d, j, nt)
        K.barrier()
    K.barrier()

    es2 = contextlib.ExitStack()
    with (contextlib.nullcontext(es_root) if (set(K.dump) & {"hl0","hl1","hT","hTp","z32","n2b","n2st"}) else es2) as es:
      if "2" in subs:
        K.sb(es, "hl0", [128, IFF], BF16)
        K.sb(es, "hl1", [128, IFF], BF16)
        K.sb(es, "hT0", [128, IFF], BF16)
        K.sb(es, "hTp0", [128, IFF], BF16)
        K.sb(es, "hT1", [128, IFF], BF16)
        K.sb(es, "hTp1", [128, IFF], BF16)
        K.sb(es, "z32", [128, IFF], F32)
        K.sb(es, "n2b", [128, IFF], BF16)
        K.sb(es, "n2st", [128, IFF], BF16)
        if "2" in subs:
            for t in range(nt):
                emit_b2a_tile(K, h_d, n2t_d, t)
    K.barrier()

    es3 = contextlib.ExitStack()
    with (contextlib.nullcontext(es_root) if (set(K.dump) & {"wdstg0","wdstg1","tq2","trn2","wdtj","n2buf0","n2buf1","x2b0","x2b1","outst0","outst1"}) else es3) as es:
      if "3" in subs:
        K.sb(es, "wdstg0", [128, 2048], F32)
        K.sb(es, "wdstg1", [128, 2048], F32)
        K.sb(es, "tq20", [128, 2048], BF16)
        K.sb(es, "tq21", [128, 2048], BF16)
        K.sb(es, "wdtj", [128, 64 * 512], BF16)
        K.sb(es, "n2buf0", [128, IFF], BF16)
        K.sb(es, "n2buf1", [128, IFF], BF16)
        K.sb(es, "x2b0", [128, 512], F32)
        K.sb(es, "x2b1", [128, 512], F32)
        K.sb(es, "outst0", [128, 512], F32)
        K.sb(es, "outst1", [128, 512], F32)
        if "3" in subs:
            for j in range(4):
                emit_b2b_j(K, wd_in, n2t_d, x2_d, out, j, nt)


def emit_ln_cols(K, xtile, dsc_name):
    S, ap = K.S, K.ap
    K.wu_("vector", "cols_act")
    K.v(lambda e: e.reduce_sum(ap("cols")[:, 0:1], xtile, AX))
    m_sum = S.mark("ve")
    K.wu_("scalar", "cols_free")
    K.a(lambda e: e.activation(ap(dsc_name)[:], xtile, AF.Square,
                               accum_out=ap("cols")[:, 1:2]))
    m_sq = S.mark("ac")
    K.wm("scalar", m_sum)
    K.aself()
    K.a(lambda e: e.activation(ap("cols")[:, 2:3], ap("cols")[:, 0:1],
                               AF.Identity, scale=1.0 / 2048))
    K.wm("vector", m_sq)
    K.v(lambda e: e.tensor_scalar(ap("cols")[:, 3:4], ap("cols")[:, 1:2],
                                  1.0 / 2048, EPS, OP.mult, OP.add))
    m_q = S.mark("ve")
    K.aself()
    K.a(lambda e: e.activation(ap("cols")[:, 4:5], ap("cols")[:, 2:3], AF.Square))
    K.wm("scalar", m_q)
    K.aself()
    K.a(lambda e: e.activation(ap("cols")[:, 5:6], ap("cols")[:, 4:5],
                               AF.Identity, scale=-1.0, bias=ap("cols")[:, 3:4]))
    K.aself()
    K.a(lambda e: e.activation(ap("cols")[:, 6:7], ap("cols")[:, 5:6], AF.Ln))
    K.aself()
    K.a(lambda e: e.activation(ap("cols")[:, 6:7], ap("cols")[:, 6:7],
                               AF.Exp, scale=-0.5))
    K.aself()
    K.a(lambda e: e.activation(ap("cols")[:, 7:8], ap("cols")[:, 2:3],
                               AF.Identity, scale=ap("cols")[:, 6:7]))
    K.u["cols_act"] = S.mark("ac")
    return S.mark("ac")


def emit_dmat(K, src_ap3, dst_ap3, src_mark, eng="sync"):
    S = K.S
    q = {"sync": "qs", "scalar": "qa"}[eng]
    K.wm(eng, src_mark)
    S.do(eng, lambda e, o=dst_ap3, i=src_ap3: e.dma_start_transpose(o, i),
         q, 16)
    return S.mark(q)


def blk3(ap2, nblk, blk_stride, off=0):
    if blk_stride == 128:
        return ap2[:, off:off + nblk * 128].rearrange("p (k q) -> p k q", q=128)
    k0, off_in = divmod(off, blk_stride)
    assert off_in + 128 <= blk_stride
    v = ap2.rearrange("p (k s) -> p k s", s=blk_stride)
    return v[:, k0:k0 + nblk, off_in:off_in + 128]


def emit_b0_tile(K, x2_d, t):
    S, ap = K.S, K.ap
    psA = K.psA
    bi = t % 2
    xn = "x2a0" if bi == 0 else "x2a1"
    K.wu_("sync", ("x2a", bi))
    S.dma("sync", ap(xn)[:], x2_d[t * 128:(t + 1) * 128, :], "qs")
    mx = S.mark("qs")
    K.wm("vector", mx)
    K.wm("scalar", mx)
    m_act = emit_ln_cols(K, ap(xn)[:], "dsc0")
    K.wm("vector", m_act)
    K.wu_("vector", "lnb_r")
    K.v(lambda e, xn=xn: e.tensor_scalar(ap("lnb")[:], ap(xn)[:],
                                         ap("cols")[:, 6:7], ap("cols")[:, 7:8],
                                         OP.mult, OP.subtract))
    K.u[("x2a", bi)] = S.mark("ve")
    K.u["cols_free"] = S.mark("ve")
    m_lnb = S.mark("ve")
    K.wu_("sync", "lnT_r")
    m_t = emit_dmat(K, blk3(ap("lnb")[:], 16, 128), blk3(ap("lnT")[:], 16, 128),
                    m_lnb)
    K.u["lnb_r"] = m_t
    K.wm("vector", m_t)
    K.vself()
    for st_i, stride in ((0, 512), (1, 1024)):
        a, b = ("lnT", "lnTp") if st_i == 0 else ("lnTp", "lnT")
        nrun = 2048 // (2 * stride)
        for r in range(nrun):
            off = r * 2 * stride
            K.v(lambda e, a=a, b=b, off=off, stride=stride: e.tensor_tensor(
                ap(b)[:, off:off + stride], ap(a)[:, off:off + stride],
                ap(a)[:, off + stride:off + 2 * stride], OP.add))
            K.v(lambda e, a=a, b=b, off=off, stride=stride: e.tensor_tensor(
                ap(b)[:, off + stride:off + 2 * stride], ap(a)[:, off:off + stride],
                ap(a)[:, off + stride:off + 2 * stride], OP.subtract))
        K.vself()
    m_stg = S.mark("ve")
    K.wm("tensor", m_stg)
    for gg in range(4):
        K.wu_("tensor", ("psA", gg))
        for lo in range(4):
            c = 4 * gg + lo
            K.p(lambda e, c=c, lo=lo, gg=gg: e.matmul(
                psA[gg].ap()[:], ap("lnT")[:, c * 128:(c + 1) * 128],
                ap("hcat16")[:, lo * 512:(lo + 1) * 512],
                start=(lo == 0), stop=(lo == 3)), inc=(lo == 3))
    m_y1 = S.mark("pe")
    K.u["lnT_r"] = m_y1
    K.wm("vector", m_y1)
    for gg in range(4):
        K.v(lambda e, gg=gg: e.tensor_reduce(ap("cols")[:, 16 + gg:17 + gg],
                                             psA[gg].ap()[:], AX, OP.max,
                                             apply_absolute_value=True))
    K.vself()
    K.v(lambda e: e.tensor_tensor(ap("cols")[:, 20:21], ap("cols")[:, 16:17],
                                  ap("cols")[:, 17:18], OP.max))
    K.v(lambda e: e.tensor_tensor(ap("cols")[:, 21:22], ap("cols")[:, 18:19],
                                  ap("cols")[:, 19:20], OP.max))
    K.vself()
    K.v(lambda e: e.tensor_tensor(ap("cols")[:, 20:21], ap("cols")[:, 20:21],
                                  ap("cols")[:, 21:22], OP.max))
    K.vself()
    K.v(lambda e: e.tensor_scalar(ap("cols")[:, 21:22], ap("cols")[:, 20:21],
                                  EPS * math.sqrt(2048.0), None, OP.max))
    K.vself()
    K.v(lambda e: e.reciprocal(ap("cols")[:, 22:23], ap("cols")[:, 21:22]))
    K.vself()
    K.v(lambda e: e.tensor_scalar(ap("cols")[:, 23:24], ap("cols")[:, 22:23],
                                  127.0, None, OP.mult))
    K.vself()
    K.v(lambda e, t=t: e.tensor_scalar(ap("cols")[:, 48 + t:49 + t],
                                       ap("cols")[:, 21:22],
                                       ap("swcol")[:, 0:1],
                                       1.0 / (127 * math.sqrt(2048.0)),
                                       OP.mult, OP.mult))
    K.v(lambda e, t=t: e.tensor_scalar(ap("cols")[:, 64 + t:65 + t],
                                       ap("cols")[:, 21:22],
                                       ap("swcol")[:, 1:2],
                                       1.0 / (127 * math.sqrt(2048.0)),
                                       OP.mult, OP.mult))
    K.vself()
    K.wm("scalar", S.mark("ve"))
    K.wu_("scalar", ("n1b_r", 0))
    for gg in range(4):
        K.a(lambda e, gg=gg: e.activation(ap("rt")[:], psA[gg].ap()[:],
                                          AF.Identity, scale=ap("cols")[:, 23:24],
                                          bias=ap("acol")[:, 0:1]))
        K.aself()
        K.a(lambda e, gg=gg: e.activation(
            ap("n1b")[:, gg * 512:(gg + 1) * 512], ap("rt")[:], AF.Identity,
            bias=ap("nacol")[:, 0:1]))
        K.u[("psA", gg)] = S.mark("ac")
        K.aself()
    K.wm("vector", S.mark("ac"))
    m_n1b = S.mark("ac")
    n1t4 = K.ap("n1T").rearrange("p (k t q) -> p k t q", t=NT, q=128)[:, :, t, :]
    K.wu_("sync", "n1T_w")
    m_t2 = emit_dmat(K, blk3(ap("n1b")[:], 16, 128), n1t4, m_n1b)
    K.u[("n1b_r", 0)] = m_t2
    K.u["n1T_ld"] = m_t2


def emit_b1_j(K, wg_in, wu_in, h_d, j, nt):
    S, ap = K.S, K.ap
    psA = K.psA
    par = j % 2
    gname = "wgT0" if par == 0 else "wgT1"
    uname = "wuT0" if par == 0 else "wuT1"
    for dst, src, widx, stg, tq, qeng, qn in (
            (gname, wg_in, 0, "wstg0", "tq0", "sync", "qs"),
            (uname, wu_in, 1, "wstg1", "tq1", "sync", "qs")):
        for g in range(4):
            K.wu_(qeng, (stg, 0))
            for i in range(4):
                k = 4 * g + i
                S.dma(qeng, ap(stg)[:, i * 512:(i + 1) * 512],
                      src[k * 128:(k + 1) * 128, j * 512:(j + 1) * 512], qn)
            mw = S.mark(qn)
            K.wm("vector", mw)
            if g == 0:
                K.wu_("vector", ("wT_r", par))
            K.v(lambda e, stg=stg, tq=tq, widx=widx: e.tensor_scalar(
                ap(tq)[:], ap(stg)[:], ap("swcol")[:, 8 + widx:9 + widx],
                None, OP.is_lt))
            K.vself()
            K.v(lambda e, stg=stg, tq=tq, dst=dst, g=g, widx=widx:
                e.scalar_tensor_tensor(
                    ap(dst)[:, g * 2048:(g + 1) * 2048], ap(stg)[:],
                    ap("swcol")[:, 4 + widx:5 + widx],
                    ap(tq)[:], OP.is_gt, OP.subtract))
            K.u[(stg, 0)] = S.mark("ve")
    m_w = S.mark("ve")
    K.wm("tensor", m_w)
    K.wu_("tensor", "n1T_ld")
    for t in range(nt):
        pg = 2 * (t % 2)
        pu = pg + 1
        K.wu_("tensor", ("psA", pg))
        K.wu_("tensor", ("psA", pu))
        for k in range(16):
            K.p(lambda e, k=k, t=t, pg=pg, gname=gname: e.matmul(
                psA[pg].ap()[:],
                ap("n1T")[:, (k * NT + t) * 128:(k * NT + t + 1) * 128],
                ap(gname)[:, k * 512:(k + 1) * 512],
                start=(k == 0), stop=(k == 15)), inc=(k == 15))
            K.p(lambda e, k=k, t=t, pu=pu, uname=uname: e.matmul(
                psA[pu].ap()[:],
                ap("n1T")[:, (k * NT + t) * 128:(k * NT + t + 1) * 128],
                ap(uname)[:, k * 512:(k + 1) * 512],
                start=(k == 0), stop=(k == 15)), inc=(k == 15))
        m_mm = S.mark("pe")
        K.wm("scalar", m_mm)
        K.wu_("scalar", ("sgq", 0))
        K.a(lambda e, t=t, pg=pg: e.activation(ap("sg")[:], psA[pg].ap()[:],
                                               AF.Silu,
                                               scale=ap("cols")[:, 48 + t:49 + t]))
        m_sg = S.mark("ac")
        K.wm("vector", m_mm)
        K.wu_("vector", ("h16", t % 2))
        K.v(lambda e, t=t, pu=pu: e.tensor_scalar(ap("uu")[:], psA[pu].ap()[:],
                                                  ap("cols")[:, 64 + t:65 + t],
                                                  None, OP.mult))
        K.wm("vector", m_sg)
        hn = "h160" if t % 2 == 0 else "h161"
        K.v(lambda e, hn=hn: e.tensor_tensor(ap(hn)[:], ap("sg")[:], ap("uu")[:],
                                             OP.mult))
        K.u[("psA", pg)] = S.mark("ve")
        K.u[("psA", pu)] = S.mark("ve")
        K.u[("sgq", 0)] = S.mark("ve")
        if t == nt - 1:
            K.u[("wT_r", par)] = S.mark("pe")
        m_h = S.mark("ve")
        K.wm("gpsimd", m_h)
        S.dma("gpsimd", h_d[t, :, j * 512:(j + 1) * 512], ap(hn)[:], "qg")
        K.u[("h16", t % 2)] = S.mark("qg")


def emit_b2a_tile(K, h_d, n2t_d, t):
    S, ap = K.S, K.ap
    psA = K.psA
    bi = t % 2
    hn = "hl0" if bi == 0 else "hl1"
    hT = "hT0" if bi == 0 else "hT1"
    hTp = "hTp0" if bi == 0 else "hTp1"
    K.wu_("sync", ("hl", bi))
    S.dma("sync", ap(hn)[:], h_d[t], "qs")
    mh = S.mark("qs")
    K.wu_("sync", ("hT_r", bi))
    m_t = emit_dmat(K, blk3(ap(hn)[:], 64, 128), blk3(ap(hT)[:], 64, 128), mh)
    K.u[("hl", bi)] = m_t
    K.wm("vector", m_t)
    K.wu_("vector", ("hTp_r", bi))
    for st_i, stride in enumerate((512, 1024, 2048, 4096)):
        a, b = (hT, hTp) if st_i % 2 == 0 else (hTp, hT)
        nrun = IFF // (2 * stride)
        for r in range(nrun):
            off = r * 2 * stride
            K.v(lambda e, a=a, b=b, off=off, stride=stride: e.tensor_tensor(
                ap(b)[:, off:off + stride], ap(a)[:, off:off + stride],
                ap(a)[:, off + stride:off + 2 * stride], OP.add))
            K.v(lambda e, a=a, b=b, off=off, stride=stride: e.tensor_tensor(
                ap(b)[:, off + stride:off + 2 * stride], ap(a)[:, off:off + stride],
                ap(a)[:, off + stride:off + 2 * stride], OP.subtract))
        K.vself()
    m_stg = S.mark("ve")
    K.u[("hTp_r", bi)] = m_stg
    K.wm("tensor", m_stg)
    for gg in range(16):
        pai = gg % 4
        K.wu_("tensor", ("psA", pai))
        for lo in range(4):
            c = 4 * gg + lo
            K.p(lambda e, c=c, lo=lo, pai=pai, hT=hT: e.matmul(
                psA[pai].ap()[:], ap(hT)[:, c * 128:(c + 1) * 128],
                ap("hcat16")[:, lo * 512:(lo + 1) * 512],
                start=(lo == 0), stop=(lo == 3)), inc=(lo == 3))
        m_mm = S.mark("pe")
        K.wm("scalar", m_mm)
        K.a(lambda e, gg=gg, pai=pai: e.activation(
            ap("z32")[:, gg * 512:(gg + 1) * 512], psA[pai].ap()[:], AF.Identity))
        K.u[("psA", pai)] = S.mark("ac")
    m_z = S.mark("ac")
    K.u[("hT_r", bi)] = S.mark("pe")
    K.wm("vector", m_z)
    K.v(lambda e: e.tensor_reduce(ap("cols")[:, 20:21], ap("z32")[:], AX, OP.max,
                                  apply_absolute_value=True))
    K.vself()
    K.v(lambda e: e.tensor_scalar(ap("cols")[:, 21:22], ap("cols")[:, 20:21],
                                  EPS * math.sqrt(8192.0), None, OP.max))
    K.vself()
    K.v(lambda e: e.reciprocal(ap("cols")[:, 22:23], ap("cols")[:, 21:22]))
    K.vself()
    K.v(lambda e: e.tensor_scalar(ap("cols")[:, 23:24], ap("cols")[:, 22:23],
                                  127.0, None, OP.mult))
    K.v(lambda e, t=t: e.tensor_scalar(ap("cols")[:, 80 + t:81 + t],
                                       ap("cols")[:, 21:22],
                                       ap("swcol")[:, 2:3],
                                       1.0 / (127 * math.sqrt(8192.0)),
                                       OP.mult, OP.mult))
    K.vself()
    m_sc2 = S.mark("ve")
    K.wm("scalar", m_sc2)
    K.a(lambda e: e.activation(ap("z32")[:], ap("z32")[:], AF.Identity,
                               scale=ap("cols")[:, 23:24], bias=ap("acol")[:, 0:1]))
    K.aself()
    K.a(lambda e: e.activation(ap("n2b")[:], ap("z32")[:], AF.Identity,
                               bias=ap("nacol")[:, 0:1]))
    K.aself()
    K.wm("vector", S.mark("ac"))
    m_n2b = S.mark("ac")
    K.wu_("sync", "n2st_st")
    m_t = emit_dmat(K, blk3(ap("n2b")[:], 64, 128), blk3(ap("n2st")[:], 64, 128),
                    m_n2b)
    K.wm("gpsimd", m_t)
    S.dma("gpsimd", n2t_d[:, t, :, :].transpose([1, 0, 2]),
          ap("n2st")[:].rearrange("b (k tok) -> b k tok", tok=128), "qg")
    K.u["n2st_st"] = S.mark("qg")
    K.u["n2t_sp"] = S.mark("qg")


def emit_b2b_j(K, wd_in, n2t_d, x2_d, out, j, nt):
    S, ap = K.S, K.ap
    psA = K.psA
    for g in range(16):
        bi = g % 2
        stg = "wdstg0" if bi == 0 else "wdstg1"
        tq2 = "tq20" if bi == 0 else "tq21"
        K.wu_("scalar", ("wdstg", bi))
        for i in range(4):
            k = 4 * g + i
            S.dma("scalar", ap(stg)[:, i * 512:(i + 1) * 512],
                  wd_in[k * 128:(k + 1) * 128, j * 512:(j + 1) * 512], "qa")
        mw = S.mark("qa")
        K.wm("vector", mw)
        if g == 0:
            K.wu_("vector", "wdtj_r")
        K.v(lambda e, stg=stg, tq2=tq2: e.tensor_scalar(
            ap(tq2)[:], ap(stg)[:], ap("swcol")[:, 10:11], None, OP.is_lt))
        K.vself()
        K.v(lambda e, stg=stg, tq2=tq2, g=g: e.scalar_tensor_tensor(
            ap("wdtj")[:, g * 2048:(g + 1) * 2048], ap(stg)[:],
            ap("swcol")[:, 6:7], ap(tq2)[:], OP.is_gt, OP.subtract))
        K.u[("wdstg", bi)] = S.mark("ve")
    m_w = S.mark("ve")
    K.wm("tensor", m_w)
    for t in range(nt):
        bi = t % 2
        nb = "n2buf0" if bi == 0 else "n2buf1"
        xb = "x2b0" if bi == 0 else "x2b1"
        on = "outst0" if bi == 0 else "outst1"
        nqeng, nqn = ("sync", "qs") if bi == 0 else ("gpsimd", "qg")
        K.wu_(nqeng, ("n2buf", bi))
        S.dma(nqeng, ap(nb)[:].rearrange("b (k tok) -> b k tok", tok=128),
              n2t_d[:, t, :, :].transpose([1, 0, 2]), nqn)
        K.wu_("gpsimd", ("x2b", bi))
        S.dma("gpsimd", ap(xb)[:],
              x2_d[t * 128:(t + 1) * 128, j * 512:(j + 1) * 512], "qg")
        mn = S.mark(nqn)
        mx = S.mark("qg")
        pai = t % 4
        K.wm("tensor", mn)
        K.wu_("tensor", ("psA", pai))
        for k in range(64):
            K.p(lambda e, k=k, nb=nb, pai=pai: e.matmul(
                psA[pai].ap()[:],
                ap(nb)[:, k * 128:(k + 1) * 128],
                ap("wdtj")[:, k * 512:(k + 1) * 512],
                start=(k == 0), stop=(k == 63)), inc=(k == 63))
        m_mm = S.mark("pe")
        K.u[("n2buf", bi)] = m_mm
        if t == nt - 1:
            K.u["wdtj_r"] = m_mm
        K.wm("vector", m_mm)
        K.wm("vector", mx)
        K.wu_("vector", ("outst", bi))
        K.v(lambda e, t=t, xb=xb, on=on, pai=pai: e.scalar_tensor_tensor(
            ap(on)[:], psA[pai].ap()[:], ap("cols")[:, 80 + t:81 + t],
            ap(xb)[:], OP.mult, OP.add))
        K.u[("psA", pai)] = S.mark("ve")
        K.u[("x2b", bi)] = S.mark("ve")
        m_o = S.mark("ve")
        K.wm("gpsimd", m_o)
        S.dma("gpsimd", out[t * 128:(t + 1) * 128, j * 512:(j + 1) * 512],
              ap(on)[:], "qg")
        K.u[("outst", bi)] = S.mark("qg")


_PERMQ = np.concatenate([np.concatenate([np.arange(h * 128, (h + 1) * 128, 2),
                                         np.arange(h * 128 + 1, (h + 1) * 128, 2)])
                         for h in range(NQ)])
_PERMK = np.concatenate([np.concatenate([np.arange(h * 128, (h + 1) * 128, 2),
                                         np.arange(h * 128 + 1, (h + 1) * 128, 2)])
                         for h in range(NKV)])


def host_inputs(L0_attn_x, L0_attn_Wq, L0_attn_bq, L0_attn_Wk, L0_attn_bk,
                L0_attn_Wv, L0_attn_bv, L0_attn_Wo, L0_attn_bo,
                L0_ffn_w_gate, L0_ffn_w_up, L0_ffn_w_down):
    f32 = np.float32
    X = np.asarray(L0_attn_x, f32)
    wq = np.asarray(L0_attn_Wq, f32)[:, _PERMQ]
    wk = np.asarray(L0_attn_Wk, f32)[:, _PERMK]
    wv = np.asarray(L0_attn_Wv, f32)
    wqkv = np.ascontiguousarray(np.concatenate([wq, wk, wv], axis=1))
    bqkv = np.ascontiguousarray(np.concatenate(
        [np.asarray(L0_attn_bq, f32)[_PERMQ], np.asarray(L0_attn_bk, f32)[_PERMK],
         np.asarray(L0_attn_bv, f32)], axis=0))[None, :]
    wo = np.ascontiguousarray(np.asarray(L0_attn_Wo, f32))
    bo = np.ascontiguousarray(np.asarray(L0_attn_bo, f32)[None, :])
    wg = np.ascontiguousarray(np.asarray(L0_ffn_w_gate, f32).T)
    wu = np.ascontiguousarray(np.asarray(L0_ffn_w_up, f32).T)
    wd = np.ascontiguousarray(np.asarray(L0_ffn_w_down, f32).T)

    ident = np.eye(128, dtype=f32)
    r = np.arange(128)
    tri_lo = np.where(r[None, :] >= r[:, None], 0.0, NEG).astype(f32)
    tri_hi = np.where(r[None, :] <= r[:, None], 0.0, NEG).astype(f32)
    tri = np.ascontiguousarray(np.concatenate(
        [tri_lo, tri_hi, tri_lo.T, tri_hi.T], axis=1))
    swp = np.zeros((128, 128), f32)
    for p in range(128):
        swp[(p + 64) % 128, p] = 1.0
    sgn = np.concatenate([-np.ones((64, 1), f32), np.ones((64, 1), f32)])
    h128 = _hadamard(128)
    h4 = _hadamard(4)
    hcat = np.concatenate(
        [np.concatenate([h4[lp, lo] * h128 for lp in range(4)], axis=1)
         for lo in range(4)], axis=1).astype(f32)

    maps = []
    for c in range(8):
        b, h = c // 2, c % 2
        xs = np.zeros((TOKH, E), f32)
        if h == 0:
            xs[HALO:] = X[b, 0:TOK]
        else:
            xs[:] = X[b, TOK - HALO: SEQ]
        pos0 = float(h * TOK - HALO)
        blk = np.zeros((128, 4), f32)
        if h == 0:
            blk[:, :] = NEG
        maps.append(dict(
            x=xs, wqkv=wqkv, wo=wo, bqkv=bqkv, bo=bo,
            wg=wg, wu=wu, wd=wd,
            pos0=np.full((128, 1), pos0, f32),
            swp=swp, sgn=sgn,
            blkmask=np.ascontiguousarray(blk), ident=ident,
            tri=tri, hcat=np.ascontiguousarray(hcat),
        ))
    return maps


def kernel(**inputs):
    K = build_program(debug=False, phases="AB")
    maps = host_inputs(**inputs)
    res = run_bass_kernel_spmd(K.nc, maps, core_ids=list(range(8)))
    outp = np.empty((B, SEQ, E), np.float32)
    for c in range(8):
        b, h = c // 2, c % 2
        outp[b, h * TOK:(h + 1) * TOK] = res.results[c]["out"]
    return outp


# version 09
# speedup vs baseline: 1.0780x; 1.0780x over previous
import math
import contextlib
import numpy as np

import concourse.bass as bass
import concourse.mybir as mybir
from concourse.bass_utils import run_bass_kernel_spmd
from concourse.alu_op_type import AluOpType as OP

F32 = mybir.dt.float32
BF16 = mybir.dt.bfloat16
AF = mybir.ActivationFunctionType
AX = mybir.AxisListType.X

B, SEQ, E = 4, 4096, 2048
NQ, NKV, HD = 16, 4, 128
IFF = 8192
TOK = 2048
HALO = 512
TOKH = TOK + HALO
NT = TOK // 128
NTH = TOKH // 128
NEG = -1.0e30
C23 = 1.5 * 2.0 ** 23
MAGIC_K = float(np.float32((1.5 * 2.0 ** 52 - C23) * 2.0 ** 30))
EPS = 1.0e-5
ISR = float(np.float32(1.0 / math.sqrt(HD)))
TWO_PI_HI = float(np.float32(2.0 * math.pi))
TWO_PI_LO = float(np.float32(2.0 * math.pi - np.float64(np.float32(2.0 * math.pi))))
HALF_PI = float(np.float32(0.5 * math.pi))
PI_CLAMP = 3.1415925
NLN1E4_O64 = float(np.float32(-math.log(10000.0) / 64.0))

ENGINES = ("sync", "scalar", "vector", "tensor", "gpsimd")


def _hadamard(n):
    h = np.array([[1.0]])
    while h.shape[0] < n:
        h = np.block([[h, h], [h, -h]])
    return h


class Sched:

    def __init__(self):
        self.q = {e: [] for e in ENGINES}
        self.cnt = {}
        self.sems = {}

    def sem(self, name):
        self.cnt.setdefault(name, 0)

    def on(self, eng, fn):
        self.q[eng].append(fn)

    def wait(self, eng, sem, val=None):
        v = self.cnt[sem] if val is None else val
        if v > 0:
            self.on(eng, lambda e, s=sem, vv=v: e.wait_ge(self.sems[s], vv))

    def do(self, eng, build, inc=None, k=1):
        if inc is not None:
            self.cnt[inc] += k

            def fn(e, b=build, s=inc, kk=k):
                b(e).then_inc(self.sems[s], kk)
            self.on(eng, fn)
        else:
            self.on(eng, lambda e, b=build: b(e))

    def dma(self, eng, out, in_, inc):
        self.do(eng, lambda e, o=out, i=in_: e.dma_start(out=o, in_=i), inc, 16)

    def mark(self, sem):
        return (sem, self.cnt[sem])

    def emit(self, block):
        for eng in ENGINES:
            ops = self.q[eng]

            def section(e, ops=ops):
                for fn in ops:
                    fn(e)
            getattr(block, eng)(section)


class Kern:
    def __init__(self, debug=False):
        self.nc = bass.Bass("TRN2", target_bir_lowering=False)
        self.debug = debug
        self.S = Sched()
        for s in ("qs", "qa", "qg", "pe", "ve", "ac"):
            self.S.sem(s)
        self.t = {}
        self.dram = {}
        self.u = {}

    def sb(self, es, name, shape, dtype):
        h = es.enter_context(self.nc.sbuf_tensor(name, list(shape), dtype))
        self.t[name] = h
        return h

    def psum(self, es, name, shape, dtype):
        h = es.enter_context(self.nc.psum_tensor(name, list(shape), dtype))
        self.t[name] = h
        return h

    def din(self, name, shape, dtype=F32):
        h = self.nc.dram_tensor(name, list(shape), dtype, kind="ExternalInput")
        self.dram[name] = h
        return h

    def dout(self, name, shape, dtype=F32):
        h = self.nc.dram_tensor(name, list(shape), dtype, kind="ExternalOutput")
        self.dram[name] = h
        return h

    def dtmp(self, name, shape, dtype):
        h = self.nc.dram_tensor(name, list(shape), dtype, kind="Internal")
        self.dram[name] = h
        return h

    def ap(self, name):
        return self.t[name].ap()

    def w(self, eng, *sems):
        for s in sems:
            self.S.wait(eng, s)

    def wm(self, eng, mark):
        if mark is not None:
            self.S.wait(eng, mark[0], mark[1])

    def wu_(self, eng, key):
        self.wm(eng, self.u.get(key))

    def v(self, build):
        self.S.do("vector", build, "ve")

    def a(self, build):
        self.S.do("scalar", build, "ac")

    def p(self, build, inc=False):
        self.S.do("tensor", build, "pe" if inc else None)

    def vself(self):
        self.S.wait("vector", "ve")

    def aself(self):
        self.S.wait("scalar", "ac")

    def barrier(self):
        for eng in ENGINES:
            for s in ("pe", "ve", "ac", "qs", "qa", "qg"):
                self.S.wait(eng, s)


def build_program(debug=False, phases="AB", nt=NT, dump=(), subs="0123"):
    K = Kern(debug=debug)
    K.dump = dump
    nc, S = K.nc, K.S
    ap = K.ap

    x_in = K.din("x", [TOKH, E]).ap()
    wqkv_in = K.din("wqkv", [E, 3072]).ap()
    wo_in = K.din("wo", [NQ * HD, E]).ap()
    bqkv_in = K.din("bqkv", [1, 3072]).ap()
    bo_in = K.din("bo", [1, E]).ap()
    wg_in = K.din("wg", [E, IFF]).ap()
    wu_in = K.din("wu", [E, IFF]).ap()
    wd_in = K.din("wd", [IFF, E]).ap()
    pos0_in = K.din("pos0", [128, 1]).ap()
    blk_in = K.din("blkmask", [128, 4]).ap()
    ident_in = K.din("ident", [128, 128]).ap()
    tri_in = K.din("tri", [128, 512]).ap()
    hcat_in = K.din("hcat", [128, 2048]).ap()
    swp_in = K.din("swp", [128, 128]).ap()
    sgn_in = K.din("sgn", [128, 1]).ap()

    out = K.dout("out", [TOK, E]).ap()
    wqkv16_d = K.dtmp("wqkv16_d", [E, 3072], BF16).ap()
    wo16_d = K.dtmp("wo16_d", [NQ * HD, E], BF16).ap()
    if "A" not in phases and debug:
        x2_d = K.din("x2_d", [TOK, E]).ap()
    elif debug:
        x2_d = K.dout("x2_d", [TOK, E]).ap()
    else:
        x2_d = K.dtmp("x2_d", [TOK, E], F32).ap()
    if debug:
        h_d = K.dout("h_d", [NT, 128, IFF], BF16).ap()
        n2t_d = K.dout("n2t_d", [64, NT, 128, 128], BF16).ap()
    else:
        h_d = K.dtmp("h_d", [NT, 128, IFF], BF16).ap()
        n2t_d = K.dtmp("n2t_d", [64, NT, 128, 128], BF16).ap()

    es_root = contextlib.ExitStack()
    with es_root:
        K.sb(es_root, "ident16", [128, 128], BF16)
        K.sb(es_root, "swp16", [128, 128], BF16)
        K.sb(es_root, "sgnc", [128, 1], F32)
        K.sb(es_root, "trim", [128, 512], F32)
        K.sb(es_root, "acol", [128, 1], F32)
        K.sb(es_root, "nacol", [128, 1], F32)
        K.sb(es_root, "m12c", [128, 1], F32)
        K.sb(es_root, "onesc", [128, 1], F32)
        K.sb(es_root, "biasrow", [1, 5632], BF16)
        K.sb(es_root, "ones_r32", [1, 128], F32)
        K.sb(es_root, "blkcol", [128, 4], F32)
        K.sb(es_root, "pos0c", [128, 1], F32)
        K.sb(es_root, "freqc", [128, 1], F32)
        K.sb(es_root, "swcol", [128, 12], F32)
        K.sb(es_root, "cols", [128, 112], F32)
        K.sb(es_root, "wsum", [128, 4], F32)
        K.sb(es_root, "onesc16", [128, 1], BF16)
        K.sb(es_root, "wsb0", [128, 512], F32)
        K.sb(es_root, "wsb1", [128, 512], F32)
        K.sb(es_root, "wsb2", [128, 512], F32)
        K.sb(es_root, "wsb3", [128, 512], F32)
        K.sb(es_root, "wsd", [128, 512], BF16)
        K.sb(es_root, "srow", [1, 16], F32)
        K.sb(es_root, "cs1", [128, TOKH], BF16)
        K.sb(es_root, "cs2", [128, TOKH], BF16)

        psA = [K.psum(es_root, f"psA{i}", [128, 512], F32) for i in range(4)]
        psS = [K.psum(es_root, f"psS{i}", [128, 640], F32) for i in range(2)]
        K.psA, K.psS = psA, psS

        with contextlib.ExitStack() as es:
            stg = K.sb(es, "stg", [128, 2048], F32)
            stg2 = K.sb(es, "stg2", [128, 2048], F32)
            w16s = K.sb(es, "w16s", [128, 3072], BF16)
            ang = K.sb(es, "ang", [128, TOKH], F32)
            rr = K.sb(es, "rr", [128, TOKH], F32)
            emit_pre(K, x2_d, wqkv16_d, wo16_d,
                     dict(wqkv_in=wqkv_in, wo_in=wo_in, bqkv_in=bqkv_in,
                          bo_in=bo_in, tri_in=tri_in, blk_in=blk_in,
                          pos0_in=pos0_in, ident_in=ident_in, hcat_in=hcat_in,
                          swp_in=swp_in, sgn_in=sgn_in,
                          wg_in=wg_in, wu_in=wu_in, wd_in=wd_in))
        K.barrier()

        if "A" in phases:
            es_a = contextlib.ExitStack()
            with (contextlib.nullcontext(es_root) if K.dump else es_a) as es:
                K.sb(es, "wbuf0", [128, 16 * 512], BF16)
                K.sb(es, "wbuf1", [128, 16 * 512], BF16)
                K.sb(es, "wvbuf", [128, 8 * 512], BF16)
                K.sb(es, "wobuf", [128, 8 * 512], BF16)
                K.sb(es, "ln1t", [128, 16 * 512], BF16)
                K.sb(es, "ln1b", [128, 2048], BF16)
                K.sb(es, "disc", [128, 1536], BF16)
                K.sb(es, "qrot", [128, 16 * 512], BF16)
                K.sb(es, "krot", [128, 4 * TOKH], BF16)
                K.sb(es, "vsb", [128, NTH * 512], BF16)
                K.sb(es, "xa0", [128, 2048], F32)
                K.sb(es, "xa1", [128, 2048], F32)
                K.sb(es, "xb0", [128, 512], F32)
                K.sb(es, "xb1", [128, 512], F32)
                K.sb(es, "x2s0", [128, 512], F32)
                K.sb(es, "x2s1", [128, 512], F32)
                K.sb(es, "pt16", [128, 2 * 640], BF16)
                K.sb(es, "rbsb", [128, 2048], BF16)
                K.sb(es, "avb", [128, 4 * 2048], BF16)
                emit_attention(K, x_in, wqkv16_d, wo16_d, x2_d, nt)
            K.barrier()

        emit_wscale_fin(K)
        K.barrier()
        if "B" in phases:
            emit_ffn(K, es_root, x2_d, h_d, n2t_d, wg_in, wu_in, wd_in, out, nt,
                     subs=subs)

        for name in K.dump:
            h = K.t[name]
            sh = list(h.ap().shape)
            dt_ = h.ap().dtype
            d = K.dout("dump_" + name, sh, dt_)
            for s in ("pe", "ve", "ac"):
                S.wait("gpsimd", s)
            S.dma("gpsimd", d.ap()[:], h.ap()[:], "qg")
        S.wait("gpsimd", "qg")
        S.wait("gpsimd", "qs")
        S.wait("gpsimd", "qa")
        for eng in ("sync", "scalar"):
            S.wait(eng, "qs")
            S.wait(eng, "qa")

        block = es_root.enter_context(nc.Block())
        for name in list(S.cnt):
            S.sems[name] = es_root.enter_context(nc.semaphore("sem_" + name))
        S.emit(block)
    return K


def emit_pre(K, x2_d, wqkv16_d, wo16_d, D):
    S, ap = K.S, K.ap
    S.dma("sync", ap("trim")[:], D["tri_in"][:], "qs")
    S.dma("sync", ap("blkcol")[:], D["blk_in"][:], "qs")
    S.dma("sync", ap("pos0c")[:], D["pos0_in"][:], "qs")
    S.dma("sync", ap("stg")[:, 0:128], D["ident_in"][:], "qs")
    S.dma("sync", ap("stg2")[:, 0:128], D["swp_in"][:], "qs")
    S.dma("sync", ap("sgnc")[:], D["sgn_in"][:], "qs")
    S.dma("sync", ap("stg")[0:1, 128:2048], D["bqkv_in"][0:1, 0:1920], "qs")
    m = S.mark("qs")
    K.wm("vector", m)
    K.v(lambda e: e.tensor_copy(ap("ident16")[:], ap("stg")[:, 0:128]))
    K.v(lambda e: e.tensor_copy(ap("swp16")[:], ap("stg2")[:, 0:128]))
    K.v(lambda e: e.tensor_copy(ap("biasrow")[0:1, 0:1920], ap("stg")[0:1, 128:2048]))
    K.wm("sync", S.mark("ve"))
    S.dma("sync", ap("stg")[0:1, 0:1152], D["bqkv_in"][0:1, 1920:3072], "qs")
    S.dma("sync", ap("stg2")[0:1, 0:2048], D["bo_in"][0:1, :], "qs")
    m = S.mark("qs")
    K.wm("vector", m)
    K.v(lambda e: e.tensor_copy(ap("biasrow")[0:1, 1920:3072], ap("stg")[0:1, 0:1152]))
    K.v(lambda e: e.tensor_copy(ap("biasrow")[0:1, 3072:5120], ap("stg2")[0:1, 0:2048]))
    K.u["stg_rd"] = S.mark("ve")
    K.v(lambda e: e.memset(ap("onesc")[:], 1.0))
    K.v(lambda e: e.memset(ap("m12c")[:], -12.0))
    K.v(lambda e: e.memset(ap("onesc16")[:], 1.0))
    K.v(lambda e: e.memset(ap("biasrow")[0:1, 5120:5632], 1.0))
    K.v(lambda e: e.memset(ap("ones_r32")[:], 1.0))
    K.v(lambda e: e.tensor_scalar(ap("cols")[:, 0:1], ap("onesc")[:], 2.0 ** -30,
                                  1.0, OP.add, OP.subtract))
    K.vself()
    K.v(lambda e: e.tensor_scalar(ap("acol")[:], ap("cols")[:, 0:1], MAGIC_K,
                                  C23, OP.mult, OP.add))
    K.vself()
    K.v(lambda e: e.tensor_scalar(ap("nacol")[:], ap("acol")[:], -1.0, None,
                                  OP.mult))
    m_acol = S.mark("ve")

    S.do("gpsimd", lambda e: e.iota(ap("freqc")[:], [[1, 1]], channel_multiplier=1,
                                    allow_small_or_imprecise_dtypes=True), "qg", 1)
    S.do("gpsimd", lambda e: e.iota(ap("ang")[:], [[1, TOKH]], channel_multiplier=0,
                                    allow_small_or_imprecise_dtypes=True), "qg", 1)
    m_io = S.mark("qg")
    K.wm("vector", m_io)
    K.v(lambda e: e.tensor_scalar(ap("cols")[:, 9:10], ap("freqc")[:], 64.0, None,
                                  OP.is_ge))
    K.vself()
    K.v(lambda e: e.scalar_tensor_tensor(ap("freqc")[:], ap("cols")[:, 9:10],
                                         -64.0, ap("freqc")[:], OP.mult, OP.add))
    m_f0 = S.mark("ve")
    K.wm("scalar", m_f0)
    K.a(lambda e: e.activation(ap("freqc")[:], ap("freqc")[:], AF.Exp,
                               scale=NLN1E4_O64))
    m_frq = S.mark("ac")
    K.wm("vector", m_frq)
    K.v(lambda e: e.tensor_scalar(ap("ang")[:], ap("ang")[:], ap("pos0c")[:, 0:1],
                                  ap("freqc")[:, 0:1], OP.add, OP.mult))
    K.vself()
    for idx, shift in ((0, 0.0), (1, HALF_PI)):
        K.wm("vector", m_acol)
        K.v(lambda e, shift=shift: e.tensor_scalar(
            ap("rr")[:], ap("ang")[:], 1.0 / (2 * math.pi),
            shift / (2 * math.pi), OP.mult, OP.add))
        K.vself()
        K.v(lambda e: e.tensor_scalar(ap("rr")[:], ap("rr")[:],
                                      ap("acol")[:, 0:1], ap("acol")[:, 0:1],
                                      OP.add, OP.subtract))
        K.vself()
        K.v(lambda e: e.scalar_tensor_tensor(
            ap("rr")[:], ap("rr")[:], -TWO_PI_HI, ap("ang")[:],
            OP.mult, OP.add))
        K.vself()
        K.v(lambda e, shift=shift: e.tensor_scalar(
            ap("rr")[:], ap("rr")[:], shift, -PI_CLAMP, OP.add, OP.max))
        K.vself()
        K.v(lambda e: e.tensor_scalar(ap("rr")[:], ap("rr")[:], PI_CLAMP, None,
                                      OP.min))
        m_r = S.mark("ve")
        K.wm("scalar", m_r)
        if idx == 0:
            K.a(lambda e: e.activation(ap("rr")[:], ap("rr")[:], AF.Sin))
            m_sn = S.mark("ac")
            K.wm("vector", m_sn)
            K.v(lambda e: e.tensor_scalar(ap("cs2")[:], ap("rr")[:],
                                          ap("sgnc")[:, 0:1], None, OP.mult))
        else:
            K.a(lambda e: e.activation(ap("cs1")[:], ap("rr")[:], AF.Sin))

    for kc in range(16):
        K.wu_("sync", "stg_rd")
        S.dma("sync", ap("stg")[:, 0:2048], D["wqkv_in"][kc * 128:(kc + 1) * 128, 0:2048], "qs")
        S.dma("sync", ap("stg2")[:, 0:1024], D["wqkv_in"][kc * 128:(kc + 1) * 128, 2048:3072], "qs")
        m = S.mark("qs")
        K.wm("vector", m)
        K.wu_("vector", "w16s_st")
        K.v(lambda e: e.tensor_copy(ap("w16s")[:, 0:2048], ap("stg")[:, 0:2048]))
        K.v(lambda e: e.tensor_copy(ap("w16s")[:, 2048:3072], ap("stg2")[:, 0:1024]))
        K.u["stg_rd"] = S.mark("ve")
        m = S.mark("ve")
        K.wm("scalar", m)
        S.dma("scalar", wqkv16_d[kc * 128:(kc + 1) * 128, :], ap("w16s")[:, 0:3072], "qa")
        K.u["w16s_st"] = S.mark("qa")
    for kc in range(16):
        K.wu_("sync", "stg_rd")
        S.dma("sync", ap("stg")[:, 0:2048], D["wo_in"][kc * 128:(kc + 1) * 128, :], "qs")
        m = S.mark("qs")
        K.wm("vector", m)
        K.wu_("vector", "w16s_st")
        K.v(lambda e: e.tensor_copy(ap("w16s")[:, 0:2048], ap("stg")[:, 0:2048]))
        K.u["stg_rd"] = S.mark("ve")
        m = S.mark("ve")
        K.wm("scalar", m)
        S.dma("scalar", wo16_d[kc * 128:(kc + 1) * 128, :], ap("w16s")[:, 0:2048], "qa")
        K.u["w16s_st"] = S.mark("qa")

    K.v(lambda e: e.memset(ap("wsum")[:], 0.0))
    K.wscale_ws0 = S.mark("ve")
    K.wscale_cast = S.mark("qa")
    K.wscale_list = []
    for widx, (srcw, rows) in enumerate(((D["wg_in"], E), (D["wu_in"], E),
                                         (D["wd_in"], IFF))):
        ncol = IFF if rows == E else E
        for r in range(rows // 128):
            for c in range(ncol // 512):
                K.wscale_list.append((widx, srcw, r * 128, c * 512))
    K.wscale_i = 0
    K.wscale_prev = [None, None, None, None]


def emit_wscale_chunks(K, n):
    S, ap = K.S, K.ap
    for _ in range(n):
        i = K.wscale_i
        if i >= len(K.wscale_list):
            return
        K.wscale_i += 1
        widx, srcw, r, c = K.wscale_list[i]
        bi = i % 4
        buf = f"wsb{bi}"
        if K.wscale_prev[bi] is not None:
            K.wm("gpsimd", K.wscale_prev[bi])
        S.dma("gpsimd", ap(buf)[:], srcw[r:r + 128, c:c + 512], "qg")
        m = S.mark("qg")
        K.wm("scalar", m)
        if i == 0:
            K.wm("scalar", K.wscale_ws0)
        K.wu_("scalar", "wsacc")
        K.a(lambda e, buf=buf: e.activation(ap("wsd")[:], ap(buf)[:],
                                            AF.Abs, accum_out=ap("cols")[:, 40:41]))
        K.wscale_prev[bi] = S.mark("ac")
        K.wm("vector", K.wscale_prev[bi])
        K.v(lambda e, widx=widx: e.tensor_tensor(
            ap("wsum")[:, widx:widx + 1], ap("wsum")[:, widx:widx + 1],
            ap("cols")[:, 40:41], OP.add))
        K.u["wsacc"] = S.mark("ve")


def emit_wscale_fin(K):
    S, ap = K.S, K.ap
    emit_wscale_chunks(K, len(K.wscale_list))
    m_wsum = S.mark("ve")
    K.wm("tensor", m_wsum)
    K.wu_("tensor", ("psS", 0))
    K.p(lambda e: e.matmul(K.psS[0].ap()[0:1, 0:4], ap("onesc")[:],
                           ap("wsum")[:, 0:4], start=True, stop=True), inc=True)
    m = S.mark("pe")
    K.wm("vector", m)
    K.v(lambda e: e.tensor_copy(ap("srow")[0:1, 0:4], K.psS[0].ap()[0:1, 0:4]))
    K.vself()
    K.v(lambda e: e.tensor_scalar(ap("srow")[0:1, 4:8], ap("srow")[0:1, 0:4],
                                  1.0 / (IFF * E), EPS, OP.mult, OP.max))
    K.vself()
    K.v(lambda e: e.tensor_scalar(ap("srow")[0:1, 8:12], ap("srow")[0:1, 4:8],
                                  0.5, None, OP.mult))
    K.vself()
    K.v(lambda e: e.tensor_scalar(ap("srow")[0:1, 12:16], ap("srow")[0:1, 8:12],
                                  -1.0, None, OP.mult))
    m_s = S.mark("ve")
    K.wm("tensor", m_s)
    K.p(lambda e: e.matmul(K.psS[0].ap()[:, 0:4], ap("ones_r32")[0:1, 0:128],
                           ap("srow")[0:1, 4:8], start=True, stop=True))
    K.p(lambda e: e.matmul(K.psS[0].ap()[:, 4:8], ap("ones_r32")[0:1, 0:128],
                           ap("srow")[0:1, 8:12], start=True, stop=True))
    K.p(lambda e: e.matmul(K.psS[0].ap()[:, 8:12], ap("ones_r32")[0:1, 0:128],
                           ap("srow")[0:1, 12:16], start=True, stop=True), inc=True)
    m = S.mark("pe")
    K.wm("vector", m)
    K.v(lambda e: e.tensor_copy(ap("swcol")[:, 0:12], K.psS[0].ap()[:, 0:12]))
    K.u[("psS", 0)] = S.mark("ve")


def emit_attention(K, x_in, wqkv16_d, wo16_d, x2_d, nt):
    S, ap = K.S, K.ap
    psA, psS = K.psA, K.psS
    ngroups = (nt + 4) // 4
    xa = ["xa0", "xa1"]
    xb = ["xb0", "xb1"]
    x2s = ["x2s0", "x2s1"]
    wbuf = ["wbuf0", "wbuf1"]

    def emit_ln_group(g):

        for t4 in range(4):
            tt = g * 4 + t4
            bi = t4 % 2
            K.wu_("sync", ("xa", bi))
            S.dma("sync", ap(xa[bi])[:], x_in[tt * 128:(tt + 1) * 128, :], "qs")
            mx = S.mark("qs")
            K.wm("vector", mx)
            K.wu_("vector", "cols_act")
            K.v(lambda e, bi=bi: e.reduce_sum(ap("cols")[:, 0:1], ap(xa[bi])[:], AX))
            m_sum = S.mark("ve")
            K.wm("scalar", mx)
            K.wu_("scalar", "ln1b")
            K.a(lambda e, bi=bi: e.activation(ap("ln1b")[:], ap(xa[bi])[:], AF.Square,
                                              accum_out=ap("cols")[:, 1:2]))
            K.u["disc_sq"] = S.mark("ac")
            K.wm("scalar", m_sum)
            K.wu_("scalar", "cols_free")
            K.aself()
            K.a(lambda e: e.activation(ap("cols")[:, 2:3], ap("cols")[:, 0:1],
                                       AF.Identity, scale=1.0 / 2048))
            K.wu_("vector", "disc_sq")
            K.v(lambda e: e.tensor_scalar(ap("cols")[:, 3:4], ap("cols")[:, 1:2],
                                          1.0 / 2048, EPS, OP.mult, OP.add))
            m_q = S.mark("ve")
            K.wm("scalar", m_q)
            K.aself()
            K.a(lambda e: e.activation(ap("cols")[:, 4:5], ap("cols")[:, 2:3],
                                       AF.Square))
            K.aself()
            K.a(lambda e: e.activation(ap("cols")[:, 5:6], ap("cols")[:, 4:5],
                                       AF.Identity, scale=-1.0, bias=ap("cols")[:, 3:4]))
            K.aself()
            K.a(lambda e: e.activation(ap("cols")[:, 6:7], ap("cols")[:, 5:6], AF.Ln))
            K.aself()
            K.a(lambda e: e.activation(ap("cols")[:, 6:7], ap("cols")[:, 6:7],
                                       AF.Exp, scale=-0.5))
            K.aself()
            K.a(lambda e: e.activation(ap("cols")[:, 7:8], ap("cols")[:, 2:3],
                                       AF.Identity, scale=ap("cols")[:, 6:7]))
            m_act = S.mark("ac")
            K.u["cols_act"] = m_act
            K.wm("vector", m_act)
            K.wu_("vector", "ln1b")
            K.v(lambda e, bi=bi: e.tensor_scalar(ap("ln1b")[:], ap(xa[bi])[:],
                                                 ap("cols")[:, 6:7], ap("cols")[:, 7:8],
                                                 OP.mult, OP.subtract))
            K.u[("xa", bi)] = S.mark("ve")
            K.u["cols_free"] = S.mark("ve")
            m_ln = S.mark("ve")
            K.wu_("sync", "ln1t_r")
            m_t = emit_dmat(K, blk3(ap("ln1b")[:], 16, 128),
                            blk3(ap("ln1t")[:], 16, 512, off=t4 * 128), m_ln)
            K.u["ln1b"] = m_t
        m_ln1t = S.mark("qs")


        return m_ln1t

    m_ln1t = emit_ln_group(0)
    for g in range(ngroups):
        own = g >= 1
        og_list = [list(range(0, 4)), list(range(4, 8)), list(range(8, 12)),
                   list(range(12, 16)), list(range(16, 20))]
        if not own:
            og_list = [list(range(16, 20))]
        for ogi, og in enumerate(og_list):
            wb = wbuf[ogi % 2]
            c0 = og[0] * 128
            ncol = len(og) * 128
            K.wu_("sync", ("wbuf", ogi % 2))
            S.dma("sync", ap(wb)[:, 0:16 * ncol].rearrange("p (k c) -> p k c", c=ncol),
                  wqkv16_d[:, c0:c0 + ncol].rearrange("(k p) c -> p k c", p=128), "qs")
            m_w = S.mark("qs")
            K.wm("tensor", m_w)
            K.wm("tensor", m_ln1t)
            for oi, o in enumerate(og):
                if not own and o < 16:
                    continue
                pai = oi % 3
                K.wu_("tensor", ("psA", pai))
                for k in range(16):
                    K.p(lambda e, k=k, o=o, wb=wb, pai=pai, ncol=ncol, c0=c0: e.matmul(
                        psA[pai].ap()[:],
                        ap(wb)[:, k * ncol + o * 128 - c0: k * ncol + (o + 1) * 128 - c0],
                        ap("ln1t")[:, k * 512:(k + 1) * 512],
                        start=(k == 0), stop=False))
                K.p(lambda e, o=o, pai=pai: e.matmul(
                    psA[pai].ap()[:], ap("biasrow")[0:1, o * 128:(o + 1) * 128],
                    ap("biasrow")[0:1, 5120:5632], start=False, stop=True), inc=True)
                m_mm = S.mark("pe")
                emit_rope_evac(K, pai, o, g, m_mm)
                K.u[("psA", pai)] = S.mark("ve")
            K.u[("wbuf", ogi % 2)] = S.mark("pe")
        for kh in range(2):
            K.wu_("sync", "wvbuf")
            S.dma("sync", ap("wvbuf")[:].rearrange("p (k c) -> p k c", c=512),
                  wqkv16_d[kh * 1024:(kh + 1) * 1024, 2560:3072].rearrange(
                      "(k p) c -> p k c", p=128), "qs")
            m_wv = S.mark("qs")
            K.wm("tensor", m_wv)
            for t4 in range(4):
                if kh == 0:
                    K.wu_("tensor", ("psAv", t4))
                    K.wu_("tensor", ("psA", t4))
                for kk in range(8):
                    k = kh * 8 + kk
                    K.p(lambda e, k=k, kk=kk, t4=t4: e.matmul(
                        psA[t4].ap()[:],
                        ap("ln1t")[:, k * 512 + t4 * 128: k * 512 + (t4 + 1) * 128],
                        ap("wvbuf")[:, kk * 512:(kk + 1) * 512],
                        start=(k == 0), stop=False),
                        inc=(kh == 0 and t4 == 3 and kk == 7))
                if kh == 1:
                    K.p(lambda e, t4=t4: e.matmul(
                        psA[t4].ap()[:], ap("biasrow")[0:1, 5120:5248],
                        ap("biasrow")[0:1, 2560:3072], start=False, stop=True),
                        inc=(t4 == 3))
            K.u["wvbuf"] = S.mark("pe")
        m_vmm = S.mark("pe")
        for t4 in range(4):
            tt = g * 4 + t4
            K.wm("vector", m_vmm)
            K.v(lambda e, tt=tt, t4=t4: e.tensor_copy(
                ap("vsb")[:, tt * 512:(tt + 1) * 512], psA[t4].ap()[:]))
            K.u[("psAv", t4)] = S.mark("ve")
            K.u[("psA", t4)] = S.mark("ve")
        K.u["ln1t_r"] = S.mark("pe")

        if g + 1 < ngroups:
            m_ln1t_next = emit_ln_group(g + 1)
        else:
            m_ln1t_next = m_ln1t
        if own:
            m_a1 = S.mark("ve")
            for t4 in range(4):
                qt = g * 4 + t4 - 4
                emit_attn_tile(K, qt, t4, m_a1)
            m_avb = S.mark("ve")
            for ob in range(4):
                for kh in range(2):
                    K.wu_("sync", "wobuf")
                    S.dma("sync", ap("wobuf")[:].rearrange("p (k c) -> p k c", c=512),
                          wo16_d[kh * 1024:(kh + 1) * 1024,
                                 ob * 512:(ob + 1) * 512].rearrange(
                                     "(k p) c -> p k c", p=128), "qs")
                    m_wo = S.mark("qs")
                    for t4 in range(4):
                        K.wm("tensor", m_wo)
                        K.wm("tensor", m_avb)
                        if kh == 0:
                            K.wu_("tensor", ("psA", t4))
                        for hh in range(8):
                            h = kh * 8 + hh
                            K.p(lambda e, h=h, hh=hh, t4=t4, kh=kh: e.matmul(
                                psA[t4].ap()[:],
                                ap("avb")[:, t4 * 2048 + h * 128: t4 * 2048 + (h + 1) * 128],
                                ap("wobuf")[:, hh * 512:(hh + 1) * 512],
                                start=(h == 0), stop=False),
                                inc=(kh == 0 and t4 == 3 and hh == 7))
                        if kh == 1:
                            K.p(lambda e, ob=ob, t4=t4: e.matmul(
                                psA[t4].ap()[:], ap("biasrow")[0:1, 5120:5248],
                                ap("biasrow")[0:1, 3072 + ob * 512:3072 + (ob + 1) * 512],
                                start=False, stop=True), inc=(t4 == 3))
                    K.u["wobuf"] = S.mark("pe")
                m_mmall = S.mark("pe")
                for t4 in range(4):
                    qt = g * 4 + t4 - 4
                    bi = t4 % 2
                    K.wu_("scalar", ("xb", bi))
                    S.dma("scalar", ap(xb[bi])[:],
                          x_in[(qt + 4) * 128:(qt + 5) * 128,
                               ob * 512:(ob + 1) * 512], "qa")
                    m_xb = S.mark("qa")
                    K.wm("vector", m_mmall)
                    K.wm("vector", m_xb)
                    K.wu_("vector", ("x2st", t4 % 2))
                    K.v(lambda e, t4=t4, bi=bi: e.tensor_tensor(
                        ap(x2s[t4 % 2])[:], psA[t4].ap()[:],
                        ap(xb[bi])[:], OP.add))
                    K.u[("psA", t4)] = S.mark("ve")
                    K.u[("xb", bi)] = S.mark("ve")
                    m_x2 = S.mark("ve")
                    K.wm("scalar", m_x2)
                    S.dma("scalar", x2_d[qt * 128:(qt + 1) * 128, ob * 512:(ob + 1) * 512],
                          ap(x2s[t4 % 2])[:], "qa")
                    K.u[("x2st", t4 % 2)] = S.mark("qa")
            K.u["avb"] = S.mark("pe")


        m_ln1t = m_ln1t_next

def emit_rope_evac(K, pai, o, g, m_mm):
    S, ap = K.S, K.ap
    tok0 = g * 512
    K.wm("vector", m_mm)
    K.wu_("vector", "disc_rope")
    K.v(lambda e: e.tensor_copy(ap("disc")[:, 0:512], K.psA[pai].ap()[:]))
    m_qb = S.mark("ve")
    K.wm("tensor", m_qb)
    K.wu_("tensor", ("psS", 0))
    K.p(lambda e: e.matmul(K.psS[0].ap()[:, 0:512], ap("swp16")[:],
                           ap("disc")[:, 0:512], start=True, stop=True), inc=True)
    m_sw = S.mark("pe")
    K.wm("vector", m_sw)
    K.v(lambda e: e.tensor_copy(ap("disc")[:, 512:1024], K.psS[0].ap()[:, 0:512]))
    K.u[("psS", 0)] = S.mark("ve")
    K.vself()
    K.v(lambda e, tok0=tok0: e.tensor_tensor(
        ap("disc")[:, 1024:1536], ap("disc")[:, 0:512],
        ap("cs1")[:, tok0:tok0 + 512], OP.mult))
    K.v(lambda e, tok0=tok0: e.tensor_tensor(
        ap("disc")[:, 512:1024], ap("disc")[:, 512:1024],
        ap("cs2")[:, tok0:tok0 + 512], OP.mult))
    K.vself()
    if o < 16:
        off = o * 512
        K.wu_("vector", "qrot_free")
        K.v(lambda e, off=off: e.tensor_tensor(
            ap("qrot")[:, off:off + 512], ap("disc")[:, 1024:1536],
            ap("disc")[:, 512:1024], OP.add))
    else:
        off = (o - 16) * TOKH + tok0
        K.v(lambda e, off=off: e.tensor_tensor(
            ap("krot")[:, off:off + 512], ap("disc")[:, 1024:1536],
            ap("disc")[:, 512:1024], OP.add))
    K.u["disc_rope"] = S.mark("ve")


def emit_attn_tile(K, qt, t4, m_a1):
    S, ap = K.S, K.ap
    psA, psS = K.psA, K.psS
    K.wm("tensor", m_a1)

    def scores(h):
        hv = h // 4
        hp = h % 2
        ps = psS[hp]
        K.wu_("tensor", ("psS", hp))
        for kb in range(5):
            K.p(lambda e, h=h, hv=hv, kb=kb, ps=ps: e.matmul(
                ps.ap()[:, kb * 128:(kb + 1) * 128],
                ap("krot")[:, hv * TOKH + (qt + kb) * 128: hv * TOKH + (qt + kb + 1) * 128],
                ap("qrot")[:, h * 512 + t4 * 128: h * 512 + (t4 + 1) * 128],
                start=True, stop=True), inc=(kb == 4))
        return S.mark("pe")

    def softmax(h, m_s):
        hp = h % 2
        ps = psS[hp]
        K.wm("vector", m_s)
        K.v(lambda e, ps=ps: e.tensor_tensor(ps.ap()[:, 0:128], ps.ap()[:, 0:128],
                                             ap("trim")[:, 256:384], OP.add))
        K.v(lambda e, ps=ps: e.tensor_tensor(ps.ap()[:, 512:640],
                                             ps.ap()[:, 512:640],
                                             ap("trim")[:, 384:512], OP.add))
        for kb in range(0, max(0, 4 - qt)):
            K.v(lambda e, kb=kb, ps=ps: e.tensor_scalar(
                ps.ap()[:, kb * 128:(kb + 1) * 128],
                ps.ap()[:, kb * 128:(kb + 1) * 128],
                ap("blkcol")[:, qt + kb: qt + kb + 1], None, OP.add))
        m_msk = S.mark("ve")
        K.wm("scalar", m_msk)
        K.wu_("scalar", ("pt16", hp))
        K.a(lambda e, ps=ps, hp=hp: e.activation(
            ap("pt16")[:, hp * 640:hp * 640 + 640], ps.ap()[:], AF.Exp,
            scale=ISR, bias=ap("m12c")[:, 0:1]))
        m_exp = S.mark("ac")
        K.u[("psS", hp)] = m_exp
        return m_exp

    def av(h, m_exp):
        hp = h % 2
        hv = h // 4
        pa = psA[hp]
        K.wm("tensor", m_exp)
        K.wu_("tensor", ("psA", hp))
        for kb in range(5):
            K.p(lambda e, kb=kb, hv=hv, hp=hp, pa=pa: e.matmul(
                pa.ap()[:, 0:128],
                ap("vsb")[:, (qt + kb) * 512 + hv * 128: (qt + kb) * 512 + (hv + 1) * 128],
                ap("pt16")[:, hp * 640 + kb * 128: hp * 640 + (kb + 1) * 128],
                start=(kb == 0), stop=(kb == 4)))
        for kb in range(5):
            K.p(lambda e, kb=kb, hp=hp, pa=pa: e.matmul(
                pa.ap()[0:1, 128:256], ap("onesc16")[:],
                ap("pt16")[:, hp * 640 + kb * 128: hp * 640 + (kb + 1) * 128],
                start=(kb == 0), stop=(kb == 4)), inc=(kb == 4))
        m_av = S.mark("pe")
        K.u[("pt16", hp)] = m_av
        K.wm("vector", m_av)
        K.wu_("vector", "avb")
        K.v(lambda e, h=h, pa=pa: e.tensor_copy(
            ap("avb")[:, t4 * 2048 + h * 128: t4 * 2048 + (h + 1) * 128],
            pa.ap()[:, 0:128]))
        K.v(lambda e, h=h, pa=pa: e.tensor_copy(
            ap("rbsb")[0:1, h * 128:(h + 1) * 128], pa.ap()[0:1, 128:256]))
        K.u[("psA", hp)] = S.mark("ve")

    pend = None
    for h in range(16):
        m_s = scores(h)
        if pend is not None:
            av(*pend)
        m_exp = softmax(h, m_s)
        pend = (h, m_exp)
        emit_wscale_chunks(K, 3)
    av(*pend)
    K.vself()
    def _recip_bf16(e):
        with K.nc.allow_low_precision(reason="softmax 1/L at bf16"):
            return e.reciprocal(ap("rbsb")[0:1, 0:2048], ap("rbsb")[0:1, 0:2048])
    K.v(_recip_bf16)
    m_rl = S.mark("ve")
    K.wm("tensor", m_rl)
    for c4 in range(4):
        K.wu_("tensor", ("psAr", c4 % 2))
        K.p(lambda e, c4=c4: e.matmul(
            psA[2 + c4 % 2].ap()[:], ap("biasrow")[0:1, 5120:5248],
            ap("rbsb")[0:1, c4 * 512:(c4 + 1) * 512], start=True, stop=True),
            inc=True)
        m_rb = S.mark("pe")
        K.wm("vector", m_rb)
        K.wu_("vector", ("rbsb", c4 % 2))
        K.v(lambda e, c4=c4: e.tensor_copy(ap("rbsb")[:, c4 * 512:(c4 + 1) * 512],
                                           psA[2 + c4 % 2].ap()[:]))
        K.u[("psAr", c4 % 2)] = S.mark("ve")
    K.vself()
    K.v(lambda e: e.tensor_tensor(ap("avb")[:, t4 * 2048:(t4 + 1) * 2048],
                                  ap("avb")[:, t4 * 2048:(t4 + 1) * 2048],
                                  ap("rbsb")[:, 0:2048], OP.mult))
    K.u["avb"] = S.mark("ve")
    K.u[("rbsb", 0)] = S.mark("ve")
    K.u[("rbsb", 1)] = S.mark("ve")
    K.u["qrot_free"] = S.mark("pe")


def emit_ffn(K, es_root, x2_d, h_d, n2t_d, wg_in, wu_in, wd_in, out, nt,
             subs="0123"):
    import contextlib
    S, ap = K.S, K.ap
    psA, psS = K.psA, K.psS

    K.sb(es_root, "hcat16", [128, 2048], BF16)
    es01 = contextlib.ExitStack()
    with (contextlib.nullcontext(es_root) if K.dump else es01) as esn:
        K.sb(esn, "n1T", [128, 16 * NT * 128], BF16)
        es0 = contextlib.ExitStack()
        with (contextlib.nullcontext(es_root) if (set(K.dump) & {"x2a0","lnb","lnT","lnTp","n1b","rt","dsc0"}) else es0) as es:
          if True:
            K.sb(es, "x2a0", [128, 2048], F32)
            K.sb(es, "x2a1", [128, 2048], F32)
            K.sb(es, "lnb", [128, 2048], BF16)
            K.sb(es, "lnT", [128, 2048], BF16)
            K.sb(es, "lnTp", [128, 2048], BF16)
            K.sb(es, "n1b", [128, 2048], BF16)
            K.sb(es, "rt", [128, 512], F32)
            K.sb(es, "dsc0", [128, 2048], BF16)
            S.dma("sync", ap("x2a0")[:], K.dram["hcat"].ap()[:], "qs")
            K.wm("vector", S.mark("qs"))
            K.v(lambda e: e.tensor_copy(ap("hcat16")[:], ap("x2a0")[:]))
            K.u[("x2a", 0)] = S.mark("ve")
            if "0" in subs:
                for t in range(nt):
                    emit_b0_tile(K, x2_d, t)
        K.barrier()

        es1 = contextlib.ExitStack()
        with (contextlib.nullcontext(es_root) if (set(K.dump) & {"wstg0","wstg1","tq","trn","wgT0","wgT1","wuT0","wuT1","sg","uu","h160","h161"}) else es1) as es:
          if "1" in subs:
            K.sb(es, "wstg0", [128, 2048], F32)
            K.sb(es, "wstg1", [128, 2048], F32)
            K.sb(es, "tq0", [128, 2048], BF16)
            K.sb(es, "tq1", [128, 2048], BF16)
            K.sb(es, "wgT0", [128, 16 * 512], BF16)
            K.sb(es, "wgT1", [128, 16 * 512], BF16)
            K.sb(es, "wuT0", [128, 16 * 512], BF16)
            K.sb(es, "wuT1", [128, 16 * 512], BF16)
            K.sb(es, "sg", [128, 512], F32)
            K.sb(es, "uu", [128, 512], F32)
            K.sb(es, "h160", [128, 512], BF16)
            K.sb(es, "h161", [128, 512], BF16)
            if "1" in subs:
                for j in range(16):
                    emit_b1_j(K, wg_in, wu_in, h_d, j, nt)
        K.barrier()
    K.barrier()

    es2 = contextlib.ExitStack()
    with (contextlib.nullcontext(es_root) if (set(K.dump) & {"hl0","hl1","hT","hTp","z32","n2b","n2st"}) else es2) as es:
      if "2" in subs:
        K.sb(es, "hl0", [128, IFF], BF16)
        K.sb(es, "hl1", [128, IFF], BF16)
        K.sb(es, "hT0", [128, IFF], BF16)
        K.sb(es, "hTp0", [128, IFF], BF16)
        K.sb(es, "hT1", [128, IFF], BF16)
        K.sb(es, "hTp1", [128, IFF], BF16)
        K.sb(es, "z32", [128, IFF], F32)
        K.sb(es, "n2b", [128, IFF], BF16)
        K.sb(es, "n2st", [128, IFF], BF16)
        if "2" in subs:
            for t in range(nt):
                emit_b2a_tile(K, h_d, n2t_d, t)
    K.barrier()

    es3 = contextlib.ExitStack()
    with (contextlib.nullcontext(es_root) if (set(K.dump) & {"wdstg0","wdstg1","tq2","trn2","wdtj","n2buf0","n2buf1","x2b0","x2b1","outst0","outst1"}) else es3) as es:
      if "3" in subs:
        K.sb(es, "wdstg0", [128, 2048], F32)
        K.sb(es, "wdstg1", [128, 2048], F32)
        K.sb(es, "tq20", [128, 2048], BF16)
        K.sb(es, "tq21", [128, 2048], BF16)
        K.sb(es, "wdtj", [128, 64 * 512], BF16)
        K.sb(es, "n2buf0", [128, IFF], BF16)
        K.sb(es, "n2buf1", [128, IFF], BF16)
        K.sb(es, "x2b0", [128, 512], F32)
        K.sb(es, "x2b1", [128, 512], F32)
        K.sb(es, "outst0", [128, 512], F32)
        K.sb(es, "outst1", [128, 512], F32)
        if "3" in subs:
            for j in range(4):
                emit_b2b_j(K, wd_in, n2t_d, x2_d, out, j, nt)


def emit_ln_cols(K, xtile, dsc_name):
    S, ap = K.S, K.ap
    K.wu_("vector", "cols_act")
    K.v(lambda e: e.reduce_sum(ap("cols")[:, 0:1], xtile, AX))
    m_sum = S.mark("ve")
    K.wu_("scalar", "cols_free")
    K.a(lambda e: e.activation(ap(dsc_name)[:], xtile, AF.Square,
                               accum_out=ap("cols")[:, 1:2]))
    m_sq = S.mark("ac")
    K.wm("scalar", m_sum)
    K.aself()
    K.a(lambda e: e.activation(ap("cols")[:, 2:3], ap("cols")[:, 0:1],
                               AF.Identity, scale=1.0 / 2048))
    K.wm("vector", m_sq)
    K.v(lambda e: e.tensor_scalar(ap("cols")[:, 3:4], ap("cols")[:, 1:2],
                                  1.0 / 2048, EPS, OP.mult, OP.add))
    m_q = S.mark("ve")
    K.aself()
    K.a(lambda e: e.activation(ap("cols")[:, 4:5], ap("cols")[:, 2:3], AF.Square))
    K.wm("scalar", m_q)
    K.aself()
    K.a(lambda e: e.activation(ap("cols")[:, 5:6], ap("cols")[:, 4:5],
                               AF.Identity, scale=-1.0, bias=ap("cols")[:, 3:4]))
    K.aself()
    K.a(lambda e: e.activation(ap("cols")[:, 6:7], ap("cols")[:, 5:6], AF.Ln))
    K.aself()
    K.a(lambda e: e.activation(ap("cols")[:, 6:7], ap("cols")[:, 6:7],
                               AF.Exp, scale=-0.5))
    K.aself()
    K.a(lambda e: e.activation(ap("cols")[:, 7:8], ap("cols")[:, 2:3],
                               AF.Identity, scale=ap("cols")[:, 6:7]))
    K.u["cols_act"] = S.mark("ac")
    return S.mark("ac")


def emit_dmat(K, src_ap3, dst_ap3, src_mark, eng="sync"):
    S = K.S
    q = {"sync": "qs", "scalar": "qa"}[eng]
    K.wm(eng, src_mark)
    S.do(eng, lambda e, o=dst_ap3, i=src_ap3: e.dma_start_transpose(o, i),
         q, 16)
    return S.mark(q)


def blk3(ap2, nblk, blk_stride, off=0):
    if blk_stride == 128:
        return ap2[:, off:off + nblk * 128].rearrange("p (k q) -> p k q", q=128)
    k0, off_in = divmod(off, blk_stride)
    assert off_in + 128 <= blk_stride
    v = ap2.rearrange("p (k s) -> p k s", s=blk_stride)
    return v[:, k0:k0 + nblk, off_in:off_in + 128]


def emit_b0_tile(K, x2_d, t):
    S, ap = K.S, K.ap
    psA = K.psA
    bi = t % 2
    xn = "x2a0" if bi == 0 else "x2a1"
    K.wu_("sync", ("x2a", bi))
    S.dma("sync", ap(xn)[:], x2_d[t * 128:(t + 1) * 128, :], "qs")
    mx = S.mark("qs")
    K.wm("vector", mx)
    K.wm("scalar", mx)
    m_act = emit_ln_cols(K, ap(xn)[:], "dsc0")
    K.wm("vector", m_act)
    K.wu_("vector", "lnb_r")
    K.v(lambda e, xn=xn: e.tensor_scalar(ap("lnb")[:], ap(xn)[:],
                                         ap("cols")[:, 6:7], ap("cols")[:, 7:8],
                                         OP.mult, OP.subtract))
    K.u[("x2a", bi)] = S.mark("ve")
    K.u["cols_free"] = S.mark("ve")
    m_lnb = S.mark("ve")
    K.wu_("sync", "lnT_r")
    m_t = emit_dmat(K, blk3(ap("lnb")[:], 16, 128), blk3(ap("lnT")[:], 16, 128),
                    m_lnb)
    K.u["lnb_r"] = m_t
    K.wm("vector", m_t)
    K.vself()
    for st_i, stride in ((0, 512), (1, 1024)):
        a, b = ("lnT", "lnTp") if st_i == 0 else ("lnTp", "lnT")
        nrun = 2048 // (2 * stride)
        for r in range(nrun):
            off = r * 2 * stride
            K.v(lambda e, a=a, b=b, off=off, stride=stride: e.tensor_tensor(
                ap(b)[:, off:off + stride], ap(a)[:, off:off + stride],
                ap(a)[:, off + stride:off + 2 * stride], OP.add))
            K.v(lambda e, a=a, b=b, off=off, stride=stride: e.tensor_tensor(
                ap(b)[:, off + stride:off + 2 * stride], ap(a)[:, off:off + stride],
                ap(a)[:, off + stride:off + 2 * stride], OP.subtract))
        K.vself()
    m_stg = S.mark("ve")
    K.wm("tensor", m_stg)
    for gg in range(4):
        K.wu_("tensor", ("psA", gg))
        for lo in range(4):
            c = 4 * gg + lo
            K.p(lambda e, c=c, lo=lo, gg=gg: e.matmul(
                psA[gg].ap()[:], ap("lnT")[:, c * 128:(c + 1) * 128],
                ap("hcat16")[:, lo * 512:(lo + 1) * 512],
                start=(lo == 0), stop=(lo == 3)), inc=(lo == 3))
    m_y1 = S.mark("pe")
    K.u["lnT_r"] = m_y1
    K.wm("vector", m_y1)
    for gg in range(4):
        K.v(lambda e, gg=gg: e.tensor_reduce(ap("cols")[:, 16 + gg:17 + gg],
                                             psA[gg].ap()[:], AX, OP.max,
                                             apply_absolute_value=True))
    K.vself()
    K.v(lambda e: e.tensor_tensor(ap("cols")[:, 20:21], ap("cols")[:, 16:17],
                                  ap("cols")[:, 17:18], OP.max))
    K.v(lambda e: e.tensor_tensor(ap("cols")[:, 21:22], ap("cols")[:, 18:19],
                                  ap("cols")[:, 19:20], OP.max))
    K.vself()
    K.v(lambda e: e.tensor_tensor(ap("cols")[:, 20:21], ap("cols")[:, 20:21],
                                  ap("cols")[:, 21:22], OP.max))
    K.vself()
    K.v(lambda e: e.tensor_scalar(ap("cols")[:, 21:22], ap("cols")[:, 20:21],
                                  EPS * math.sqrt(2048.0), None, OP.max))
    K.vself()
    K.v(lambda e: e.reciprocal(ap("cols")[:, 22:23], ap("cols")[:, 21:22]))
    K.vself()
    K.v(lambda e: e.tensor_scalar(ap("cols")[:, 23:24], ap("cols")[:, 22:23],
                                  127.0, None, OP.mult))
    K.vself()
    K.v(lambda e, t=t: e.tensor_scalar(ap("cols")[:, 48 + t:49 + t],
                                       ap("cols")[:, 21:22],
                                       ap("swcol")[:, 0:1],
                                       1.0 / (127 * math.sqrt(2048.0)),
                                       OP.mult, OP.mult))
    K.v(lambda e, t=t: e.tensor_scalar(ap("cols")[:, 64 + t:65 + t],
                                       ap("cols")[:, 21:22],
                                       ap("swcol")[:, 1:2],
                                       1.0 / (127 * math.sqrt(2048.0)),
                                       OP.mult, OP.mult))
    K.vself()
    K.wm("scalar", S.mark("ve"))
    K.wu_("scalar", ("n1b_r", 0))
    for gg in range(4):
        K.a(lambda e, gg=gg: e.activation(ap("rt")[:], psA[gg].ap()[:],
                                          AF.Identity, scale=ap("cols")[:, 23:24],
                                          bias=ap("acol")[:, 0:1]))
        K.aself()
        K.a(lambda e, gg=gg: e.activation(
            ap("n1b")[:, gg * 512:(gg + 1) * 512], ap("rt")[:], AF.Identity,
            bias=ap("nacol")[:, 0:1]))
        K.u[("psA", gg)] = S.mark("ac")
        K.aself()
    K.wm("vector", S.mark("ac"))
    m_n1b = S.mark("ac")
    n1t4 = K.ap("n1T").rearrange("p (k t q) -> p k t q", t=NT, q=128)[:, :, t, :]
    K.wu_("sync", "n1T_w")
    m_t2 = emit_dmat(K, blk3(ap("n1b")[:], 16, 128), n1t4, m_n1b)
    K.u[("n1b_r", 0)] = m_t2
    K.u["n1T_ld"] = m_t2


def emit_b1_j(K, wg_in, wu_in, h_d, j, nt):
    S, ap = K.S, K.ap
    psA = K.psA
    par = j % 2
    gname = "wgT0" if par == 0 else "wgT1"
    uname = "wuT0" if par == 0 else "wuT1"
    for dst, src, widx, stg, tq, qeng, qn in (
            (gname, wg_in, 0, "wstg0", "tq0", "sync", "qs"),
            (uname, wu_in, 1, "wstg1", "tq1", "sync", "qs")):
        for g in range(4):
            K.wu_(qeng, (stg, 0))
            for i in range(4):
                k = 4 * g + i
                S.dma(qeng, ap(stg)[:, i * 512:(i + 1) * 512],
                      src[k * 128:(k + 1) * 128, j * 512:(j + 1) * 512], qn)
            mw = S.mark(qn)
            K.wm("vector", mw)
            if g == 0:
                K.wu_("vector", ("wT_r", par))
            K.v(lambda e, stg=stg, tq=tq, widx=widx: e.tensor_scalar(
                ap(tq)[:], ap(stg)[:], ap("swcol")[:, 8 + widx:9 + widx],
                None, OP.is_lt))
            K.vself()
            K.v(lambda e, stg=stg, tq=tq, dst=dst, g=g, widx=widx:
                e.scalar_tensor_tensor(
                    ap(dst)[:, g * 2048:(g + 1) * 2048], ap(stg)[:],
                    ap("swcol")[:, 4 + widx:5 + widx],
                    ap(tq)[:], OP.is_gt, OP.subtract))
            K.u[(stg, 0)] = S.mark("ve")
    m_w = S.mark("ve")
    K.wm("tensor", m_w)
    K.wu_("tensor", "n1T_ld")
    for t in range(nt):
        pg = 2 * (t % 2)
        pu = pg + 1
        K.wu_("tensor", ("psA", pg))
        K.wu_("tensor", ("psA", pu))
        for k in range(16):
            K.p(lambda e, k=k, t=t, pg=pg, gname=gname: e.matmul(
                psA[pg].ap()[:],
                ap("n1T")[:, (k * NT + t) * 128:(k * NT + t + 1) * 128],
                ap(gname)[:, k * 512:(k + 1) * 512],
                start=(k == 0), stop=(k == 15)), inc=(k == 15))
            K.p(lambda e, k=k, t=t, pu=pu, uname=uname: e.matmul(
                psA[pu].ap()[:],
                ap("n1T")[:, (k * NT + t) * 128:(k * NT + t + 1) * 128],
                ap(uname)[:, k * 512:(k + 1) * 512],
                start=(k == 0), stop=(k == 15)), inc=(k == 15))
        m_mm = S.mark("pe")
        K.wm("scalar", m_mm)
        K.wu_("scalar", ("sgq", 0))
        K.a(lambda e, t=t, pg=pg: e.activation(ap("sg")[:], psA[pg].ap()[:],
                                               AF.Silu,
                                               scale=ap("cols")[:, 48 + t:49 + t]))
        m_sg = S.mark("ac")
        K.wm("vector", m_mm)
        K.wu_("vector", ("h16", t % 2))
        K.v(lambda e, t=t, pu=pu: e.tensor_scalar(ap("uu")[:], psA[pu].ap()[:],
                                                  ap("cols")[:, 64 + t:65 + t],
                                                  None, OP.mult))
        K.wm("vector", m_sg)
        hn = "h160" if t % 2 == 0 else "h161"
        K.v(lambda e, hn=hn: e.tensor_tensor(ap(hn)[:], ap("sg")[:], ap("uu")[:],
                                             OP.mult))
        K.u[("psA", pg)] = S.mark("ve")
        K.u[("psA", pu)] = S.mark("ve")
        K.u[("sgq", 0)] = S.mark("ve")
        if t == nt - 1:
            K.u[("wT_r", par)] = S.mark("pe")
        m_h = S.mark("ve")
        K.wm("gpsimd", m_h)
        S.dma("gpsimd", h_d[t, :, j * 512:(j + 1) * 512], ap(hn)[:], "qg")
        K.u[("h16", t % 2)] = S.mark("qg")


def emit_b2a_tile(K, h_d, n2t_d, t):
    S, ap = K.S, K.ap
    psA = K.psA
    bi = t % 2
    hn = "hl0" if bi == 0 else "hl1"
    hT = "hT0" if bi == 0 else "hT1"
    hTp = "hTp0" if bi == 0 else "hTp1"
    K.wu_("sync", ("hl", bi))
    S.dma("sync", ap(hn)[:], h_d[t], "qs")
    mh = S.mark("qs")
    K.wu_("sync", ("hT_r", bi))
    m_t = emit_dmat(K, blk3(ap(hn)[:], 64, 128), blk3(ap(hT)[:], 64, 128), mh)
    K.u[("hl", bi)] = m_t
    K.wm("vector", m_t)
    K.wu_("vector", ("hTp_r", bi))
    for st_i, stride in enumerate((512, 1024, 2048, 4096)):
        a, b = (hT, hTp) if st_i % 2 == 0 else (hTp, hT)
        nrun = IFF // (2 * stride)
        for r in range(nrun):
            off = r * 2 * stride
            K.v(lambda e, a=a, b=b, off=off, stride=stride: e.tensor_tensor(
                ap(b)[:, off:off + stride], ap(a)[:, off:off + stride],
                ap(a)[:, off + stride:off + 2 * stride], OP.add))
            K.v(lambda e, a=a, b=b, off=off, stride=stride: e.tensor_tensor(
                ap(b)[:, off + stride:off + 2 * stride], ap(a)[:, off:off + stride],
                ap(a)[:, off + stride:off + 2 * stride], OP.subtract))
        K.vself()
    m_stg = S.mark("ve")
    K.u[("hTp_r", bi)] = m_stg
    K.wm("tensor", m_stg)
    for gg in range(16):
        pai = gg % 4
        K.wu_("tensor", ("psA", pai))
        for lo in range(4):
            c = 4 * gg + lo
            K.p(lambda e, c=c, lo=lo, pai=pai, hT=hT: e.matmul(
                psA[pai].ap()[:], ap(hT)[:, c * 128:(c + 1) * 128],
                ap("hcat16")[:, lo * 512:(lo + 1) * 512],
                start=(lo == 0), stop=(lo == 3)), inc=(lo == 3))
        m_mm = S.mark("pe")
        K.wm("scalar", m_mm)
        K.a(lambda e, gg=gg, pai=pai: e.activation(
            ap("z32")[:, gg * 512:(gg + 1) * 512], psA[pai].ap()[:], AF.Identity))
        K.u[("psA", pai)] = S.mark("ac")
    m_z = S.mark("ac")
    K.u[("hT_r", bi)] = S.mark("pe")
    K.wm("vector", m_z)
    K.v(lambda e: e.tensor_reduce(ap("cols")[:, 20:21], ap("z32")[:], AX, OP.max,
                                  apply_absolute_value=True))
    K.vself()
    K.v(lambda e: e.tensor_scalar(ap("cols")[:, 21:22], ap("cols")[:, 20:21],
                                  EPS * math.sqrt(8192.0), None, OP.max))
    K.vself()
    K.v(lambda e: e.reciprocal(ap("cols")[:, 22:23], ap("cols")[:, 21:22]))
    K.vself()
    K.v(lambda e: e.tensor_scalar(ap("cols")[:, 23:24], ap("cols")[:, 22:23],
                                  127.0, None, OP.mult))
    K.v(lambda e, t=t: e.tensor_scalar(ap("cols")[:, 80 + t:81 + t],
                                       ap("cols")[:, 21:22],
                                       ap("swcol")[:, 2:3],
                                       1.0 / (127 * math.sqrt(8192.0)),
                                       OP.mult, OP.mult))
    K.vself()
    m_sc2 = S.mark("ve")
    K.wm("scalar", m_sc2)
    K.a(lambda e: e.activation(ap("z32")[:], ap("z32")[:], AF.Identity,
                               scale=ap("cols")[:, 23:24], bias=ap("acol")[:, 0:1]))
    K.aself()
    K.a(lambda e: e.activation(ap("n2b")[:], ap("z32")[:], AF.Identity,
                               bias=ap("nacol")[:, 0:1]))
    K.aself()
    K.wm("vector", S.mark("ac"))
    m_n2b = S.mark("ac")
    K.wu_("sync", "n2st_st")
    m_t = emit_dmat(K, blk3(ap("n2b")[:], 64, 128), blk3(ap("n2st")[:], 64, 128),
                    m_n2b)
    K.wm("gpsimd", m_t)
    S.dma("gpsimd", n2t_d[:, t, :, :].transpose([1, 0, 2]),
          ap("n2st")[:].rearrange("b (k tok) -> b k tok", tok=128), "qg")
    K.u["n2st_st"] = S.mark("qg")
    K.u["n2t_sp"] = S.mark("qg")


def emit_b2b_j(K, wd_in, n2t_d, x2_d, out, j, nt):
    S, ap = K.S, K.ap
    psA = K.psA
    for g in range(16):
        bi = g % 2
        stg = "wdstg0" if bi == 0 else "wdstg1"
        tq2 = "tq20" if bi == 0 else "tq21"
        K.wu_("scalar", ("wdstg", bi))
        for i in range(4):
            k = 4 * g + i
            S.dma("scalar", ap(stg)[:, i * 512:(i + 1) * 512],
                  wd_in[k * 128:(k + 1) * 128, j * 512:(j + 1) * 512], "qa")
        mw = S.mark("qa")
        K.wm("vector", mw)
        if g == 0:
            K.wu_("vector", "wdtj_r")
        K.v(lambda e, stg=stg, tq2=tq2: e.tensor_scalar(
            ap(tq2)[:], ap(stg)[:], ap("swcol")[:, 10:11], None, OP.is_lt))
        K.vself()
        K.v(lambda e, stg=stg, tq2=tq2, g=g: e.scalar_tensor_tensor(
            ap("wdtj")[:, g * 2048:(g + 1) * 2048], ap(stg)[:],
            ap("swcol")[:, 6:7], ap(tq2)[:], OP.is_gt, OP.subtract))
        K.u[("wdstg", bi)] = S.mark("ve")
    m_w = S.mark("ve")
    K.wm("tensor", m_w)
    for t in range(nt):
        bi = t % 2
        nb = "n2buf0" if bi == 0 else "n2buf1"
        xb = "x2b0" if bi == 0 else "x2b1"
        on = "outst0" if bi == 0 else "outst1"
        K.wu_("sync", ("n2buf", bi))
        S.dma("sync", ap(nb)[:].rearrange("b (k tok) -> b k tok", tok=128),
              n2t_d[:, t, :, :].transpose([1, 0, 2]), "qs")
        K.wu_("scalar", ("x2b", bi))
        S.dma("scalar", ap(xb)[:],
              x2_d[t * 128:(t + 1) * 128, j * 512:(j + 1) * 512], "qa")
        mn = S.mark("qs")
        mx = S.mark("qa")
        pai = t % 4
        K.wm("tensor", mn)
        K.wu_("tensor", ("psA", pai))
        for k in range(64):
            K.p(lambda e, k=k, nb=nb, pai=pai: e.matmul(
                psA[pai].ap()[:],
                ap(nb)[:, k * 128:(k + 1) * 128],
                ap("wdtj")[:, k * 512:(k + 1) * 512],
                start=(k == 0), stop=(k == 63)), inc=(k == 63))
        m_mm = S.mark("pe")
        K.u[("n2buf", bi)] = m_mm
        if t == nt - 1:
            K.u["wdtj_r"] = m_mm
        K.wm("vector", m_mm)
        K.wm("vector", mx)
        K.wu_("vector", ("outst", bi))
        K.v(lambda e, t=t, xb=xb, on=on, pai=pai: e.scalar_tensor_tensor(
            ap(on)[:], psA[pai].ap()[:], ap("cols")[:, 80 + t:81 + t],
            ap(xb)[:], OP.mult, OP.add))
        K.u[("psA", pai)] = S.mark("ve")
        K.u[("x2b", bi)] = S.mark("ve")
        m_o = S.mark("ve")
        K.wm("gpsimd", m_o)
        S.dma("gpsimd", out[t * 128:(t + 1) * 128, j * 512:(j + 1) * 512],
              ap(on)[:], "qg")
        K.u[("outst", bi)] = S.mark("qg")


_PERMQ = np.concatenate([np.concatenate([np.arange(h * 128, (h + 1) * 128, 2),
                                         np.arange(h * 128 + 1, (h + 1) * 128, 2)])
                         for h in range(NQ)])
_PERMK = np.concatenate([np.concatenate([np.arange(h * 128, (h + 1) * 128, 2),
                                         np.arange(h * 128 + 1, (h + 1) * 128, 2)])
                         for h in range(NKV)])


def host_inputs(L0_attn_x, L0_attn_Wq, L0_attn_bq, L0_attn_Wk, L0_attn_bk,
                L0_attn_Wv, L0_attn_bv, L0_attn_Wo, L0_attn_bo,
                L0_ffn_w_gate, L0_ffn_w_up, L0_ffn_w_down):
    f32 = np.float32
    X = np.asarray(L0_attn_x, f32)
    wq = np.asarray(L0_attn_Wq, f32)[:, _PERMQ]
    wk = np.asarray(L0_attn_Wk, f32)[:, _PERMK]
    wv = np.asarray(L0_attn_Wv, f32)
    wqkv = np.ascontiguousarray(np.concatenate([wq, wk, wv], axis=1))
    bqkv = np.ascontiguousarray(np.concatenate(
        [np.asarray(L0_attn_bq, f32)[_PERMQ], np.asarray(L0_attn_bk, f32)[_PERMK],
         np.asarray(L0_attn_bv, f32)], axis=0))[None, :]
    wo = np.ascontiguousarray(np.asarray(L0_attn_Wo, f32))
    bo = np.ascontiguousarray(np.asarray(L0_attn_bo, f32)[None, :])
    wg = np.ascontiguousarray(np.asarray(L0_ffn_w_gate, f32).T)
    wu = np.ascontiguousarray(np.asarray(L0_ffn_w_up, f32).T)
    wd = np.ascontiguousarray(np.asarray(L0_ffn_w_down, f32).T)

    ident = np.eye(128, dtype=f32)
    r = np.arange(128)
    tri_lo = np.where(r[None, :] >= r[:, None], 0.0, NEG).astype(f32)
    tri_hi = np.where(r[None, :] <= r[:, None], 0.0, NEG).astype(f32)
    tri = np.ascontiguousarray(np.concatenate(
        [tri_lo, tri_hi, tri_lo.T, tri_hi.T], axis=1))
    swp = np.zeros((128, 128), f32)
    for p in range(128):
        swp[(p + 64) % 128, p] = 1.0
    sgn = np.concatenate([-np.ones((64, 1), f32), np.ones((64, 1), f32)])
    h128 = _hadamard(128)
    h4 = _hadamard(4)
    hcat = np.concatenate(
        [np.concatenate([h4[lp, lo] * h128 for lp in range(4)], axis=1)
         for lo in range(4)], axis=1).astype(f32)

    maps = []
    for c in range(8):
        b, h = c // 2, c % 2
        xs = np.zeros((TOKH, E), f32)
        if h == 0:
            xs[HALO:] = X[b, 0:TOK]
        else:
            xs[:] = X[b, TOK - HALO: SEQ]
        pos0 = float(h * TOK - HALO)
        blk = np.zeros((128, 4), f32)
        if h == 0:
            blk[:, :] = NEG
        maps.append(dict(
            x=xs, wqkv=wqkv, wo=wo, bqkv=bqkv, bo=bo,
            wg=wg, wu=wu, wd=wd,
            pos0=np.full((128, 1), pos0, f32),
            swp=swp, sgn=sgn,
            blkmask=np.ascontiguousarray(blk), ident=ident,
            tri=tri, hcat=np.ascontiguousarray(hcat),
        ))
    return maps


def kernel(**inputs):
    K = build_program(debug=False, phases="AB")
    maps = host_inputs(**inputs)
    res = run_bass_kernel_spmd(K.nc, maps, core_ids=list(range(8)))
    outp = np.empty((B, SEQ, E), np.float32)
    for c in range(8):
        b, h = c // 2, c % 2
        outp[b, h * TOK:(h + 1) * TOK] = res.results[c]["out"]
    return outp


# version 10 (final)
# speedup vs baseline: 1.0791x; 1.0010x over previous
import math
import contextlib
import numpy as np

import concourse.bass as bass
import concourse.mybir as mybir
from concourse.bass_utils import run_bass_kernel_spmd
from concourse.alu_op_type import AluOpType as OP

F32 = mybir.dt.float32
BF16 = mybir.dt.bfloat16
AF = mybir.ActivationFunctionType
AX = mybir.AxisListType.X

B, SEQ, E = 4, 4096, 2048
NQ, NKV, HD = 16, 4, 128
IFF = 8192
TOK = 2048
HALO = 512
TOKH = TOK + HALO
NT = TOK // 128
NTH = TOKH // 128
NEG = -1.0e30
C23 = 1.5 * 2.0 ** 23
MAGIC_K = float(np.float32((1.5 * 2.0 ** 52 - C23) * 2.0 ** 30))
EPS = 1.0e-5
ISR = float(np.float32(1.0 / math.sqrt(HD)))
TWO_PI_HI = float(np.float32(2.0 * math.pi))
TWO_PI_LO = float(np.float32(2.0 * math.pi - np.float64(np.float32(2.0 * math.pi))))
HALF_PI = float(np.float32(0.5 * math.pi))
PI_CLAMP = 3.1415925
NLN1E4_O64 = float(np.float32(-math.log(10000.0) / 64.0))

ENGINES = ("sync", "scalar", "vector", "tensor", "gpsimd")


def _hadamard(n):
    h = np.array([[1.0]])
    while h.shape[0] < n:
        h = np.block([[h, h], [h, -h]])
    return h


class Sched:

    def __init__(self):
        self.q = {e: [] for e in ENGINES}
        self.cnt = {}
        self.sems = {}

    def sem(self, name):
        self.cnt.setdefault(name, 0)

    def on(self, eng, fn):
        self.q[eng].append(fn)

    def wait(self, eng, sem, val=None):
        v = self.cnt[sem] if val is None else val
        if v > 0:
            self.on(eng, lambda e, s=sem, vv=v: e.wait_ge(self.sems[s], vv))

    def do(self, eng, build, inc=None, k=1):
        if inc is not None:
            self.cnt[inc] += k

            def fn(e, b=build, s=inc, kk=k):
                b(e).then_inc(self.sems[s], kk)
            self.on(eng, fn)
        else:
            self.on(eng, lambda e, b=build: b(e))

    def dma(self, eng, out, in_, inc):
        self.do(eng, lambda e, o=out, i=in_: e.dma_start(out=o, in_=i), inc, 16)

    def mark(self, sem):
        return (sem, self.cnt[sem])

    def emit(self, block):
        for eng in ENGINES:
            ops = self.q[eng]

            def section(e, ops=ops):
                for fn in ops:
                    fn(e)
            getattr(block, eng)(section)


class Kern:
    def __init__(self, debug=False):
        self.nc = bass.Bass("TRN2", target_bir_lowering=False)
        self.debug = debug
        self.S = Sched()
        for s in ("qs", "qa", "qg", "pe", "ve", "ac"):
            self.S.sem(s)
        self.t = {}
        self.dram = {}
        self.u = {}

    def sb(self, es, name, shape, dtype):
        h = es.enter_context(self.nc.sbuf_tensor(name, list(shape), dtype))
        self.t[name] = h
        return h

    def psum(self, es, name, shape, dtype):
        h = es.enter_context(self.nc.psum_tensor(name, list(shape), dtype))
        self.t[name] = h
        return h

    def din(self, name, shape, dtype=F32):
        h = self.nc.dram_tensor(name, list(shape), dtype, kind="ExternalInput")
        self.dram[name] = h
        return h

    def dout(self, name, shape, dtype=F32):
        h = self.nc.dram_tensor(name, list(shape), dtype, kind="ExternalOutput")
        self.dram[name] = h
        return h

    def dtmp(self, name, shape, dtype):
        h = self.nc.dram_tensor(name, list(shape), dtype, kind="Internal")
        self.dram[name] = h
        return h

    def ap(self, name):
        return self.t[name].ap()

    def w(self, eng, *sems):
        for s in sems:
            self.S.wait(eng, s)

    def wm(self, eng, mark):
        if mark is not None:
            self.S.wait(eng, mark[0], mark[1])

    def wu_(self, eng, key):
        self.wm(eng, self.u.get(key))

    def v(self, build):
        self.S.do("vector", build, "ve")

    def a(self, build):
        self.S.do("scalar", build, "ac")

    def p(self, build, inc=False):
        self.S.do("tensor", build, "pe" if inc else None)

    def vself(self):
        self.S.wait("vector", "ve")

    def aself(self):
        self.S.wait("scalar", "ac")

    def barrier(self):
        for eng in ENGINES:
            for s in ("pe", "ve", "ac", "qs", "qa", "qg"):
                self.S.wait(eng, s)


def build_program(debug=False, phases="AB", nt=NT, dump=(), subs="0123"):
    K = Kern(debug=debug)
    K.dump = dump
    nc, S = K.nc, K.S
    ap = K.ap

    x_in = K.din("x", [TOKH, E]).ap()
    wqkv_in = K.din("wqkv", [E, 3072]).ap()
    wo_in = K.din("wo", [NQ * HD, E]).ap()
    bqkv_in = K.din("bqkv", [1, 3072]).ap()
    bo_in = K.din("bo", [1, E]).ap()
    wg_in = K.din("wg", [E, IFF]).ap()
    wu_in = K.din("wu", [E, IFF]).ap()
    wd_in = K.din("wd", [IFF, E]).ap()
    pos0_in = K.din("pos0", [128, 1]).ap()
    blk_in = K.din("blkmask", [128, 4]).ap()
    ident_in = K.din("ident", [128, 128]).ap()
    tri_in = K.din("tri", [128, 512]).ap()
    hcat_in = K.din("hcat", [128, 2048]).ap()
    swp_in = K.din("swp", [128, 128]).ap()
    sgn_in = K.din("sgn", [128, 1]).ap()

    out = K.dout("out", [TOK, E]).ap()
    wqkv16_d = K.dtmp("wqkv16_d", [E, 3072], BF16).ap()
    wo16_d = K.dtmp("wo16_d", [NQ * HD, E], BF16).ap()
    if "A" not in phases and debug:
        x2_d = K.din("x2_d", [TOK, E]).ap()
    elif debug:
        x2_d = K.dout("x2_d", [TOK, E]).ap()
    else:
        x2_d = K.dtmp("x2_d", [TOK, E], F32).ap()
    if debug:
        h_d = K.dout("h_d", [NT, 128, IFF], BF16).ap()
        n2t_d = K.dout("n2t_d", [64, NT, 128, 128], BF16).ap()
    else:
        h_d = K.dtmp("h_d", [NT, 128, IFF], BF16).ap()
        n2t_d = K.dtmp("n2t_d", [64, NT, 128, 128], BF16).ap()

    es_root = contextlib.ExitStack()
    with es_root:
        K.sb(es_root, "ident16", [128, 128], BF16)
        K.sb(es_root, "swp16", [128, 128], BF16)
        K.sb(es_root, "sgnc", [128, 1], F32)
        K.sb(es_root, "trim", [128, 512], F32)
        K.sb(es_root, "acol", [128, 1], F32)
        K.sb(es_root, "nacol", [128, 1], F32)
        K.sb(es_root, "m12c", [128, 1], F32)
        K.sb(es_root, "onesc", [128, 1], F32)
        K.sb(es_root, "biasrow", [1, 5632], BF16)
        K.sb(es_root, "ones_r32", [1, 128], F32)
        K.sb(es_root, "blkcol", [128, 4], F32)
        K.sb(es_root, "pos0c", [128, 1], F32)
        K.sb(es_root, "freqc", [128, 1], F32)
        K.sb(es_root, "swcol", [128, 12], F32)
        K.sb(es_root, "cols", [128, 112], F32)
        K.sb(es_root, "wsum", [128, 4], F32)
        K.sb(es_root, "onesc16", [128, 1], BF16)
        K.sb(es_root, "wsb0", [128, 512], F32)
        K.sb(es_root, "wsb1", [128, 512], F32)
        K.sb(es_root, "wsb2", [128, 512], F32)
        K.sb(es_root, "wsb3", [128, 512], F32)
        K.sb(es_root, "wsd", [128, 512], BF16)
        K.sb(es_root, "srow", [1, 16], F32)
        K.sb(es_root, "cs1", [128, TOKH], BF16)
        K.sb(es_root, "cs2", [128, TOKH], BF16)

        psA = [K.psum(es_root, f"psA{i}", [128, 512], F32) for i in range(4)]
        psS = [K.psum(es_root, f"psS{i}", [128, 640], F32) for i in range(2)]
        K.psA, K.psS = psA, psS

        with contextlib.ExitStack() as es:
            stg = K.sb(es, "stg", [128, 2048], F32)
            stg2 = K.sb(es, "stg2", [128, 2048], F32)
            w16s = K.sb(es, "w16s", [128, 3072], BF16)
            ang = K.sb(es, "ang", [128, TOKH], F32)
            rr = K.sb(es, "rr", [128, TOKH], F32)
            emit_pre(K, x2_d, wqkv16_d, wo16_d,
                     dict(wqkv_in=wqkv_in, wo_in=wo_in, bqkv_in=bqkv_in,
                          bo_in=bo_in, tri_in=tri_in, blk_in=blk_in,
                          pos0_in=pos0_in, ident_in=ident_in, hcat_in=hcat_in,
                          swp_in=swp_in, sgn_in=sgn_in,
                          wg_in=wg_in, wu_in=wu_in, wd_in=wd_in))
        K.barrier()

        if "A" in phases:
            es_a = contextlib.ExitStack()
            with (contextlib.nullcontext(es_root) if K.dump else es_a) as es:
                K.sb(es, "wbuf0", [128, 16 * 512], BF16)
                K.sb(es, "wbuf1", [128, 16 * 512], BF16)
                K.sb(es, "wvbuf", [128, 8 * 512], BF16)
                K.sb(es, "wobuf", [128, 8 * 512], BF16)
                K.sb(es, "ln1t", [128, 16 * 512], BF16)
                K.sb(es, "ln1b", [128, 2048], BF16)
                K.sb(es, "disc", [128, 1536], BF16)
                K.sb(es, "qrot", [128, 16 * 512], BF16)
                K.sb(es, "krot", [128, 4 * TOKH], BF16)
                K.sb(es, "vsb", [128, NTH * 512], BF16)
                K.sb(es, "xa0", [128, 2048], F32)
                K.sb(es, "xa1", [128, 2048], F32)
                K.sb(es, "xb0", [128, 512], F32)
                K.sb(es, "xb1", [128, 512], F32)
                K.sb(es, "x2s0", [128, 512], F32)
                K.sb(es, "x2s1", [128, 512], F32)
                K.sb(es, "pt16", [128, 2 * 640], BF16)
                K.sb(es, "rbsb", [128, 2048], BF16)
                K.sb(es, "avb", [128, 4 * 2048], BF16)
                emit_attention(K, x_in, wqkv16_d, wo16_d, x2_d, nt)
            K.barrier()

        emit_wscale_fin(K)
        K.barrier()
        if "B" in phases:
            emit_ffn(K, es_root, x2_d, h_d, n2t_d, wg_in, wu_in, wd_in, out, nt,
                     subs=subs)

        for name in K.dump:
            h = K.t[name]
            sh = list(h.ap().shape)
            dt_ = h.ap().dtype
            d = K.dout("dump_" + name, sh, dt_)
            for s in ("pe", "ve", "ac"):
                S.wait("gpsimd", s)
            S.dma("gpsimd", d.ap()[:], h.ap()[:], "qg")
        S.wait("gpsimd", "qg")
        S.wait("gpsimd", "qs")
        S.wait("gpsimd", "qa")
        for eng in ("sync", "scalar"):
            S.wait(eng, "qs")
            S.wait(eng, "qa")

        block = es_root.enter_context(nc.Block())
        for name in list(S.cnt):
            S.sems[name] = es_root.enter_context(nc.semaphore("sem_" + name))
        S.emit(block)
    return K


def emit_pre(K, x2_d, wqkv16_d, wo16_d, D):
    S, ap = K.S, K.ap
    S.dma("sync", ap("trim")[:], D["tri_in"][:], "qs")
    S.dma("sync", ap("blkcol")[:], D["blk_in"][:], "qs")
    S.dma("sync", ap("pos0c")[:], D["pos0_in"][:], "qs")
    S.dma("sync", ap("stg")[:, 0:128], D["ident_in"][:], "qs")
    S.dma("sync", ap("stg2")[:, 0:128], D["swp_in"][:], "qs")
    S.dma("sync", ap("sgnc")[:], D["sgn_in"][:], "qs")
    S.dma("sync", ap("stg")[0:1, 128:2048], D["bqkv_in"][0:1, 0:1920], "qs")
    m = S.mark("qs")
    K.wm("vector", m)
    K.v(lambda e: e.tensor_copy(ap("ident16")[:], ap("stg")[:, 0:128]))
    K.v(lambda e: e.tensor_copy(ap("swp16")[:], ap("stg2")[:, 0:128]))
    K.v(lambda e: e.tensor_copy(ap("biasrow")[0:1, 0:1920], ap("stg")[0:1, 128:2048]))
    K.wm("sync", S.mark("ve"))
    S.dma("sync", ap("stg")[0:1, 0:1152], D["bqkv_in"][0:1, 1920:3072], "qs")
    S.dma("sync", ap("stg2")[0:1, 0:2048], D["bo_in"][0:1, :], "qs")
    m = S.mark("qs")
    K.wm("vector", m)
    K.v(lambda e: e.tensor_copy(ap("biasrow")[0:1, 1920:3072], ap("stg")[0:1, 0:1152]))
    K.v(lambda e: e.tensor_copy(ap("biasrow")[0:1, 3072:5120], ap("stg2")[0:1, 0:2048]))
    K.u["stg_rd"] = S.mark("ve")
    K.v(lambda e: e.memset(ap("onesc")[:], 1.0))
    K.v(lambda e: e.memset(ap("m12c")[:], -12.0))
    K.v(lambda e: e.memset(ap("onesc16")[:], 1.0))
    K.v(lambda e: e.memset(ap("biasrow")[0:1, 5120:5632], 1.0))
    K.v(lambda e: e.memset(ap("ones_r32")[:], 1.0))
    K.v(lambda e: e.tensor_scalar(ap("cols")[:, 0:1], ap("onesc")[:], 2.0 ** -30,
                                  1.0, OP.add, OP.subtract))
    K.vself()
    K.v(lambda e: e.tensor_scalar(ap("acol")[:], ap("cols")[:, 0:1], MAGIC_K,
                                  C23, OP.mult, OP.add))
    K.vself()
    K.v(lambda e: e.tensor_scalar(ap("nacol")[:], ap("acol")[:], -1.0, None,
                                  OP.mult))
    m_acol = S.mark("ve")

    S.do("gpsimd", lambda e: e.iota(ap("freqc")[:], [[1, 1]], channel_multiplier=1,
                                    allow_small_or_imprecise_dtypes=True), "qg", 1)
    S.do("gpsimd", lambda e: e.iota(ap("ang")[:], [[1, TOKH]], channel_multiplier=0,
                                    allow_small_or_imprecise_dtypes=True), "qg", 1)
    m_io = S.mark("qg")
    K.wm("vector", m_io)
    K.v(lambda e: e.tensor_scalar(ap("cols")[:, 9:10], ap("freqc")[:], 64.0, None,
                                  OP.is_ge))
    K.vself()
    K.v(lambda e: e.scalar_tensor_tensor(ap("freqc")[:], ap("cols")[:, 9:10],
                                         -64.0, ap("freqc")[:], OP.mult, OP.add))
    m_f0 = S.mark("ve")
    K.wm("scalar", m_f0)
    K.a(lambda e: e.activation(ap("freqc")[:], ap("freqc")[:], AF.Exp,
                               scale=NLN1E4_O64))
    m_frq = S.mark("ac")
    K.wm("vector", m_frq)
    K.v(lambda e: e.tensor_scalar(ap("ang")[:], ap("ang")[:], ap("pos0c")[:, 0:1],
                                  ap("freqc")[:, 0:1], OP.add, OP.mult))
    K.vself()
    for idx, shift in ((0, 0.0), (1, HALF_PI)):
        K.wm("vector", m_acol)
        K.v(lambda e, shift=shift: e.tensor_scalar(
            ap("rr")[:], ap("ang")[:], 1.0 / (2 * math.pi),
            shift / (2 * math.pi), OP.mult, OP.add))
        K.vself()
        K.v(lambda e: e.tensor_scalar(ap("rr")[:], ap("rr")[:],
                                      ap("acol")[:, 0:1], ap("acol")[:, 0:1],
                                      OP.add, OP.subtract))
        K.vself()
        K.v(lambda e: e.scalar_tensor_tensor(
            ap("rr")[:], ap("rr")[:], -TWO_PI_HI, ap("ang")[:],
            OP.mult, OP.add))
        K.vself()
        K.v(lambda e, shift=shift: e.tensor_scalar(
            ap("rr")[:], ap("rr")[:], shift, -PI_CLAMP, OP.add, OP.max))
        K.vself()
        K.v(lambda e: e.tensor_scalar(ap("rr")[:], ap("rr")[:], PI_CLAMP, None,
                                      OP.min))
        m_r = S.mark("ve")
        K.wm("scalar", m_r)
        if idx == 0:
            K.a(lambda e: e.activation(ap("rr")[:], ap("rr")[:], AF.Sin))
            m_sn = S.mark("ac")
            K.wm("vector", m_sn)
            K.v(lambda e: e.tensor_scalar(ap("cs2")[:], ap("rr")[:],
                                          ap("sgnc")[:, 0:1], None, OP.mult))
        else:
            K.a(lambda e: e.activation(ap("cs1")[:], ap("rr")[:], AF.Sin))

    for kc in range(16):
        K.wu_("sync", "stg_rd")
        S.dma("sync", ap("stg")[:, 0:2048], D["wqkv_in"][kc * 128:(kc + 1) * 128, 0:2048], "qs")
        S.dma("sync", ap("stg2")[:, 0:1024], D["wqkv_in"][kc * 128:(kc + 1) * 128, 2048:3072], "qs")
        m = S.mark("qs")
        K.wm("vector", m)
        K.wu_("vector", "w16s_st")
        K.v(lambda e: e.tensor_copy(ap("w16s")[:, 0:2048], ap("stg")[:, 0:2048]))
        K.v(lambda e: e.tensor_copy(ap("w16s")[:, 2048:3072], ap("stg2")[:, 0:1024]))
        K.u["stg_rd"] = S.mark("ve")
        m = S.mark("ve")
        K.wm("scalar", m)
        S.dma("scalar", wqkv16_d[kc * 128:(kc + 1) * 128, :], ap("w16s")[:, 0:3072], "qa")
        K.u["w16s_st"] = S.mark("qa")
    for kc in range(16):
        K.wu_("sync", "stg_rd")
        S.dma("sync", ap("stg")[:, 0:2048], D["wo_in"][kc * 128:(kc + 1) * 128, :], "qs")
        m = S.mark("qs")
        K.wm("vector", m)
        K.wu_("vector", "w16s_st")
        K.v(lambda e: e.tensor_copy(ap("w16s")[:, 0:2048], ap("stg")[:, 0:2048]))
        K.u["stg_rd"] = S.mark("ve")
        m = S.mark("ve")
        K.wm("scalar", m)
        S.dma("scalar", wo16_d[kc * 128:(kc + 1) * 128, :], ap("w16s")[:, 0:2048], "qa")
        K.u["w16s_st"] = S.mark("qa")

    K.v(lambda e: e.memset(ap("wsum")[:], 0.0))
    K.wscale_ws0 = S.mark("ve")
    K.wscale_cast = S.mark("qa")
    K.wscale_list = []
    for widx, (srcw, rows) in enumerate(((D["wg_in"], E), (D["wu_in"], E),
                                         (D["wd_in"], IFF))):
        ncol = IFF if rows == E else E
        for r in range(rows // 128):
            for c in range(ncol // 512):
                K.wscale_list.append((widx, srcw, r * 128, c * 512))
    K.wscale_i = 0
    K.wscale_prev = [None, None, None, None]


def emit_wscale_chunks(K, n):
    S, ap = K.S, K.ap
    for _ in range(n):
        i = K.wscale_i
        if i >= len(K.wscale_list):
            return
        K.wscale_i += 1
        widx, srcw, r, c = K.wscale_list[i]
        bi = i % 4
        buf = f"wsb{bi}"
        if K.wscale_prev[bi] is not None:
            K.wm("gpsimd", K.wscale_prev[bi])
        S.dma("gpsimd", ap(buf)[:], srcw[r:r + 128, c:c + 512], "qg")
        m = S.mark("qg")
        K.wm("scalar", m)
        if i == 0:
            K.wm("scalar", K.wscale_ws0)
        K.wu_("scalar", "wsacc")
        K.a(lambda e, buf=buf: e.activation(ap("wsd")[:], ap(buf)[:],
                                            AF.Abs, accum_out=ap("cols")[:, 40:41]))
        K.wscale_prev[bi] = S.mark("ac")
        K.wm("vector", K.wscale_prev[bi])
        K.v(lambda e, widx=widx: e.tensor_tensor(
            ap("wsum")[:, widx:widx + 1], ap("wsum")[:, widx:widx + 1],
            ap("cols")[:, 40:41], OP.add))
        K.u["wsacc"] = S.mark("ve")


def emit_wscale_fin(K):
    S, ap = K.S, K.ap
    emit_wscale_chunks(K, len(K.wscale_list))
    m_wsum = S.mark("ve")
    K.wm("tensor", m_wsum)
    K.wu_("tensor", ("psS", 0))
    K.p(lambda e: e.matmul(K.psS[0].ap()[0:1, 0:4], ap("onesc")[:],
                           ap("wsum")[:, 0:4], start=True, stop=True), inc=True)
    m = S.mark("pe")
    K.wm("vector", m)
    K.v(lambda e: e.tensor_copy(ap("srow")[0:1, 0:4], K.psS[0].ap()[0:1, 0:4]))
    K.vself()
    K.v(lambda e: e.tensor_scalar(ap("srow")[0:1, 4:8], ap("srow")[0:1, 0:4],
                                  1.0 / (IFF * E), EPS, OP.mult, OP.max))
    K.vself()
    K.v(lambda e: e.tensor_scalar(ap("srow")[0:1, 8:12], ap("srow")[0:1, 4:8],
                                  0.5, None, OP.mult))
    K.vself()
    K.v(lambda e: e.tensor_scalar(ap("srow")[0:1, 12:16], ap("srow")[0:1, 8:12],
                                  -1.0, None, OP.mult))
    m_s = S.mark("ve")
    K.wm("tensor", m_s)
    K.p(lambda e: e.matmul(K.psS[0].ap()[:, 0:4], ap("ones_r32")[0:1, 0:128],
                           ap("srow")[0:1, 4:8], start=True, stop=True))
    K.p(lambda e: e.matmul(K.psS[0].ap()[:, 4:8], ap("ones_r32")[0:1, 0:128],
                           ap("srow")[0:1, 8:12], start=True, stop=True))
    K.p(lambda e: e.matmul(K.psS[0].ap()[:, 8:12], ap("ones_r32")[0:1, 0:128],
                           ap("srow")[0:1, 12:16], start=True, stop=True), inc=True)
    m = S.mark("pe")
    K.wm("vector", m)
    K.v(lambda e: e.tensor_copy(ap("swcol")[:, 0:12], K.psS[0].ap()[:, 0:12]))
    K.u[("psS", 0)] = S.mark("ve")


def emit_attention(K, x_in, wqkv16_d, wo16_d, x2_d, nt):
    S, ap = K.S, K.ap
    psA, psS = K.psA, K.psS
    ngroups = (nt + 4) // 4
    xa = ["xa0", "xa1"]
    xb = ["xb0", "xb1"]
    x2s = ["x2s0", "x2s1"]
    wbuf = ["wbuf0", "wbuf1"]

    def emit_ln_group(g):

        for t4 in range(4):
            tt = g * 4 + t4
            bi = t4 % 2
            K.wu_("sync", ("xa", bi))
            S.dma("sync", ap(xa[bi])[:], x_in[tt * 128:(tt + 1) * 128, :], "qs")
            mx = S.mark("qs")
            K.wm("vector", mx)
            K.wu_("vector", "cols_act")
            K.v(lambda e, bi=bi: e.reduce_sum(ap("cols")[:, 0:1], ap(xa[bi])[:], AX))
            m_sum = S.mark("ve")
            K.wm("scalar", mx)
            K.wu_("scalar", "ln1b")
            K.a(lambda e, bi=bi: e.activation(ap("ln1b")[:], ap(xa[bi])[:], AF.Square,
                                              accum_out=ap("cols")[:, 1:2]))
            K.u["disc_sq"] = S.mark("ac")
            K.wm("scalar", m_sum)
            K.wu_("scalar", "cols_free")
            K.aself()
            K.a(lambda e: e.activation(ap("cols")[:, 2:3], ap("cols")[:, 0:1],
                                       AF.Identity, scale=1.0 / 2048))
            K.wu_("vector", "disc_sq")
            K.v(lambda e: e.tensor_scalar(ap("cols")[:, 3:4], ap("cols")[:, 1:2],
                                          1.0 / 2048, EPS, OP.mult, OP.add))
            m_q = S.mark("ve")
            K.wm("scalar", m_q)
            K.aself()
            K.a(lambda e: e.activation(ap("cols")[:, 4:5], ap("cols")[:, 2:3],
                                       AF.Square))
            K.aself()
            K.a(lambda e: e.activation(ap("cols")[:, 5:6], ap("cols")[:, 4:5],
                                       AF.Identity, scale=-1.0, bias=ap("cols")[:, 3:4]))
            K.aself()
            K.a(lambda e: e.activation(ap("cols")[:, 6:7], ap("cols")[:, 5:6], AF.Ln))
            K.aself()
            K.a(lambda e: e.activation(ap("cols")[:, 6:7], ap("cols")[:, 6:7],
                                       AF.Exp, scale=-0.5))
            K.aself()
            K.a(lambda e: e.activation(ap("cols")[:, 7:8], ap("cols")[:, 2:3],
                                       AF.Identity, scale=ap("cols")[:, 6:7]))
            m_act = S.mark("ac")
            K.u["cols_act"] = m_act
            K.wm("vector", m_act)
            K.wu_("vector", "ln1b")
            K.v(lambda e, bi=bi: e.tensor_scalar(ap("ln1b")[:], ap(xa[bi])[:],
                                                 ap("cols")[:, 6:7], ap("cols")[:, 7:8],
                                                 OP.mult, OP.subtract))
            K.u[("xa", bi)] = S.mark("ve")
            K.u["cols_free"] = S.mark("ve")
            m_ln = S.mark("ve")
            K.wu_("sync", "ln1t_r")
            m_t = emit_dmat(K, blk3(ap("ln1b")[:], 16, 128),
                            blk3(ap("ln1t")[:], 16, 512, off=t4 * 128), m_ln)
            K.u["ln1b"] = m_t
        m_ln1t = S.mark("qs")


        return m_ln1t

    m_ln1t = emit_ln_group(0)
    for g in range(ngroups):
        own = g >= 1
        og_list = [list(range(0, 4)), list(range(4, 8)), list(range(8, 12)),
                   list(range(12, 16)), list(range(16, 20))]
        if not own:
            og_list = [list(range(16, 20))]
        for ogi, og in enumerate(og_list):
            wb = wbuf[ogi % 2]
            c0 = og[0] * 128
            ncol = len(og) * 128
            K.wu_("sync", ("wbuf", ogi % 2))
            S.dma("sync", ap(wb)[:, 0:16 * ncol].rearrange("p (k c) -> p k c", c=ncol),
                  wqkv16_d[:, c0:c0 + ncol].rearrange("(k p) c -> p k c", p=128), "qs")
            m_w = S.mark("qs")
            K.wm("tensor", m_w)
            K.wm("tensor", m_ln1t)
            for oi, o in enumerate(og):
                if not own and o < 16:
                    continue
                pai = oi % 3
                K.wu_("tensor", ("psA", pai))
                for k in range(16):
                    K.p(lambda e, k=k, o=o, wb=wb, pai=pai, ncol=ncol, c0=c0: e.matmul(
                        psA[pai].ap()[:],
                        ap(wb)[:, k * ncol + o * 128 - c0: k * ncol + (o + 1) * 128 - c0],
                        ap("ln1t")[:, k * 512:(k + 1) * 512],
                        start=(k == 0), stop=False))
                K.p(lambda e, o=o, pai=pai: e.matmul(
                    psA[pai].ap()[:], ap("biasrow")[0:1, o * 128:(o + 1) * 128],
                    ap("biasrow")[0:1, 5120:5632], start=False, stop=True), inc=True)
                m_mm = S.mark("pe")
                emit_rope_evac(K, pai, o, g, m_mm)
                K.u[("psA", pai)] = S.mark("ve")
            K.u[("wbuf", ogi % 2)] = S.mark("pe")
        for kh in range(2):
            K.wu_("sync", "wvbuf")
            S.dma("sync", ap("wvbuf")[:].rearrange("p (k c) -> p k c", c=512),
                  wqkv16_d[kh * 1024:(kh + 1) * 1024, 2560:3072].rearrange(
                      "(k p) c -> p k c", p=128), "qs")
            m_wv = S.mark("qs")
            K.wm("tensor", m_wv)
            for t4 in range(4):
                if kh == 0:
                    K.wu_("tensor", ("psAv", t4))
                    K.wu_("tensor", ("psA", t4))
                for kk in range(8):
                    k = kh * 8 + kk
                    K.p(lambda e, k=k, kk=kk, t4=t4: e.matmul(
                        psA[t4].ap()[:],
                        ap("ln1t")[:, k * 512 + t4 * 128: k * 512 + (t4 + 1) * 128],
                        ap("wvbuf")[:, kk * 512:(kk + 1) * 512],
                        start=(k == 0), stop=False),
                        inc=(kh == 0 and t4 == 3 and kk == 7))
                if kh == 1:
                    K.p(lambda e, t4=t4: e.matmul(
                        psA[t4].ap()[:], ap("biasrow")[0:1, 5120:5248],
                        ap("biasrow")[0:1, 2560:3072], start=False, stop=True),
                        inc=(t4 == 3))
            K.u["wvbuf"] = S.mark("pe")
        m_vmm = S.mark("pe")
        for t4 in range(4):
            tt = g * 4 + t4
            K.wm("vector", m_vmm)
            K.v(lambda e, tt=tt, t4=t4: e.tensor_copy(
                ap("vsb")[:, tt * 512:(tt + 1) * 512], psA[t4].ap()[:]))
            K.u[("psAv", t4)] = S.mark("ve")
            K.u[("psA", t4)] = S.mark("ve")
        K.u["ln1t_r"] = S.mark("pe")

        if g + 1 < ngroups:
            m_ln1t_next = emit_ln_group(g + 1)
        else:
            m_ln1t_next = m_ln1t
        if own:
            m_a1 = S.mark("ve")
            for t4 in range(4):
                qt = g * 4 + t4 - 4
                emit_attn_tile(K, qt, t4, m_a1)
            m_avb = S.mark("ve")
            for ob in range(4):
                for kh in range(2):
                    K.wu_("sync", "wobuf")
                    S.dma("sync", ap("wobuf")[:].rearrange("p (k c) -> p k c", c=512),
                          wo16_d[kh * 1024:(kh + 1) * 1024,
                                 ob * 512:(ob + 1) * 512].rearrange(
                                     "(k p) c -> p k c", p=128), "qs")
                    m_wo = S.mark("qs")
                    for t4 in range(4):
                        K.wm("tensor", m_wo)
                        K.wm("tensor", m_avb)
                        if kh == 0:
                            K.wu_("tensor", ("psA", t4))
                        for hh in range(8):
                            h = kh * 8 + hh
                            K.p(lambda e, h=h, hh=hh, t4=t4, kh=kh: e.matmul(
                                psA[t4].ap()[:],
                                ap("avb")[:, t4 * 2048 + h * 128: t4 * 2048 + (h + 1) * 128],
                                ap("wobuf")[:, hh * 512:(hh + 1) * 512],
                                start=(h == 0), stop=False),
                                inc=(kh == 0 and t4 == 3 and hh == 7))
                        if kh == 1:
                            K.p(lambda e, ob=ob, t4=t4: e.matmul(
                                psA[t4].ap()[:], ap("biasrow")[0:1, 5120:5248],
                                ap("biasrow")[0:1, 3072 + ob * 512:3072 + (ob + 1) * 512],
                                start=False, stop=True), inc=(t4 == 3))
                    K.u["wobuf"] = S.mark("pe")
                m_mmall = S.mark("pe")
                for t4 in range(4):
                    qt = g * 4 + t4 - 4
                    bi = t4 % 2
                    K.wu_("scalar", ("xb", bi))
                    S.dma("scalar", ap(xb[bi])[:],
                          x_in[(qt + 4) * 128:(qt + 5) * 128,
                               ob * 512:(ob + 1) * 512], "qa")
                    m_xb = S.mark("qa")
                    K.wm("vector", m_mmall)
                    K.wm("vector", m_xb)
                    K.wu_("vector", ("x2st", t4 % 2))
                    K.v(lambda e, t4=t4, bi=bi: e.tensor_tensor(
                        ap(x2s[t4 % 2])[:], psA[t4].ap()[:],
                        ap(xb[bi])[:], OP.add))
                    K.u[("psA", t4)] = S.mark("ve")
                    K.u[("xb", bi)] = S.mark("ve")
                    m_x2 = S.mark("ve")
                    K.wm("scalar", m_x2)
                    S.dma("scalar", x2_d[qt * 128:(qt + 1) * 128, ob * 512:(ob + 1) * 512],
                          ap(x2s[t4 % 2])[:], "qa")
                    K.u[("x2st", t4 % 2)] = S.mark("qa")
            K.u["avb"] = S.mark("pe")


        m_ln1t = m_ln1t_next

def emit_rope_evac(K, pai, o, g, m_mm):
    S, ap = K.S, K.ap
    tok0 = g * 512
    K.wm("vector", m_mm)
    K.wu_("vector", "disc_rope")
    K.v(lambda e: e.tensor_copy(ap("disc")[:, 0:512], K.psA[pai].ap()[:]))
    m_qb = S.mark("ve")
    K.wm("tensor", m_qb)
    K.wu_("tensor", ("psS", 0))
    K.p(lambda e: e.matmul(K.psS[0].ap()[:, 0:512], ap("swp16")[:],
                           ap("disc")[:, 0:512], start=True, stop=True), inc=True)
    m_sw = S.mark("pe")
    K.wm("vector", m_sw)
    K.v(lambda e: e.tensor_copy(ap("disc")[:, 512:1024], K.psS[0].ap()[:, 0:512]))
    K.u[("psS", 0)] = S.mark("ve")
    K.vself()
    K.v(lambda e, tok0=tok0: e.tensor_tensor(
        ap("disc")[:, 1024:1536], ap("disc")[:, 0:512],
        ap("cs1")[:, tok0:tok0 + 512], OP.mult))
    K.v(lambda e, tok0=tok0: e.tensor_tensor(
        ap("disc")[:, 512:1024], ap("disc")[:, 512:1024],
        ap("cs2")[:, tok0:tok0 + 512], OP.mult))
    K.vself()
    if o < 16:
        off = o * 512
        K.wu_("vector", "qrot_free")
        K.v(lambda e, off=off: e.tensor_tensor(
            ap("qrot")[:, off:off + 512], ap("disc")[:, 1024:1536],
            ap("disc")[:, 512:1024], OP.add))
    else:
        off = (o - 16) * TOKH + tok0
        K.v(lambda e, off=off: e.tensor_tensor(
            ap("krot")[:, off:off + 512], ap("disc")[:, 1024:1536],
            ap("disc")[:, 512:1024], OP.add))
    K.u["disc_rope"] = S.mark("ve")


def emit_attn_tile(K, qt, t4, m_a1):
    S, ap = K.S, K.ap
    psA, psS = K.psA, K.psS
    K.wm("tensor", m_a1)

    def scores(h):
        hv = h // 4
        hp = h % 2
        ps = psS[hp]
        K.wu_("tensor", ("psS", hp))
        for kb in range(5):
            K.p(lambda e, h=h, hv=hv, kb=kb, ps=ps: e.matmul(
                ps.ap()[:, kb * 128:(kb + 1) * 128],
                ap("krot")[:, hv * TOKH + (qt + kb) * 128: hv * TOKH + (qt + kb + 1) * 128],
                ap("qrot")[:, h * 512 + t4 * 128: h * 512 + (t4 + 1) * 128],
                start=True, stop=True), inc=(kb == 4))
        return S.mark("pe")

    def softmax(h, m_s):
        hp = h % 2
        ps = psS[hp]
        K.wm("vector", m_s)
        K.v(lambda e, ps=ps: e.tensor_tensor(ps.ap()[:, 0:128], ps.ap()[:, 0:128],
                                             ap("trim")[:, 256:384], OP.add))
        K.v(lambda e, ps=ps: e.tensor_tensor(ps.ap()[:, 512:640],
                                             ps.ap()[:, 512:640],
                                             ap("trim")[:, 384:512], OP.add))
        for kb in range(0, max(0, 4 - qt)):
            K.v(lambda e, kb=kb, ps=ps: e.tensor_scalar(
                ps.ap()[:, kb * 128:(kb + 1) * 128],
                ps.ap()[:, kb * 128:(kb + 1) * 128],
                ap("blkcol")[:, qt + kb: qt + kb + 1], None, OP.add))
        m_msk = S.mark("ve")
        K.wm("scalar", m_msk)
        K.wu_("scalar", ("pt16", hp))
        K.a(lambda e, ps=ps, hp=hp: e.activation(
            ap("pt16")[:, hp * 640:hp * 640 + 640], ps.ap()[:], AF.Exp,
            scale=ISR, bias=ap("m12c")[:, 0:1]))
        m_exp = S.mark("ac")
        K.u[("psS", hp)] = m_exp
        return m_exp

    def av(h, m_exp):
        hp = h % 2
        hv = h // 4
        pa = psA[hp]
        K.wm("tensor", m_exp)
        K.wu_("tensor", ("psA", hp))
        for kb in range(5):
            K.p(lambda e, kb=kb, hv=hv, hp=hp, pa=pa: e.matmul(
                pa.ap()[:, 0:128],
                ap("vsb")[:, (qt + kb) * 512 + hv * 128: (qt + kb) * 512 + (hv + 1) * 128],
                ap("pt16")[:, hp * 640 + kb * 128: hp * 640 + (kb + 1) * 128],
                start=(kb == 0), stop=(kb == 4)))
        for kb in range(5):
            K.p(lambda e, kb=kb, hp=hp, pa=pa: e.matmul(
                pa.ap()[0:1, 128:256], ap("onesc16")[:],
                ap("pt16")[:, hp * 640 + kb * 128: hp * 640 + (kb + 1) * 128],
                start=(kb == 0), stop=(kb == 4)), inc=(kb == 4))
        m_av = S.mark("pe")
        K.u[("pt16", hp)] = m_av
        K.wm("vector", m_av)
        K.wu_("vector", "avb")
        K.v(lambda e, h=h, pa=pa: e.tensor_copy(
            ap("avb")[:, t4 * 2048 + h * 128: t4 * 2048 + (h + 1) * 128],
            pa.ap()[:, 0:128]))
        K.v(lambda e, h=h, pa=pa: e.tensor_copy(
            ap("rbsb")[0:1, h * 128:(h + 1) * 128], pa.ap()[0:1, 128:256]))
        K.u[("psA", hp)] = S.mark("ve")

    pend = None
    for h in range(16):
        m_s = scores(h)
        if pend is not None:
            av(*pend)
        m_exp = softmax(h, m_s)
        pend = (h, m_exp)
        emit_wscale_chunks(K, 3)
    av(*pend)
    K.vself()
    def _recip_bf16(e):
        with K.nc.allow_low_precision(reason="softmax 1/L at bf16"):
            return e.reciprocal(ap("rbsb")[0:1, 0:2048], ap("rbsb")[0:1, 0:2048])
    K.v(_recip_bf16)
    m_rl = S.mark("ve")
    K.wm("tensor", m_rl)
    for c4 in range(4):
        K.wu_("tensor", ("psAr", c4 % 2))
        K.p(lambda e, c4=c4: e.matmul(
            psA[2 + c4 % 2].ap()[:], ap("biasrow")[0:1, 5120:5248],
            ap("rbsb")[0:1, c4 * 512:(c4 + 1) * 512], start=True, stop=True),
            inc=True)
        m_rb = S.mark("pe")
        K.wm("vector", m_rb)
        K.wu_("vector", ("rbsb", c4 % 2))
        K.v(lambda e, c4=c4: e.tensor_copy(ap("rbsb")[:, c4 * 512:(c4 + 1) * 512],
                                           psA[2 + c4 % 2].ap()[:]))
        K.u[("psAr", c4 % 2)] = S.mark("ve")
    K.vself()
    K.v(lambda e: e.tensor_tensor(ap("avb")[:, t4 * 2048:(t4 + 1) * 2048],
                                  ap("avb")[:, t4 * 2048:(t4 + 1) * 2048],
                                  ap("rbsb")[:, 0:2048], OP.mult))
    K.u["avb"] = S.mark("ve")
    K.u[("rbsb", 0)] = S.mark("ve")
    K.u[("rbsb", 1)] = S.mark("ve")
    K.u["qrot_free"] = S.mark("pe")


def emit_ffn(K, es_root, x2_d, h_d, n2t_d, wg_in, wu_in, wd_in, out, nt,
             subs="0123"):
    import contextlib
    S, ap = K.S, K.ap
    psA, psS = K.psA, K.psS

    K.sb(es_root, "hcat16", [128, 2048], BF16)
    es01 = contextlib.ExitStack()
    with (contextlib.nullcontext(es_root) if K.dump else es01) as esn:
        K.sb(esn, "n1T", [128, 16 * NT * 128], BF16)
        es0 = contextlib.ExitStack()
        with (contextlib.nullcontext(es_root) if (set(K.dump) & {"x2a0","lnb","lnT","lnTp","n1b","rt","dsc0"}) else es0) as es:
          if True:
            K.sb(es, "x2a0", [128, 2048], F32)
            K.sb(es, "x2a1", [128, 2048], F32)
            K.sb(es, "lnb", [128, 2048], BF16)
            K.sb(es, "lnT", [128, 2048], BF16)
            K.sb(es, "lnTp", [128, 2048], BF16)
            K.sb(es, "n1b", [128, 2048], BF16)
            K.sb(es, "rt", [128, 512], F32)
            K.sb(es, "dsc0", [128, 2048], BF16)
            S.dma("sync", ap("x2a0")[:], K.dram["hcat"].ap()[:], "qs")
            K.wm("vector", S.mark("qs"))
            K.v(lambda e: e.tensor_copy(ap("hcat16")[:], ap("x2a0")[:]))
            K.u[("x2a", 0)] = S.mark("ve")
            if "0" in subs:
                for t in range(nt):
                    emit_b0_tile(K, x2_d, t)
        K.barrier()

        es1 = contextlib.ExitStack()
        with (contextlib.nullcontext(es_root) if (set(K.dump) & {"wstg0","wstg1","tq","trn","wgT0","wgT1","wuT0","wuT1","sg","uu","h160","h161"}) else es1) as es:
          if "1" in subs:
            K.sb(es, "wstg0", [128, 2048], F32)
            K.sb(es, "wstg1", [128, 2048], F32)
            K.sb(es, "tq0", [128, 2048], BF16)
            K.sb(es, "tq1", [128, 2048], BF16)
            K.sb(es, "wgT0", [128, 16 * 512], BF16)
            K.sb(es, "wgT1", [128, 16 * 512], BF16)
            K.sb(es, "wuT0", [128, 16 * 512], BF16)
            K.sb(es, "wuT1", [128, 16 * 512], BF16)
            K.sb(es, "sg", [128, 512], F32)
            K.sb(es, "uu", [128, 512], F32)
            K.sb(es, "h160", [128, 512], BF16)
            K.sb(es, "h161", [128, 512], BF16)
            if "1" in subs:
                for j in range(16):
                    emit_b1_j(K, wg_in, wu_in, h_d, j, nt)
        K.barrier()
    K.barrier()

    es2 = contextlib.ExitStack()
    with (contextlib.nullcontext(es_root) if (set(K.dump) & {"hl0","hl1","hT","hTp","z32","n2b","n2st"}) else es2) as es:
      if "2" in subs:
        K.sb(es, "hl0", [128, IFF], BF16)
        K.sb(es, "hl1", [128, IFF], BF16)
        K.sb(es, "hT0", [128, IFF], BF16)
        K.sb(es, "hTp0", [128, IFF], BF16)
        K.sb(es, "hT1", [128, IFF], BF16)
        K.sb(es, "hTp1", [128, IFF], BF16)
        K.sb(es, "z32", [128, IFF], F32)
        K.sb(es, "n2b", [128, IFF], BF16)
        K.sb(es, "n2st", [128, IFF], BF16)
        if "2" in subs:
            for t in range(nt):
                emit_b2a_tile(K, h_d, n2t_d, t)
    K.barrier()

    es3 = contextlib.ExitStack()
    with (contextlib.nullcontext(es_root) if (set(K.dump) & {"wdstg0","wdstg1","tq2","trn2","wdtj","n2buf0","n2buf1","x2b0","x2b1","outst0","outst1"}) else es3) as es:
      if "3" in subs:
        K.sb(es, "wdstg0", [128, 2048], F32)
        K.sb(es, "wdstg1", [128, 2048], F32)
        K.sb(es, "tq20", [128, 2048], BF16)
        K.sb(es, "tq21", [128, 2048], BF16)
        K.sb(es, "wdtj", [128, 64 * 512], BF16)
        K.sb(es, "n2buf0", [128, IFF], BF16)
        K.sb(es, "n2buf1", [128, IFF], BF16)
        K.sb(es, "x2b0", [128, 512], F32)
        K.sb(es, "x2b1", [128, 512], F32)
        K.sb(es, "outst0", [128, 512], F32)
        K.sb(es, "outst1", [128, 512], F32)
        if "3" in subs:
            for j in range(4):
                emit_b2b_j(K, wd_in, n2t_d, x2_d, out, j, nt)


def emit_ln_cols(K, xtile, dsc_name):
    S, ap = K.S, K.ap
    K.wu_("vector", "cols_act")
    K.v(lambda e: e.reduce_sum(ap("cols")[:, 0:1], xtile, AX))
    m_sum = S.mark("ve")
    K.wu_("scalar", "cols_free")
    K.a(lambda e: e.activation(ap(dsc_name)[:], xtile, AF.Square,
                               accum_out=ap("cols")[:, 1:2]))
    m_sq = S.mark("ac")
    K.wm("scalar", m_sum)
    K.aself()
    K.a(lambda e: e.activation(ap("cols")[:, 2:3], ap("cols")[:, 0:1],
                               AF.Identity, scale=1.0 / 2048))
    K.wm("vector", m_sq)
    K.v(lambda e: e.tensor_scalar(ap("cols")[:, 3:4], ap("cols")[:, 1:2],
                                  1.0 / 2048, EPS, OP.mult, OP.add))
    m_q = S.mark("ve")
    K.aself()
    K.a(lambda e: e.activation(ap("cols")[:, 4:5], ap("cols")[:, 2:3], AF.Square))
    K.wm("scalar", m_q)
    K.aself()
    K.a(lambda e: e.activation(ap("cols")[:, 5:6], ap("cols")[:, 4:5],
                               AF.Identity, scale=-1.0, bias=ap("cols")[:, 3:4]))
    K.aself()
    K.a(lambda e: e.activation(ap("cols")[:, 6:7], ap("cols")[:, 5:6], AF.Ln))
    K.aself()
    K.a(lambda e: e.activation(ap("cols")[:, 6:7], ap("cols")[:, 6:7],
                               AF.Exp, scale=-0.5))
    K.aself()
    K.a(lambda e: e.activation(ap("cols")[:, 7:8], ap("cols")[:, 2:3],
                               AF.Identity, scale=ap("cols")[:, 6:7]))
    K.u["cols_act"] = S.mark("ac")
    return S.mark("ac")


def emit_dmat(K, src_ap3, dst_ap3, src_mark, eng="sync"):
    S = K.S
    q = {"sync": "qs", "scalar": "qa"}[eng]
    K.wm(eng, src_mark)
    S.do(eng, lambda e, o=dst_ap3, i=src_ap3: e.dma_start_transpose(o, i),
         q, 16)
    return S.mark(q)


def blk3(ap2, nblk, blk_stride, off=0):
    if blk_stride == 128:
        return ap2[:, off:off + nblk * 128].rearrange("p (k q) -> p k q", q=128)
    k0, off_in = divmod(off, blk_stride)
    assert off_in + 128 <= blk_stride
    v = ap2.rearrange("p (k s) -> p k s", s=blk_stride)
    return v[:, k0:k0 + nblk, off_in:off_in + 128]


def emit_b0_tile(K, x2_d, t):
    S, ap = K.S, K.ap
    psA = K.psA
    bi = t % 2
    xn = "x2a0" if bi == 0 else "x2a1"
    K.wu_("sync", ("x2a", bi))
    S.dma("sync", ap(xn)[:], x2_d[t * 128:(t + 1) * 128, :], "qs")
    mx = S.mark("qs")
    K.wm("vector", mx)
    K.wm("scalar", mx)
    m_act = emit_ln_cols(K, ap(xn)[:], "dsc0")
    K.wm("vector", m_act)
    K.wu_("vector", "lnb_r")
    K.v(lambda e, xn=xn: e.tensor_scalar(ap("lnb")[:], ap(xn)[:],
                                         ap("cols")[:, 6:7], ap("cols")[:, 7:8],
                                         OP.mult, OP.subtract))
    K.u[("x2a", bi)] = S.mark("ve")
    K.u["cols_free"] = S.mark("ve")
    m_lnb = S.mark("ve")
    K.wu_("sync", "lnT_r")
    m_t = emit_dmat(K, blk3(ap("lnb")[:], 16, 128), blk3(ap("lnT")[:], 16, 128),
                    m_lnb)
    K.u["lnb_r"] = m_t
    K.wm("vector", m_t)
    K.vself()
    for st_i, stride in ((0, 512), (1, 1024)):
        a, b = ("lnT", "lnTp") if st_i == 0 else ("lnTp", "lnT")
        nrun = 2048 // (2 * stride)
        for r in range(nrun):
            off = r * 2 * stride
            K.v(lambda e, a=a, b=b, off=off, stride=stride: e.tensor_tensor(
                ap(b)[:, off:off + stride], ap(a)[:, off:off + stride],
                ap(a)[:, off + stride:off + 2 * stride], OP.add))
            K.v(lambda e, a=a, b=b, off=off, stride=stride: e.tensor_tensor(
                ap(b)[:, off + stride:off + 2 * stride], ap(a)[:, off:off + stride],
                ap(a)[:, off + stride:off + 2 * stride], OP.subtract))
        K.vself()
    m_stg = S.mark("ve")
    K.wm("tensor", m_stg)
    for gg in range(4):
        K.wu_("tensor", ("psA", gg))
        for lo in range(4):
            c = 4 * gg + lo
            K.p(lambda e, c=c, lo=lo, gg=gg: e.matmul(
                psA[gg].ap()[:], ap("lnT")[:, c * 128:(c + 1) * 128],
                ap("hcat16")[:, lo * 512:(lo + 1) * 512],
                start=(lo == 0), stop=(lo == 3)), inc=(lo == 3))
    m_y1 = S.mark("pe")
    K.u["lnT_r"] = m_y1
    K.wm("vector", m_y1)
    for gg in range(4):
        K.v(lambda e, gg=gg: e.tensor_reduce(ap("cols")[:, 16 + gg:17 + gg],
                                             psA[gg].ap()[:], AX, OP.max,
                                             apply_absolute_value=True))
    K.vself()
    K.v(lambda e: e.tensor_tensor(ap("cols")[:, 20:21], ap("cols")[:, 16:17],
                                  ap("cols")[:, 17:18], OP.max))
    K.v(lambda e: e.tensor_tensor(ap("cols")[:, 21:22], ap("cols")[:, 18:19],
                                  ap("cols")[:, 19:20], OP.max))
    K.vself()
    K.v(lambda e: e.tensor_tensor(ap("cols")[:, 20:21], ap("cols")[:, 20:21],
                                  ap("cols")[:, 21:22], OP.max))
    K.vself()
    K.v(lambda e: e.tensor_scalar(ap("cols")[:, 21:22], ap("cols")[:, 20:21],
                                  EPS * math.sqrt(2048.0), None, OP.max))
    K.vself()
    K.v(lambda e: e.reciprocal(ap("cols")[:, 22:23], ap("cols")[:, 21:22]))
    K.vself()
    K.v(lambda e: e.tensor_scalar(ap("cols")[:, 23:24], ap("cols")[:, 22:23],
                                  127.0, None, OP.mult))
    K.vself()
    K.v(lambda e, t=t: e.tensor_scalar(ap("cols")[:, 48 + t:49 + t],
                                       ap("cols")[:, 21:22],
                                       ap("swcol")[:, 0:1],
                                       1.0 / (127 * math.sqrt(2048.0)),
                                       OP.mult, OP.mult))
    K.v(lambda e, t=t: e.tensor_scalar(ap("cols")[:, 64 + t:65 + t],
                                       ap("cols")[:, 21:22],
                                       ap("swcol")[:, 1:2],
                                       1.0 / (127 * math.sqrt(2048.0)),
                                       OP.mult, OP.mult))
    K.vself()
    K.wm("scalar", S.mark("ve"))
    K.wu_("scalar", ("n1b_r", 0))
    for gg in range(4):
        K.a(lambda e, gg=gg: e.activation(ap("rt")[:], psA[gg].ap()[:],
                                          AF.Identity, scale=ap("cols")[:, 23:24],
                                          bias=ap("acol")[:, 0:1]))
        K.aself()
        K.a(lambda e, gg=gg: e.activation(
            ap("n1b")[:, gg * 512:(gg + 1) * 512], ap("rt")[:], AF.Identity,
            bias=ap("nacol")[:, 0:1]))
        K.u[("psA", gg)] = S.mark("ac")
        K.aself()
    K.wm("vector", S.mark("ac"))
    m_n1b = S.mark("ac")
    n1t4 = K.ap("n1T").rearrange("p (k t q) -> p k t q", t=NT, q=128)[:, :, t, :]
    K.wu_("sync", "n1T_w")
    m_t2 = emit_dmat(K, blk3(ap("n1b")[:], 16, 128), n1t4, m_n1b)
    K.u[("n1b_r", 0)] = m_t2
    K.u["n1T_ld"] = m_t2


def emit_b1_j(K, wg_in, wu_in, h_d, j, nt):
    S, ap = K.S, K.ap
    psA = K.psA
    par = j % 2
    gname = "wgT0" if par == 0 else "wgT1"
    uname = "wuT0" if par == 0 else "wuT1"
    for dst, src, widx, stg, tq, qeng, qn in (
            (gname, wg_in, 0, "wstg0", "tq0", "sync", "qs"),
            (uname, wu_in, 1, "wstg1", "tq1", "sync", "qs")):
        for g in range(4):
            K.wu_(qeng, (stg, 0))
            for i in range(4):
                k = 4 * g + i
                S.dma(qeng, ap(stg)[:, i * 512:(i + 1) * 512],
                      src[k * 128:(k + 1) * 128, j * 512:(j + 1) * 512], qn)
            mw = S.mark(qn)
            K.wm("vector", mw)
            if g == 0:
                K.wu_("vector", ("wT_r", par))
            K.v(lambda e, stg=stg, tq=tq, widx=widx: e.tensor_scalar(
                ap(tq)[:], ap(stg)[:], ap("swcol")[:, 8 + widx:9 + widx],
                None, OP.is_lt))
            K.vself()
            K.v(lambda e, stg=stg, tq=tq, dst=dst, g=g, widx=widx:
                e.scalar_tensor_tensor(
                    ap(dst)[:, g * 2048:(g + 1) * 2048], ap(stg)[:],
                    ap("swcol")[:, 4 + widx:5 + widx],
                    ap(tq)[:], OP.is_gt, OP.subtract))
            K.u[(stg, 0)] = S.mark("ve")
    m_w = S.mark("ve")
    K.wm("tensor", m_w)
    K.wu_("tensor", "n1T_ld")
    for t in range(nt):
        pg = 2 * (t % 2)
        pu = pg + 1
        K.wu_("tensor", ("psA", pg))
        K.wu_("tensor", ("psA", pu))
        for k in range(16):
            K.p(lambda e, k=k, t=t, pg=pg, gname=gname: e.matmul(
                psA[pg].ap()[:],
                ap("n1T")[:, (k * NT + t) * 128:(k * NT + t + 1) * 128],
                ap(gname)[:, k * 512:(k + 1) * 512],
                start=(k == 0), stop=(k == 15)), inc=(k == 15))
            K.p(lambda e, k=k, t=t, pu=pu, uname=uname: e.matmul(
                psA[pu].ap()[:],
                ap("n1T")[:, (k * NT + t) * 128:(k * NT + t + 1) * 128],
                ap(uname)[:, k * 512:(k + 1) * 512],
                start=(k == 0), stop=(k == 15)), inc=(k == 15))
        m_mm = S.mark("pe")
        K.wm("scalar", m_mm)
        K.wu_("scalar", ("sgq", 0))
        K.a(lambda e, t=t, pg=pg: e.activation(ap("sg")[:], psA[pg].ap()[:],
                                               AF.Silu,
                                               scale=ap("cols")[:, 48 + t:49 + t]))
        K.a(lambda e, t=t, pu=pu: e.activation(ap("uu")[:], psA[pu].ap()[:],
                                               AF.Identity,
                                               scale=ap("cols")[:, 64 + t:65 + t]))
        m_sgu = S.mark("ac")
        K.u[("psA", pg)] = m_sgu
        K.u[("psA", pu)] = m_sgu
        K.wm("vector", m_sgu)
        K.wu_("vector", ("h16", t % 2))
        hn = "h160" if t % 2 == 0 else "h161"
        K.v(lambda e, hn=hn: e.tensor_tensor(ap(hn)[:], ap("sg")[:], ap("uu")[:],
                                             OP.mult))
        K.u[("sgq", 0)] = S.mark("ve")
        if t == nt - 1:
            K.u[("wT_r", par)] = S.mark("pe")
        m_h = S.mark("ve")
        K.wm("gpsimd", m_h)
        S.dma("gpsimd", h_d[t, :, j * 512:(j + 1) * 512], ap(hn)[:], "qg")
        K.u[("h16", t % 2)] = S.mark("qg")


def emit_b2a_tile(K, h_d, n2t_d, t):
    S, ap = K.S, K.ap
    psA = K.psA
    bi = t % 2
    hn = "hl0" if bi == 0 else "hl1"
    hT = "hT0" if bi == 0 else "hT1"
    hTp = "hTp0" if bi == 0 else "hTp1"
    K.wu_("sync", ("hl", bi))
    S.dma("sync", ap(hn)[:], h_d[t], "qs")
    mh = S.mark("qs")
    K.wu_("sync", ("hT_r", bi))
    m_t = emit_dmat(K, blk3(ap(hn)[:], 64, 128), blk3(ap(hT)[:], 64, 128), mh)
    K.u[("hl", bi)] = m_t
    K.wm("vector", m_t)
    K.wu_("vector", ("hTp_r", bi))
    for st_i, stride in enumerate((512, 1024, 2048, 4096)):
        a, b = (hT, hTp) if st_i % 2 == 0 else (hTp, hT)
        nrun = IFF // (2 * stride)
        for r in range(nrun):
            off = r * 2 * stride
            K.v(lambda e, a=a, b=b, off=off, stride=stride: e.tensor_tensor(
                ap(b)[:, off:off + stride], ap(a)[:, off:off + stride],
                ap(a)[:, off + stride:off + 2 * stride], OP.add))
            K.v(lambda e, a=a, b=b, off=off, stride=stride: e.tensor_tensor(
                ap(b)[:, off + stride:off + 2 * stride], ap(a)[:, off:off + stride],
                ap(a)[:, off + stride:off + 2 * stride], OP.subtract))
        K.vself()
    m_stg = S.mark("ve")
    K.u[("hTp_r", bi)] = m_stg
    K.wm("tensor", m_stg)
    for gg in range(16):
        pai = gg % 4
        K.wu_("tensor", ("psA", pai))
        for lo in range(4):
            c = 4 * gg + lo
            K.p(lambda e, c=c, lo=lo, pai=pai, hT=hT: e.matmul(
                psA[pai].ap()[:], ap(hT)[:, c * 128:(c + 1) * 128],
                ap("hcat16")[:, lo * 512:(lo + 1) * 512],
                start=(lo == 0), stop=(lo == 3)), inc=(lo == 3))
        m_mm = S.mark("pe")
        K.wm("scalar", m_mm)
        K.a(lambda e, gg=gg, pai=pai: e.activation(
            ap("z32")[:, gg * 512:(gg + 1) * 512], psA[pai].ap()[:], AF.Identity))
        K.u[("psA", pai)] = S.mark("ac")
    m_z = S.mark("ac")
    K.u[("hT_r", bi)] = S.mark("pe")
    K.wm("vector", m_z)
    K.v(lambda e: e.tensor_reduce(ap("cols")[:, 20:21], ap("z32")[:], AX, OP.max,
                                  apply_absolute_value=True))
    K.vself()
    K.v(lambda e: e.tensor_scalar(ap("cols")[:, 21:22], ap("cols")[:, 20:21],
                                  EPS * math.sqrt(8192.0), None, OP.max))
    K.vself()
    K.v(lambda e: e.reciprocal(ap("cols")[:, 22:23], ap("cols")[:, 21:22]))
    K.vself()
    K.v(lambda e: e.tensor_scalar(ap("cols")[:, 23:24], ap("cols")[:, 22:23],
                                  127.0, None, OP.mult))
    K.v(lambda e, t=t: e.tensor_scalar(ap("cols")[:, 80 + t:81 + t],
                                       ap("cols")[:, 21:22],
                                       ap("swcol")[:, 2:3],
                                       1.0 / (127 * math.sqrt(8192.0)),
                                       OP.mult, OP.mult))
    K.vself()
    m_sc2 = S.mark("ve")
    K.wm("scalar", m_sc2)
    K.a(lambda e: e.activation(ap("z32")[:], ap("z32")[:], AF.Identity,
                               scale=ap("cols")[:, 23:24], bias=ap("acol")[:, 0:1]))
    K.aself()
    K.a(lambda e: e.activation(ap("n2b")[:], ap("z32")[:], AF.Identity,
                               bias=ap("nacol")[:, 0:1]))
    K.aself()
    K.wm("vector", S.mark("ac"))
    m_n2b = S.mark("ac")
    K.wu_("sync", "n2st_st")
    m_t = emit_dmat(K, blk3(ap("n2b")[:], 64, 128), blk3(ap("n2st")[:], 64, 128),
                    m_n2b)
    K.wm("gpsimd", m_t)
    S.dma("gpsimd", n2t_d[:, t, :, :].transpose([1, 0, 2]),
          ap("n2st")[:].rearrange("b (k tok) -> b k tok", tok=128), "qg")
    K.u["n2st_st"] = S.mark("qg")
    K.u["n2t_sp"] = S.mark("qg")


def emit_b2b_j(K, wd_in, n2t_d, x2_d, out, j, nt):
    S, ap = K.S, K.ap
    psA = K.psA
    for g in range(16):
        bi = g % 2
        stg = "wdstg0" if bi == 0 else "wdstg1"
        tq2 = "tq20" if bi == 0 else "tq21"
        K.wu_("scalar", ("wdstg", bi))
        for i in range(4):
            k = 4 * g + i
            S.dma("scalar", ap(stg)[:, i * 512:(i + 1) * 512],
                  wd_in[k * 128:(k + 1) * 128, j * 512:(j + 1) * 512], "qa")
        mw = S.mark("qa")
        K.wm("vector", mw)
        if g == 0:
            K.wu_("vector", "wdtj_r")
        K.v(lambda e, stg=stg, tq2=tq2: e.tensor_scalar(
            ap(tq2)[:], ap(stg)[:], ap("swcol")[:, 10:11], None, OP.is_lt))
        K.vself()
        K.v(lambda e, stg=stg, tq2=tq2, g=g: e.scalar_tensor_tensor(
            ap("wdtj")[:, g * 2048:(g + 1) * 2048], ap(stg)[:],
            ap("swcol")[:, 6:7], ap(tq2)[:], OP.is_gt, OP.subtract))
        K.u[("wdstg", bi)] = S.mark("ve")
    m_w = S.mark("ve")
    K.wm("tensor", m_w)
    for t in range(nt):
        bi = t % 2
        nb = "n2buf0" if bi == 0 else "n2buf1"
        xb = "x2b0" if bi == 0 else "x2b1"
        on = "outst0" if bi == 0 else "outst1"
        K.wu_("sync", ("n2buf", bi))
        S.dma("sync", ap(nb)[:].rearrange("b (k tok) -> b k tok", tok=128),
              n2t_d[:, t, :, :].transpose([1, 0, 2]), "qs")
        K.wu_("scalar", ("x2b", bi))
        S.dma("scalar", ap(xb)[:],
              x2_d[t * 128:(t + 1) * 128, j * 512:(j + 1) * 512], "qa")
        mn = S.mark("qs")
        mx = S.mark("qa")
        pai = t % 4
        K.wm("tensor", mn)
        K.wu_("tensor", ("psA", pai))
        for k in range(64):
            K.p(lambda e, k=k, nb=nb, pai=pai: e.matmul(
                psA[pai].ap()[:],
                ap(nb)[:, k * 128:(k + 1) * 128],
                ap("wdtj")[:, k * 512:(k + 1) * 512],
                start=(k == 0), stop=(k == 63)), inc=(k == 63))
        m_mm = S.mark("pe")
        K.u[("n2buf", bi)] = m_mm
        if t == nt - 1:
            K.u["wdtj_r"] = m_mm
        K.wm("vector", m_mm)
        K.wm("vector", mx)
        K.wu_("vector", ("outst", bi))
        K.v(lambda e, t=t, xb=xb, on=on, pai=pai: e.scalar_tensor_tensor(
            ap(on)[:], psA[pai].ap()[:], ap("cols")[:, 80 + t:81 + t],
            ap(xb)[:], OP.mult, OP.add))
        K.u[("psA", pai)] = S.mark("ve")
        K.u[("x2b", bi)] = S.mark("ve")
        m_o = S.mark("ve")
        K.wm("gpsimd", m_o)
        S.dma("gpsimd", out[t * 128:(t + 1) * 128, j * 512:(j + 1) * 512],
              ap(on)[:], "qg")
        K.u[("outst", bi)] = S.mark("qg")


_PERMQ = np.concatenate([np.concatenate([np.arange(h * 128, (h + 1) * 128, 2),
                                         np.arange(h * 128 + 1, (h + 1) * 128, 2)])
                         for h in range(NQ)])
_PERMK = np.concatenate([np.concatenate([np.arange(h * 128, (h + 1) * 128, 2),
                                         np.arange(h * 128 + 1, (h + 1) * 128, 2)])
                         for h in range(NKV)])


def host_inputs(L0_attn_x, L0_attn_Wq, L0_attn_bq, L0_attn_Wk, L0_attn_bk,
                L0_attn_Wv, L0_attn_bv, L0_attn_Wo, L0_attn_bo,
                L0_ffn_w_gate, L0_ffn_w_up, L0_ffn_w_down):
    f32 = np.float32
    X = np.asarray(L0_attn_x, f32)
    wq = np.asarray(L0_attn_Wq, f32)[:, _PERMQ]
    wk = np.asarray(L0_attn_Wk, f32)[:, _PERMK]
    wv = np.asarray(L0_attn_Wv, f32)
    wqkv = np.ascontiguousarray(np.concatenate([wq, wk, wv], axis=1))
    bqkv = np.ascontiguousarray(np.concatenate(
        [np.asarray(L0_attn_bq, f32)[_PERMQ], np.asarray(L0_attn_bk, f32)[_PERMK],
         np.asarray(L0_attn_bv, f32)], axis=0))[None, :]
    wo = np.ascontiguousarray(np.asarray(L0_attn_Wo, f32))
    bo = np.ascontiguousarray(np.asarray(L0_attn_bo, f32)[None, :])
    wg = np.ascontiguousarray(np.asarray(L0_ffn_w_gate, f32).T)
    wu = np.ascontiguousarray(np.asarray(L0_ffn_w_up, f32).T)
    wd = np.ascontiguousarray(np.asarray(L0_ffn_w_down, f32).T)

    ident = np.eye(128, dtype=f32)
    r = np.arange(128)
    tri_lo = np.where(r[None, :] >= r[:, None], 0.0, NEG).astype(f32)
    tri_hi = np.where(r[None, :] <= r[:, None], 0.0, NEG).astype(f32)
    tri = np.ascontiguousarray(np.concatenate(
        [tri_lo, tri_hi, tri_lo.T, tri_hi.T], axis=1))
    swp = np.zeros((128, 128), f32)
    for p in range(128):
        swp[(p + 64) % 128, p] = 1.0
    sgn = np.concatenate([-np.ones((64, 1), f32), np.ones((64, 1), f32)])
    h128 = _hadamard(128)
    h4 = _hadamard(4)
    hcat = np.concatenate(
        [np.concatenate([h4[lp, lo] * h128 for lp in range(4)], axis=1)
         for lo in range(4)], axis=1).astype(f32)

    maps = []
    for c in range(8):
        b, h = c // 2, c % 2
        xs = np.zeros((TOKH, E), f32)
        if h == 0:
            xs[HALO:] = X[b, 0:TOK]
        else:
            xs[:] = X[b, TOK - HALO: SEQ]
        pos0 = float(h * TOK - HALO)
        blk = np.zeros((128, 4), f32)
        if h == 0:
            blk[:, :] = NEG
        maps.append(dict(
            x=xs, wqkv=wqkv, wo=wo, bqkv=bqkv, bo=bo,
            wg=wg, wu=wu, wd=wd,
            pos0=np.full((128, 1), pos0, f32),
            swp=swp, sgn=sgn,
            blkmask=np.ascontiguousarray(blk), ident=ident,
            tri=tri, hcat=np.ascontiguousarray(hcat),
        ))
    return maps


def kernel(**inputs):
    K = build_program(debug=False, phases="AB")
    maps = host_inputs(**inputs)
    res = run_bass_kernel_spmd(K.nc, maps, core_ids=list(range(8)))
    outp = np.empty((B, SEQ, E), np.float32)
    for c in range(8):
        b, h = c // 2, c % 2
        outp[b, h * TOK:(h + 1) * TOK] = res.results[c]["out"]
    return outp
